# version 32
# baseline (speedup 1.0000x reference)
import numpy as np
import ml_dtypes
from contextlib import ExitStack
import concourse.bass as bass
import concourse.mybir as mybir
from concourse.bass_utils import run_bass_kernel_spmd

F32 = mybir.dt.float32
BF16 = mybir.dt.bfloat16
F16 = mybir.dt.float16
ALU = mybir.AluOpType
ACT = mybir.ActivationFunctionType
AX = mybir.AxisListType

S, DM, E, DI, NST = 4096, 1024, 4096, 2048, 16
T = 256
NT = S // T
TSUB = T // 128
CT = DI // 128
AD = DM // 128
AE = E // 128
XP = T + 4
_STAGES = 3
_NTILES = NT
_DEBUG = False

C_SUM, C_SQ, C_MEAN, C_MSQ, C_VAR, C_VE, C_S0, C_T2 = 0, 2, 4, 6, 8, 10, 12, 14
C_MX, C_SX, C_ISX, C_CG, C_CU, C_CD, C_T3, C_ACC = 16, 18, 20, 22, 24, 26, 28, 32


def build_program(stages=3, ntiles=None, debug=None):
    ntiles = ntiles or _NTILES
    debug = _DEBUG if debug is None else debug
    nc = bass.Bass("TRN2", target_bir_lowering=False, detect_race_conditions=False)
    ops = []
    cnt = {}
    sems = {}

    def step(eng, fn, waits=(), incs=()):
        ops.append((eng, tuple(waits), fn, tuple(incs)))
        for s_, by in incs:
            cnt[s_] = cnt.get(s_, 0) + by

    def now(s_):
        return cnt.get(s_, 0)

    d_x = nc.dram_tensor("x", [S, DM], F32, kind="ExternalInput").ap()
    d_out = nc.dram_tensor("out", [S, DM], F32, kind="ExternalOutput").ap()
    d_wg = nc.dram_tensor("wgT", [128, AD * E], F32, kind="ExternalInput").ap()
    d_wu = nc.dram_tensor("wuT", [128, AD * E], F32, kind="ExternalInput").ap()
    d_wd = nc.dram_tensor("wdT", [128, AE * DM], F32, kind="ExternalInput").ap()
    d_win = nc.dram_tensor("win", [128, AD * E], F32, kind="ExternalInput").ap()
    d_wout = nc.dram_tensor("wout", [128, CT * DM], F32, kind="ExternalInput").ap()
    d_wx = nc.dram_tensor("wx", [128, CT * 96], F32, kind="ExternalInput").ap()
    d_wdt = nc.dram_tensor("wdt", [64, DI], F32, kind="ExternalInput").ap()
    d_alog = nc.dram_tensor("alog", [128, CT * NST], F32, kind="ExternalInput").ap()
    d_convw = nc.dram_tensor("convw", [128, CT * 4], F32, kind="ExternalInput").ap()
    d_convb = nc.dram_tensor("convb", [128, CT], F32, kind="ExternalInput").ap()
    d_dd = nc.dram_tensor("dvec", [128, CT], F32, kind="ExternalInput").ap()
    d_bdt = nc.dram_tensor("bdt", [128, CT], F32, kind="ExternalInput").ap()
    s_gu = nc.dram_tensor("s_gu", [128, AD * 2 * E], BF16, kind="Internal").ap()
    s_d = nc.dram_tensor("s_d", [128, AE * DM], BF16, kind="Internal").ap()
    s_wih = nc.dram_tensor("s_wih", [128, AD * E], BF16, kind="Internal").ap()
    s_wil = nc.dram_tensor("s_wil", [128, AD * E], BF16, kind="Internal").ap()
    s_woh = nc.dram_tensor("s_woh", [128, CT * DM], BF16, kind="Internal").ap()
    s_wol = nc.dram_tensor("s_wol", [128, CT * DM], BF16, kind="Internal").ap()

    A = nc.alloc_sbuf_tensor
    ident_bf = A("ident_bf", [128, 128], BF16)
    ident_f = A("ident_f", [128, 128], F32)
    ones_f = A("ones_f", [1, 128], F32)
    iotp = A("iotp", [128, 1], F32)
    iotf = A("iotf", [128, 128], F32)
    a_neg = A("a_neg", [128, CT * NST], F32)
    convw = A("convw_s", [128, CT * 4], F32)
    convb = A("convb_s", [128, CT], F32)
    dvec = A("dvec_s", [128, CT], F32)
    bdt = A("bdt_s", [128, CT], F32)
    wx_h = A("wx_h", [128, CT * 96], BF16)
    wx_l = A("wx_l", [128, CT * 96], BF16)
    wdt_h = A("wdt_h", [64, DI], BF16)
    wdt_l = A("wdt_l", [64, DI], BF16)
    h_state = A("h_state", [128, CT * NST], F32)
    stat = A("stat", [128, 64], F32)
    sc_bc = A("sc_bc", [128, 8], F32)
    srow = A("srow", [1, 8], F32)
    brow = A("brow", [1, 2 * T], F32)
    crow = A("crow", [1, 2 * T], F32)
    wlane = [A(f"wlane{i}", [128, 4096], BF16) for i in range(4)]
    r_t = A("r_t", [128, TSUB * DM], F32)
    xln = A("xln", [128, TSUB * DM], F32)

    Ix = A("Ix", [128, TSUB * DM], BF16)
    IxT = A("IxT", [128, AD * T], BF16)
    xlh = A("xlh", [128, AD * T], BF16)
    xll = A("xll", [128, AD * T], BF16)
    dA_b = A("dA_b", [128, 2 * T], F32)
    dBx_b = A("dBx_b", [128, 2 * T], F32)
    h_b = A("h_b", [128, 4 * T], F32)
    u16 = nc.alloc_sbuf_tensor_at("u16", [128, 2048], F16,
                                  offset=nc.lookup_mloc(h_b).addr)
    P_b = A("P_b", [128, 2 * T], F32)
    bm_sb = A("bm_sb", [128, 2 * T], F32)
    cm_sb = A("cm_sb", [128, 2 * T], F32)
    r_t2 = A("r_t2", [128, TSUB * DM], F32)
    sz_b = A("sz_b", [128, 2 * T], F32)
    ssm_sb = A("ssm_sb", [96, T], F32)
    dtr_h = A("dtr_h", [64, T], BF16)
    dtr_l = A("dtr_l", [64, T], BF16)
    va_h = A("va_h", [128, 2 * T], BF16)
    va_l = A("va_l", [128, 2 * T], BF16)
    vo_h = A("vo_h", [128, 8 * T], BF16)
    vo_l = A("vo_l", [128, 8 * T], BF16)
    sps = A("sps", [128, 1024], F32)
    sqd = nc.alloc_sbuf_tensor_at("sqd", [128, DM], F32,
                                  offset=nc.lookup_mloc(sps).addr)
    halo = A("halo", [128, CT * 4], F32)
    AR = 82176
    arena = A("arena", [128, AR], mybir.dt.uint8)
    base = nc.lookup_mloc(arena).addr

    def at(name, shape, dtype, off):
        return nc.alloc_sbuf_tensor_at(name, shape, dtype, offset=base + off)

    xinp = at("xinp", [128, CT * XP], F32, 0)
    z_t = at("z_t", [128, CT * T], F32, 16640)
    xa_t = at("xa_t", [128, CT * T], F32, 33024)
    dt_t = at("dt_t", [128, CT * T], F32, 49408)
    vv_t = at("vv_t", [128, CT * T], F32, 65792)
    gu = at("gu", [128, TSUB * E], F32, 0)
    Igu = at("Igu", [128, TSUB * E], BF16, 32768)
    IguT = at("IguT", [128, AE * T], BF16, 49152)
    wq_in = at("wq_in", [128, 4096], F32, 0)
    wq_t1 = at("wq_t1", [128, 4096], F32, 16384)
    wq_t2 = at("wq_t2", [128, 4096], F32, 32768)
    wq_o1 = at("wq_o1", [128, 4096], BF16, 49152)
    wq_o2 = at("wq_o2", [128, 4096], BF16, 57344)
    wq_t3 = at("wq_t3", [128, 4096], F32, 65792)

    rem = nc.sbuf_bytes_remaining
    assert rem >= 0, f"sbuf over budget: {rem}"

    pT0 = nc.alloc_psum_tensor("pT0", [128, 1024], BF16)
    pT1 = nc.alloc_psum_tensor("pT1", [128, 1024], BF16)
    pF = [nc.alloc_psum_tensor(f"pF{i}", [128, 512], F32) for i in range(6)]
    bankF = [pT0[:, 0:1024].bitcast(F32), pT1[:, 0:1024].bitcast(F32)] + \
            [pF[i][:, 0:512] for i in range(6)]
    SETA, SETB = (2, 3, 4, 5), (6, 7, 0, 1)

    bank_free = {i: None for i in range(8)}
    lane_free = {i: None for i in range(4)}
    dep = {}

    def wait_bank(b):
        return [bank_free[b]] if bank_free[b] else []

    def wait_lane(i):
        return [lane_free[i]] if lane_free[i] else []

    def wz(*names):
        seen = {}
        for n_ in names:
            if n_ in dep:
                s_, v_ = dep[n_]
                seen[s_] = max(seen.get(s_, 0), v_)
        return [(s_, v_) for s_, v_ in seen.items()]

    QKS = {'dw': 8, 'dx': 4}
    qstate = {q: [0, [0] * k] for q, k in QKS.items()}

    def dma(qeng, out_ap, in_ap, q, waits=()):
        st_ = qstate[q]
        k = st_[0] % QKS[q]
        st_[0] += 1
        sem = f'{q}{k}'
        w = list(waits)
        if st_[1][k]:
            w.append((sem, st_[1][k]))
        step(qeng, lambda e, o=out_ap, i=in_ap: e.dma_start(o, i),
             waits=w, incs=((sem, 16),))
        st_[1][k] = now(sem)
        return (sem, st_[1][k])

    def qdone(q):
        return [(f'{q}{k}', v) for k, v in enumerate(qstate[q][1]) if v]

    SC = lambda c: stat[:, c:c + 1]
    SC2 = lambda c: stat[:, c:c + 2]

    def vchain(fns, waits=()):
        for i_, fn_ in enumerate(fns):
            w = list(waits) if i_ == 0 else []
            w.append(('v', now('v')))
            step('vector', fn_, waits=w, incs=(('v', 1),))

    step('gpsimd', lambda g: g.iota(iotp[:], [[0, 1]], channel_multiplier=1,
                                    allow_small_or_imprecise_dtypes=True))
    step('gpsimd', lambda g: g.iota(iotf[:], [[1, 128]], channel_multiplier=0,
                                    allow_small_or_imprecise_dtypes=True),
         incs=(('g', 1),))
    for dst, src in ((a_neg[:], d_alog), (convw[:], d_convw), (convb[:], d_convb),
                     (dvec[:], d_dd), (bdt[:], d_bdt),
                     (wq_t1[:, 0:CT * 96], d_wx), (wq_t2[0:64, 0:DI], d_wdt)):
        dma('gpsimd', dst, src, 'dx')
    v_small = qdone('dx')
    def x_ap(d, tile):
        return bass.AP(d.tensor, tile * T * DM,
                       [[DM, 128], [128 * DM, TSUB], [1, DM]])
    dep['rbuf0'] = dma('gpsimd', r_t[:], x_ap(d_x, 0), 'dx')

    def vsetup(v):
        v.tensor_scalar(ident_bf[:], iotf[:], iotp[:, 0:1], None, op0=ALU.is_equal)
        v.tensor_scalar(ident_f[:], iotf[:], iotp[:, 0:1], None, op0=ALU.is_equal)
        v.memset(ones_f[:], 1.0)
        v.memset(h_state[:], 0.0)
        v.memset(stat[:, C_ACC + 3:C_ACC + 4], 0.0)
        v.memset(halo[:], 0.0)
        v.tensor_copy(wx_h[:], wq_t1[:, 0:CT * 96])
        v.tensor_tensor(wx_l[:], wq_t1[:, 0:CT * 96], wx_h[:], ALU.subtract)
        v.tensor_copy(wdt_h[:], wq_t2[0:64, 0:DI])
        return v.tensor_tensor(wdt_l[:], wq_t2[0:64, 0:DI], wdt_h[:], ALU.subtract)
    step('vector', vsetup, waits=[('g', 1)] + v_small, incs=(('v', 1),))
    dep['wq0'] = ('v', now('v'))
    dep['wq1'] = ('v', now('v'))
    ldbuf = (wq_in, wq_t1, wq_t3)
    ldi = [0]

    def ld_next():
        i_ = ldi[0] % 3
        ldi[0] += 1
        return ldbuf[i_], f'wq{i_}'
    step('scalar', lambda s: s.activation(a_neg[:], a_neg[:], ACT.Exp,
                                          bias=0.0, scale=1.0),
         waits=v_small, incs=(('s', 1),))
    step('vector', lambda v: v.tensor_single_scalar(a_neg[:], a_neg[:], -1.0, ALU.mult),
         waits=(('s', now('s')),), incs=(('v', 1),))

    raws = [d_wg, d_wu, d_wd]
    for j in range(3):
        for c in range(8):
            buf, gn = ld_next()
            wv = dma('sync', buf[:], raws[j][:, c * 4096:(c + 1) * 4096], 'dw',
                     waits=wz(gn))
            def red1(v, b=buf):
                return v.reduce_sum(SC(C_T3), b[:], axis=AX.X,
                                    apply_absolute_value=True)
            def red2(v, j=j, c=c):
                if c == 0:
                    return v.tensor_copy(SC(C_ACC + j), SC(C_T3))
                return v.tensor_tensor(SC(C_ACC + j), SC(C_ACC + j), SC(C_T3), ALU.add)
            vchain([red1, red2], waits=(wv,))
            dep[gn] = ('v', now('v'))
    step('tensor', lambda t: t.transpose(pF[4][0:4, 0:128], stat[:, C_ACC:C_ACC + 4],
                                         ident_f[:]),
         waits=wz('wq0', 'wq1', 'wq2'), incs=(('t', 1),))
    vchain([
        lambda v: v.reduce_sum(stat[0:4, 36:37], pF[4][0:4, 0:128], axis=AX.X),
        lambda v: v.tensor_scalar(stat[0:4, 37:38], stat[0:4, 36:37],
                                  1.0 / (DM * E), None, op0=ALU.mult),
        lambda v: v.tensor_scalar(stat[0:4, 38:39], stat[0:4, 37:38], 1e-5, None,
                                  op0=ALU.max),
        lambda v: v.reciprocal(stat[0:4, 39:40], stat[0:4, 38:39]),
        lambda v: v.reciprocal(stat[0:4, 40:41], stat[0:4, 39:40]),
    ], waits=(('t', now('t')),))
    bank_free[6] = ('v', now('v'))
    rv1 = dma('gpsimd', srow[0:1, 0:3], stat[0:3, 39:40], 'dx', waits=(('v', now('v')),))
    rv2 = dma('gpsimd', srow[0:1, 3:6], stat[0:3, 40:41], 'dx')
    step('tensor', lambda t: t.matmul(pF[5][:, 0:6], ones_f[:], srow[0:1, 0:6],
                                      start=True, stop=True),
         waits=[rv1, rv2] + wait_bank(7), incs=(('t', 1),))
    step('vector', lambda v: v.tensor_copy(sc_bc[:, 0:6], pF[5][:, 0:6]),
         waits=(('t', now('t')),), incs=(('v', 1),))
    bank_free[7] = ('v', now('v'))
    dep['sc_bc'] = ('v', now('v'))

    for j in range(3):
        for c in range(8):
            pin, gn = ld_next()
            out = wq_o1 if c % 2 == 0 else wq_o2
            for h in range(2):
                sl = slice(h * 2048, (h + 1) * 2048)
                wv = dma('sync', pin[:, sl],
                         raws[j][:, c * 4096 + h * 2048:
                                 c * 4096 + (h + 1) * 2048], 'dw',
                         waits=wz(gn))
                def quant(v, pin=pin, out=out, j=j, sl=sl):
                    v.tensor_scalar(wq_t2[:, sl], pin[:, sl], sc_bc[:, j:j + 1],
                                    None, op0=ALU.mult)
                    v.tensor_single_scalar(out[:, sl], wq_t2[:, sl], -0.5,
                                           ALU.is_le)
                    return v.scalar_tensor_tensor(out[:, sl], wq_t2[:, sl],
                                                  0.5, out[:, sl],
                                                  op0=ALU.is_ge,
                                                  op1=ALU.subtract)
                step('vector', quant,
                     waits=[wv] + (wz('sc_bc', f'wqo{c % 2}')
                                   if h == 0 else []), incs=(('v', 1),))
            dep[gn] = ('v', now('v'))
            vq = now('v')
            if j < 2:
                dst = bass.AP(s_gu.tensor, c * 8192 + j * 512,
                              [[AD * 2 * E, 128], [1024, 8], [1, 512]])
            else:
                dst = s_d[:, c * 4096:(c + 1) * 4096]
            dep[f'wqo{c % 2}'] = dma('sync', dst, out[:], 'dw', waits=(('v', vq),))
    for (raw, nch, dh, dl, grouped) in ((d_win, 8, s_wih, s_wil, True),
                                        (d_wout, 4, s_woh, s_wol, False)):
        for c in range(nch):
            pin, gn = ld_next()
            for h in range(2):
                sl = slice(h * 2048, (h + 1) * 2048)
                wv = dma('sync', pin[:, sl],
                         raw[:, c * 4096 + h * 2048:
                             c * 4096 + (h + 1) * 2048], 'dw',
                         waits=wz(gn))
                def split(v, pin=pin, sl=sl):
                    v.tensor_copy(wq_o1[:, sl], pin[:, sl])
                    return v.tensor_tensor(wq_o2[:, sl], pin[:, sl],
                                           wq_o1[:, sl], ALU.subtract)
                step('vector', split,
                     waits=[wv] + (wz('wqo0', 'wqo1') if h == 0 else []),
                     incs=(('v', 1),))
            dep[gn] = ('v', now('v'))
            vq = now('v')
            if grouped:
                dsth = bass.AP(s_wih.tensor, c * 512,
                               [[AD * E, 128], [4096, 8], [1, 512]])
                dstl = bass.AP(s_wil.tensor, c * 512,
                               [[AD * E, 128], [4096, 8], [1, 512]])
            else:
                dsth = s_woh[:, c * 4096:(c + 1) * 4096]
                dstl = s_wol[:, c * 4096:(c + 1) * 4096]
            dma('sync', dsth, wq_o1[:], 'dw', waits=(('v', vq),))
            ov = dma('sync', dstl, wq_o2[:], 'dw')
            dep['wqo0'] = ov
            dep['wqo1'] = ov
    for i_, pv in enumerate(qdone('dw')):
        dep[f'prol{i_}'] = pv

    cur_r = [r_t]

    def layer_norm(rwaits=None, rbuf=None):
        r_ = rbuf
        def ln_s(s):
            last = None
            for tt in range(TSUB):
                s.activation(xln[:, tt * DM:(tt + 1) * DM],
                             r_[:, tt * DM:(tt + 1) * DM],
                             ACT.Identity, bias=0.0, scale=1.0,
                             accum_out=stat[:, C_SUM + tt:C_SUM + tt + 1])
                last = s.activation(sqd[:], r_[:, tt * DM:(tt + 1) * DM],
                                    ACT.Square, bias=0.0, scale=1.0,
                                    accum_out=stat[:, C_SQ + tt:C_SQ + tt + 1])
            return last
        step('scalar', ln_s,
             waits=(wz('r') if rwaits is None else list(rwaits)) +
                   wz('xln', 'sps0', 'sps1'), incs=(('s', 1),))
        sv = now('s')
        vchain([
            lambda v: (v.tensor_scalar(SC2(C_MEAN), SC2(C_SUM), 1.0 / DM, None,
                                       op0=ALU.mult),
                       v.tensor_scalar(SC2(C_MSQ), SC2(C_SQ), 1.0 / DM, None,
                                       op0=ALU.mult))[-1],
            lambda v: v.tensor_tensor(SC2(C_VAR), SC2(C_MEAN), SC2(C_MEAN), ALU.mult),
            lambda v: v.tensor_tensor(SC2(C_VAR), SC2(C_MSQ), SC2(C_VAR), ALU.subtract),
            lambda v: v.tensor_scalar(SC2(C_VE), SC2(C_VAR), 1e-5, None, op0=ALU.add),
        ], waits=(('s', sv),))
        def sqrt_ln(s):
            s.activation(SC2(C_T2), SC2(C_VE), ACT.Ln, bias=0.0, scale=1.0)
            return s.activation(SC2(C_S0), SC2(C_T2), ACT.Exp, bias=0.0, scale=0.5)
        step('scalar', sqrt_ln, waits=(('v', now('v')),), incs=(('s', 1),))
        sv = now('s')
        def ln_apply(v):
            last = None
            for tt in range(TSUB):
                sl = slice(tt * DM, (tt + 1) * DM)
                last = v.tensor_scalar(xln[:, sl], xln[:, sl],
                                       stat[:, C_MEAN + tt:C_MEAN + tt + 1],
                                       stat[:, C_T2 + tt:C_T2 + tt + 1],
                                       op0=ALU.subtract, op1=ALU.mult)
            return last
        vchain([lambda v: v.reciprocal(SC2(C_T2), SC2(C_S0)), ln_apply],
               waits=(('s', sv),))
        dep['xln_w'] = ('v', now('v'))

    def ffn_head(so, rwaits=None, rbuf=None):
        layer_norm(rwaits=rwaits, rbuf=rbuf)
        def qtail(v):
            last = None
            for tt in range(TSUB):
                sl = slice(tt * DM, (tt + 1) * DM)
                v.tensor_scalar(u16[:, 0:DM], xln[:, sl],
                                stat[:, so + C_SX + tt:so + C_SX + tt + 1],
                                1536.0, op0=ALU.mult, op1=ALU.add)
                last = v.tensor_single_scalar(Ix[:, sl], u16[:, 0:DM], 1536.0,
                                              ALU.subtract)
            return last
        vchain([
            lambda v: (v.reduce_max(SC(so + C_MX), xln[:, 0:DM], axis=AX.X,
                                    apply_absolute_value=True),
                       v.reduce_max(SC(so + C_MX + 1), xln[:, DM:2 * DM],
                                    axis=AX.X,
                                    apply_absolute_value=True))[-1],
            lambda v: v.tensor_scalar(SC2(so + C_MX), SC2(so + C_MX), 1e-5,
                                      None, op0=ALU.max),
            lambda v: v.reciprocal(SC2(so + C_SX), SC2(so + C_MX)),
            lambda v: v.tensor_scalar(SC2(so + C_SX), SC2(so + C_SX), 127.0,
                                      None, op0=ALU.mult),
            lambda v: v.reciprocal(SC2(so + C_ISX), SC2(so + C_SX)),
            lambda v: (v.tensor_scalar(SC2(so + C_CG), SC2(so + C_ISX),
                                       sc_bc[:, 3:4], None, op0=ALU.mult),
                       v.tensor_scalar(SC2(so + C_CU), SC2(so + C_ISX),
                                       sc_bc[:, 4:5], None, op0=ALU.mult))[-1],
            qtail,
        ], waits=wz('sc_bc', 'Ix_free'))
        dep['Ix_w'] = ('v', now('v'))
        dep['xln'] = ('v', now('v'))
        ixt_a = []
        for a in range(AD):
            pb, bi = (pT0, 0) if a % 2 == 0 else (pT1, 1)
            def tr(t, a=a, pb=pb):
                t.transpose(pb[:, 0:128],
                            Ix[:, a * 128:a * 128 + 128], ident_bf[:])
                return t.transpose(pb[:, 128:256],
                                   Ix[:, DM + a * 128:DM + a * 128 + 128],
                                   ident_bf[:])
            step('tensor', tr,
                 waits=(wz('Ix_w', 'IxT_free') if a < 2 else []) + wait_bank(bi),
                 incs=(('t', 1),))
            tv = now('t')
            def cp(v, pb=pb, a=a):
                dst = bass.AP(IxT, a * T, [[AD * T, 128], [128, 2], [1, 128]])
                src = bass.AP(pb, 0, [[1024, 128], [128, 2], [1, 128]])
                return v.tensor_copy(dst, src)
            step('vector', cp, waits=(('t', tv),), incs=(('v', 1),))
            bank_free[bi] = ('v', now('v'))
            ixt_a.append(now('v'))
        dep['IxT'] = ('v', now('v'))
        dep['Ix_free'] = ('t', now('t'))
        return ixt_a

    def ffn_body(ixt_a, so, rbuf, pre_epi=None):
        for seg in range(8):
            lanes = [(seg % 2) * 2, (seg % 2) * 2 + 1]
            ch_done = []
            for a in range(AD):
                ln_ = lanes[a // 4]
                off = (a % 4) * 1024
                wv = dma('sync', wlane[ln_][:, off:off + 1024],
                         s_gu[:, a * 8192 + seg * 1024:a * 8192 + (seg + 1) * 1024],
                         'dw', waits=wait_lane(ln_) if a % 4 == 0 else ())
                ch_done.append(wv)
            bs = SETA if seg % 2 == 0 else SETB
            def mm(t, lanes=lanes, ch_done=ch_done, bs=bs):
                last = None
                for a in range(AD):
                    t.wait_ge(sems[ch_done[a][0]], ch_done[a][1])
                    t.wait_ge(sems['v'], ixt_a[a])
                    ln_ = wlane[lanes[a // 4]]
                    off = (a % 4) * 1024
                    for tt in range(TSUB):
                        lhs = IxT[:, a * T + tt * 128:a * T + tt * 128 + 128]
                        last = t.matmul(bankF[bs[2 * tt]], lhs,
                                        ln_[:, off:off + 512],
                                        start=(a == 0), stop=(a == AD - 1))
                        last = t.matmul(bankF[bs[2 * tt + 1]], lhs,
                                        ln_[:, off + 512:off + 1024],
                                        start=(a == 0), stop=(a == AD - 1))
                return last
            step('tensor', mm,
                 waits=[w for b in bs for w in wait_bank(b)],
                 incs=(('t', 1),))
            tv = now('t')
            for l_ in lanes:
                lane_free[l_] = ('t', tv)
            for tt in range(TSUB):
                spo = tt * 512
                def ev_s(s, tt=tt, spo=spo, bs=bs):
                    return s.activation(sps[:, spo:spo + 512], bankF[bs[2 * tt]],
                                        ACT.Silu, bias=0.0,
                                        scale=stat[:, so + C_CG + tt:
                                                   so + C_CG + tt + 1])
                step('scalar', ev_s, waits=[('t', tv)] + wz(f'sps{tt}'),
                     incs=(('s', 1),))
                sv = now('s')
                bank_free[bs[2 * tt]] = ('s', sv)
                def ev_v(v, tt=tt, seg=seg, spo=spo, bs=bs):
                    dst = gu[:, tt * E + seg * 512:tt * E + (seg + 1) * 512]
                    return v.scalar_tensor_tensor(dst, bankF[bs[2 * tt + 1]],
                                                  stat[:, so + C_CU + tt:
                                                       so + C_CU + tt + 1],
                                                  sps[:, spo:spo + 512],
                                                  op0=ALU.mult, op1=ALU.mult)
                def ev_m1(v, tt=tt, seg=seg):
                    sl = slice(tt * E + seg * 512, tt * E + (seg + 1) * 512)
                    return v.reduce_max(SC(C_T3), gu[:, sl], axis=AX.X,
                                        apply_absolute_value=True)
                def ev_m2(v, tt=tt, seg=seg):
                    mx = slice(so + C_MX + tt, so + C_MX + tt + 1)
                    if seg == 0:
                        return v.tensor_copy(stat[:, mx], SC(C_T3))
                    return v.tensor_tensor(stat[:, mx], stat[:, mx],
                                           SC(C_T3), ALU.max)
                step('vector', ev_v,
                     waits=[('s', sv)] +
                           (wz('wq0', 'wq1') if seg == 0 else []),
                     incs=(('v', 1),))
                vchain([ev_m1, ev_m2])
                dep[f'sps{tt}'] = ('v', now('v'))
                bank_free[bs[2 * tt + 1]] = ('v', now('v'))
        vchain([
            lambda v: v.tensor_scalar(SC2(so + C_MX), SC2(so + C_MX), 1e-5,
                                      None, op0=ALU.max),
            lambda v: v.reciprocal(SC2(so + C_SX), SC2(so + C_MX)),
            lambda v: v.tensor_scalar(SC2(so + C_SX), SC2(so + C_SX), 127.0,
                                      None, op0=ALU.mult),
            lambda v: v.reciprocal(SC2(so + C_ISX), SC2(so + C_SX)),
            lambda v: v.tensor_scalar(SC2(so + C_CD), SC2(so + C_ISX),
                                      sc_bc[:, 5:6], None, op0=ALU.mult),
            lambda v: v.tensor_scalar(SC2(so + C_CD), SC2(so + C_CD), 0.5,
                                      None, op0=ALU.mult),
        ])
        igut_b = {}
        for tt in range(TSUB):
            def qgut(v, tt=tt):
                last = None
                for hf in range(2):
                    sl = slice(tt * E + hf * 2048, tt * E + (hf + 1) * 2048)
                    v.tensor_scalar(u16[:], gu[:, sl],
                                    stat[:, so + C_SX + tt:so + C_SX + tt + 1],
                                    1536.0, op0=ALU.mult, op1=ALU.add)
                    last = v.tensor_single_scalar(Igu[:, sl], u16[:], 1536.0,
                                                  ALU.subtract)
                return last
            step('vector', qgut, waits=(('v', now('v')),), incs=(('v', 1),))
            q_done = now('v')
            for b0 in range(0, AE, 8):
                pb, bi = (pT0, 0) if (b0 // 8) % 2 == 0 else (pT1, 1)
                def tr(t, tt=tt, b0=b0, pb=pb):
                    last = None
                    for k in range(8):
                        c0 = tt * E + (b0 + k) * 128
                        last = t.transpose(pb[:, k * 128:(k + 1) * 128],
                                           Igu[:, c0:c0 + 128], ident_bf[:])
                    return last
                step('tensor', tr,
                     waits=[('v', q_done)] + wz('IguT_free') + wait_bank(bi),
                     incs=(('t', 1),))
                tv = now('t')
                def cp(v, pb=pb, tt=tt, b0=b0):
                    dst = bass.AP(IguT, tt * 128 + b0 * T,
                                  [[AE * T, 128], [T, 8], [1, 128]])
                    src = bass.AP(pb, 0, [[1024, 128], [128, 8], [1, 128]])
                    return v.tensor_copy(dst, src)
                step('vector', cp, waits=[('t', tv)] + wz('wqo0', 'wqo1'),
                     incs=(('v', 1),))
                bank_free[bi] = ('v', now('v'))
                igut_b[(tt, b0 // 8)] = now('v')
        dep['IguT'] = ('v', now('v'))
        for c in range(8):
            l_ = c % 4
            wv = dma('sync', wlane[l_][:], s_d[:, c * 4096:(c + 1) * 4096], 'dw',
                     waits=wait_lane(l_))
            def mmd(t, c=c, l_=l_, wv=wv):
                t.wait_ge(sems[wv[0]], wv[1])
                b_ = (4 * c) // 8
                t.wait_ge(sems['v'], max(igut_b[(0, b_)], igut_b[(1, b_)]))
                last = None
                for al in range(4):
                    a = c * 4 + al
                    for tt in range(TSUB):
                        lhs = IguT[:, a * T + tt * 128:a * T + tt * 128 + 128]
                        for dc in range(2):
                            last = t.matmul(
                                bankF[SETA[2 * tt + dc]], lhs,
                                wlane[l_][:, al * 1024 + dc * 512:al * 1024 + (dc + 1) * 512],
                                start=(a == 0), stop=(a == 31))
                return last
            step('tensor', mmd,
                 waits=([w for b in SETA for w in wait_bank(b)])
                       if c == 0 else (),
                 incs=(('t', 1),))
            lane_free[l_] = ('t', now('t'))
        tv = now('t')
        dep['IguT_free'] = ('t', tv)
        dep['IxT_free'] = ('t', tv)
        if pre_epi is not None:
            pre_epi()
        def epi(v, r_=rbuf):
            last = None
            for tt in range(TSUB):
                for dc in range(2):
                    sl = slice(tt * DM + dc * 512, tt * DM + (dc + 1) * 512)
                    last = v.scalar_tensor_tensor(r_[:, sl],
                                                  bankF[SETA[2 * tt + dc]],
                                                  stat[:, so + C_CD + tt:
                                                       so + C_CD + tt + 1],
                                                  r_[:, sl],
                                                  op0=ALU.mult, op1=ALU.add)
            return last
        step('vector', epi, waits=(('t', tv),), incs=(('v', 1),))
        for b in SETA:
            bank_free[b] = ('v', now('v'))
        dep['r'] = ('v', now('v'))

    def mamba(rbuf):
        layer_norm(rbuf=rbuf)
        for tt in range(TSUB):
            for b0 in range(0, AD, 4):
                bi = 6 + (b0 // 4) % 2
                pb = bankF[bi]
                def trf(t, b0=b0, pb=pb, tt=tt):
                    last = None
                    for k in range(4):
                        c0 = tt * DM + (b0 + k) * 128
                        last = t.transpose(pb[:, k * 128:(k + 1) * 128],
                                           xln[:, c0:c0 + 128], ident_f[:])
                    return last
                step('tensor', trf, waits=wz('xln_w', 'xlT_free') + wait_bank(bi),
                     incs=(('t', 1),))
                tv = now('t')
                def hilo(v, pb=pb, b0=b0, tt=tt):
                    dsth = bass.AP(xlh, tt * 128 + b0 * T,
                                   [[AD * T, 128], [T, 4], [1, 128]])
                    dstl = bass.AP(xll, tt * 128 + b0 * T,
                                   [[AD * T, 128], [T, 4], [1, 128]])
                    src = pb.rearrange("p (b c) -> p b c", b=4)
                    v.tensor_copy(dsth, src)
                    return v.tensor_tensor(dstl, src, dsth, ALU.subtract)
                step('vector', hilo, waits=(('t', tv),), incs=(('v', 1),))
                bank_free[bi] = ('v', now('v'))
        dep['xlT'] = ('v', now('v'))
        for g in range(8):
            lh, ll = (g % 2) * 2, (g % 2) * 2 + 1
            wvh = dma('sync', wlane[lh][:], s_wih[:, g * 4096:(g + 1) * 4096], 'dw',
                      waits=wait_lane(lh))
            wvl = dma('sync', wlane[ll][:], s_wil[:, g * 4096:(g + 1) * 4096], 'dw',
                      waits=wait_lane(ll))
            bs = SETA if g % 2 == 0 else SETB
            def mmz(t, lh=lh, ll=ll, wvh=wvh, wvl=wvl, bs=bs):
                t.wait_ge(sems[wvh[0]], wvh[1])
                t.wait_ge(sems[wvl[0]], wvl[1])
                last = None
                for a in range(AD):
                    for el in range(4):
                        for p_, (wl_, xr_) in enumerate(((lh, xlh), (lh, xll),
                                                         (ll, xlh))):
                            rhs = xr_[:, a * T:(a + 1) * T]
                            lhs = wlane[wl_][:, a * 512 + el * 128:a * 512 + (el + 1) * 128]
                            last = t.matmul(bankF[bs[el]][:, 0:T], lhs, rhs,
                                            start=(a == 0 and p_ == 0),
                                            stop=(a == AD - 1 and p_ == 2))
                return last
            step('tensor', mmz,
                 waits=wz('xlT') + [w for b in bs for w in wait_bank(b)],
                 incs=(('t', 1),))
            tv = now('t')
            lane_free[lh] = ('t', tv)
            lane_free[ll] = ('t', tv)
            def evz(v, g=g, bs=bs):
                last = None
                for el in range(4):
                    e = 4 * g + el
                    if e < CT:
                        dst = xinp[:, e * XP + 3:e * XP + 3 + T]
                    else:
                        dst = z_t[:, (e - CT) * T:(e - CT + 1) * T]
                    last = v.tensor_copy(dst, bankF[bs[el]][:, 0:T])
                return last
            step('vector', evz, waits=(('t', tv),), incs=(('v', 1),))
            for b in bs:
                bank_free[b] = ('v', now('v'))
            if g < 4:
                def restg(v, g=g):
                    last = None
                    for ct in range(4 * g, 4 * g + 4):
                        last = v.tensor_copy(xinp[:, ct * XP:ct * XP + 3],
                                             halo[:, ct * 4:ct * 4 + 3])
                    return last
                def convg(v, g=g):
                    last = None
                    for ct in range(4 * g, 4 * g + 4):
                        p0 = ct * XP
                        dst = xa_t[:, ct * T:(ct + 1) * T]
                        v.tensor_scalar(dst, xinp[:, p0:p0 + T],
                                        convw[:, ct * 4:ct * 4 + 1], None,
                                        op0=ALU.mult)
                        for k in range(1, 4):
                            v.scalar_tensor_tensor(dst, xinp[:, p0 + k:p0 + k + T],
                                                   convw[:, ct * 4 + k:ct * 4 + k + 1],
                                                   dst, op0=ALU.mult, op1=ALU.add)
                        last = v.tensor_copy(halo[:, ct * 4:ct * 4 + 3],
                                             xinp[:, p0 + T:p0 + T + 3])
                    return last
                vchain([restg, convg])
                def silug(s, g=g):
                    last = None
                    for ct in range(4 * g, 4 * g + 4):
                        sl = slice(ct * T, (ct + 1) * T)
                        last = s.activation(xa_t[:, sl], xa_t[:, sl], ACT.Silu,
                                            bias=convb[:, ct:ct + 1], scale=1.0)
                    return last
                step('scalar', silug, waits=(('v', now('v')),), incs=(('s', 1),))
        dep['xa'] = ('s', now('s'))
        dep['xlT_free'] = ('t', now('t'))
        for ct in range(CT):
            def spl(v, ct=ct):
                o = (ct % 2) * T
                sl = slice(ct * T, (ct + 1) * T)
                v.tensor_copy(va_h[:, o:o + T], xa_t[:, sl])
                return v.tensor_tensor(va_l[:, o:o + T], xa_t[:, sl],
                                       va_h[:, o:o + T], ALU.subtract)
            step('vector', spl, waits=wz('xa', f'va{ct % 2}'), incs=(('v', 1),))
            vv_ = now('v')
            def mmp(t, ct=ct):
                o = (ct % 2) * T
                last = None
                for p_, (wl_, xr_) in enumerate(((wx_h, va_h), (wx_h, va_l),
                                                 (wx_l, va_h))):
                    last = t.matmul(bankF[6][0:96, 0:T],
                                    wl_[:, ct * 96:(ct + 1) * 96], xr_[:, o:o + T],
                                    start=(ct == 0 and p_ == 0),
                                    stop=(ct == CT - 1 and p_ == 2))
                return last
            step('tensor', mmp,
                 waits=[('v', vv_)] + (wait_bank(6) if ct == 0 else []),
                 incs=(('t', 1),))
            dep[f'va{ct % 2}'] = ('t', now('t'))
        tv = now('t')
        step('vector', lambda v: v.tensor_copy(ssm_sb[:], bankF[6][0:96, 0:T]),
             waits=(('t', tv),), incs=(('v', 1),))
        bank_free[6] = ('v', now('v'))
        vssm = now('v')
        def spl_dtr(v):
            v.tensor_copy(dtr_h[:], ssm_sb[0:64, :])
            return v.tensor_tensor(dtr_l[:], ssm_sb[0:64, :], dtr_h[:], ALU.subtract)
        step('vector', spl_dtr, incs=(('v', 1),))
        vdtr = now('v')
        for ct in range(CT):
            bi = 6 + ct % 2
            pb = bankF[bi]
            def mmdt(t, ct=ct, pb=pb):
                last = None
                for p_, (wl_, xr_) in enumerate(((wdt_h, dtr_h), (wdt_h, dtr_l),
                                                 (wdt_l, dtr_h))):
                    last = t.matmul(pb[:, 0:T], wl_[:, ct * 128:(ct + 1) * 128],
                                    xr_[:], start=(p_ == 0), stop=(p_ == 2))
                return last
            step('tensor', mmdt, waits=[('v', vdtr)] + wait_bank(bi),
                 incs=(('t', 1),))
            tv = now('t')
            def sp_s(s, ct=ct, pb=pb):
                o = (ct % 2) * 512
                s.activation(sps[:, o:o + T], pb[:, 0:T], ACT.Exp,
                             bias=bdt[:, ct:ct + 1], scale=1.0)
                return s.activation(dt_t[:, ct * T:(ct + 1) * T], sps[:, o:o + T],
                                    ACT.Ln, bias=1.0, scale=1.0)
            step('scalar', sp_s, waits=[('t', tv)] + wz(f'sps{ct % 2}'),
                 incs=(('s', 1),))
            dep[f'sps{ct % 2}'] = ('s', now('s'))
            bank_free[bi] = ('s', now('s'))
        dep['dt'] = ('s', now('s'))
        step('vector', lambda v: v.tensor_tensor(xa_t[:], dt_t[:], xa_t[:], ALU.mult),
             waits=wz('dt'), incs=(('v', 1),))
        dep['u'] = ('v', now('v'))
        wch = []
        def wout_dma(c):
            lh, ll = (c % 2) * 2, (c % 2) * 2 + 1
            wvh = dma('sync', wlane[lh][:], s_woh[:, c * 4096:(c + 1) * 4096], 'dw',
                      waits=wait_lane(lh))
            wvl = dma('sync', wlane[ll][:], s_wol[:, c * 4096:(c + 1) * 4096], 'dw',
                      waits=wait_lane(ll))
            wch.append((lh, ll, wvh, wvl))
        wout_dma(0)
        wout_dma(1)
        vo_ready = {}
        def rowdma(n):
            bv1 = dma('gpsimd', brow[0:1, (n % 2) * T:(n % 2 + 1) * T],
                      ssm_sb[64 + n:65 + n, :], 'dx',
                      waits=[('v', vssm)] + wz(f'row{n % 2}'))
            bv2 = dma('gpsimd', crow[0:1, (n % 2) * T:(n % 2 + 1) * T],
                      ssm_sb[80 + n:81 + n, :], 'dx')
            return [bv1, bv2]
        rowv = [rowdma(0)]
        pend_h = [None]

        def flush_h(s_=None):
            if pend_h[0] is None:
                return
            col_, ho_, vv_ = pend_h[0]
            pend_h[0] = None
            def hcp(s, col_=col_, ho_=ho_):
                return s.activation(h_state[:, col_:col_ + 1],
                                    h_b[:, ho_ + T - 1:ho_ + T],
                                    ACT.Copy, bias=0.0, scale=1.0)
            step('scalar', hcp, waits=(('v', vv_),), incs=(('s', 1),))
        bc_done = {}
        def bc_piece(n, k, waits):
            bb, bc_ = (4, 5) if n % 2 == 0 else (0, 1)
            o = (n % 2) * T
            bank = bb if k < 4 else bc_
            row = brow if k < 4 else crow
            c0 = (k % 4) * (T // 4)
            def pc(t, bank=bank, row=row, o=o, c0=c0):
                return t.matmul(bankF[bank][:, c0:c0 + T // 4], ones_f[:],
                                row[0:1, o + c0:o + c0 + T // 4],
                                start=True, stop=True)
            step('tensor', pc, waits=waits, incs=(('t', 1),))
            bc_done[n] = now('t')
        for k in range(8):
            w = (rowv[0] + wait_bank(4) + wait_bank(5)) if k == 0 else ()
            bc_piece(0, k, w)
        for n in range(NST):
            bmo = (n % 2) * T
            tbc = bc_done[n]
            dep[f'row{n % 2}'] = ('t', tbc)
            if n + 1 < NST:
                rowv.append(rowdma(n + 1))
            bb, bc_ = (4, 5) if n % 2 == 0 else (0, 1)
            def evbc(s, bb=bb, bc_=bc_, bmo=bmo):
                s.activation(bm_sb[:, bmo:bmo + T], bankF[bb][:, 0:T],
                             ACT.Copy, bias=0.0, scale=1.0)
                return s.activation(cm_sb[:, bmo:bmo + T], bankF[bc_][:, 0:T],
                                    ACT.Copy, bias=0.0, scale=1.0)
            step('scalar', evbc, waits=[('t', tbc)] + wz(f'bm{n % 2}'),
                 incs=(('s', 1),))
            bank_free[bb] = ('s', now('s'))
            bank_free[bc_] = ('s', now('s'))
            last_n = (n == NST - 1)

            vo_pend = [None]

            def flush_vo():
                if vo_pend[0] is None:
                    return
                ctf, scv = vo_pend[0]
                vo_pend[0] = None
                def vo_sub(v, ct=ctf):
                    ov_ = (ct % 8) * T
                    return v.tensor_tensor(vo_l[:, ov_:ov_ + T],
                                           vv_t[:, ct * T:(ct + 1) * T],
                                           vo_h[:, ov_:ov_ + T], ALU.subtract)
                step('vector', vo_sub, waits=(('s', scv),), incs=(('v', 1),))
                vo_ready[ctf] = now('v')
                if ctf % 4 == 3:
                    emit_mmo(ctf // 4)

            def emit_fin(ct):
                flush_vo()
                o = (ct % 2) * T
                def sz_s(s, ct=ct, o=o):
                    return s.activation(sz_b[:, o:o + T],
                                        z_t[:, ct * T:(ct + 1) * T],
                                        ACT.Silu, bias=0.0, scale=1.0)
                step('scalar', sz_s, waits=wz(f'sz{ct % 2}'), incs=(('s', 1),))
                svz = now('s')
                def fin(v, ct=ct, o=o):
                    dst = vv_t[:, ct * T:(ct + 1) * T]
                    v.scalar_tensor_tensor(dst,
                                           xinp[:, ct * XP + 3:ct * XP + 3 + T],
                                           dvec[:, ct:ct + 1], dst,
                                           op0=ALU.mult, op1=ALU.add)
                    return v.tensor_tensor(dst, dst, sz_b[:, o:o + T], ALU.mult)
                step('vector', fin,
                     waits=[('s', svz)] + wz(f'acc{ct}'), incs=(('v', 1),))
                dep[f'sz{ct % 2}'] = ('v', now('v'))
                def vo_cp(s, ct=ct):
                    ov_ = (ct % 8) * T
                    return s.activation(vo_h[:, ov_:ov_ + T],
                                        vv_t[:, ct * T:(ct + 1) * T],
                                        ACT.Copy, bias=0.0, scale=1.0)
                step('scalar', vo_cp,
                     waits=[('v', now('v'))] + wz(f'voT{(ct // 4) % 2}'),
                     incs=(('s', 1),))
                vo_pend[0] = (ct, now('s'))

            def emit_mmo(c):
                if True:
                    lh, ll, wvh, wvl = wch[c]
                    vd_ = max(vo_ready[ct_] for ct_ in range(4 * c, 4 * c + 4))
                    def mmo(t, c=c, lh=lh, ll=ll, wvh=wvh, wvl=wvl,
                            vd=vd_):
                        t.wait_ge(sems[wvh[0]], wvh[1])
                        t.wait_ge(sems[wvl[0]], wvl[1])
                        t.wait_ge(sems['v'], vd)
                        last = None
                        for cti in range(4 * c, 4 * c + 4):
                            ov_ = (cti % 8) * T
                            co = (cti % 4) * 1024
                            for tt in range(TSUB):
                                for p_, (vb, wl_) in enumerate(
                                        ((vo_h, lh), (vo_l, lh), (vo_h, ll))):
                                    lhs = vb[:, ov_ + tt * 128:ov_ + tt * 128 + 128]
                                    for dc in range(2):
                                        last = t.matmul(
                                            bankF[SETA[2 * tt + dc]], lhs,
                                            wlane[wl_][:, co + dc * 512:co + (dc + 1) * 512],
                                            start=(c == 0 and cti == 0 and p_ == 0),
                                            stop=(c == 3 and cti == 15 and p_ == 2))
                        return last
                    step('tensor', mmo,
                         waits=([w for b in SETA for w in wait_bank(b)]
                                if c == 0 else ()),
                         incs=(('t', 1),))
                    dep[f'voT{c % 2}'] = ('t', now('t'))
                    lane_free[lh] = ('t', now('t'))
                    lane_free[ll] = ('t', now('t'))
                    if c + 2 < 4:
                        wout_dma(c + 2)

            for ct in range(CT):
                o = (ct % 2) * T
                ho = (ct % 4) * T
                col = ct * NST + n
                def mk_dA(s, ct=ct, o=o, col=col):
                    return s.activation(dA_b[:, o:o + T],
                                        dt_t[:, ct * T:(ct + 1) * T],
                                        ACT.Exp, bias=0.0,
                                        scale=a_neg[:, col:col + 1])
                step('scalar', mk_dA, waits=wz(f'dA{ct % 2}'), incs=(('s', 1),))
                sv = now('s')
                flush_h()
                def scn(v, ct=ct, o=o, ho=ho, col=col, n=n, bmo=bmo):
                    v.tensor_tensor(dBx_b[:, o:o + T],
                                    xa_t[:, ct * T:(ct + 1) * T],
                                    bm_sb[:, bmo:bmo + T], ALU.mult)
                    v.tensor_tensor_scan(h_b[:, ho:ho + T], dA_b[:, o:o + T],
                                         dBx_b[:, o:o + T],
                                         h_state[:, col:col + 1],
                                         op0=ALU.mult, op1=ALU.add)
                    if n == 0:
                        return v.tensor_tensor(vv_t[:, ct * T:(ct + 1) * T],
                                               h_b[:, ho:ho + T],
                                               cm_sb[:, bmo:bmo + T], ALU.mult)
                    return v.tensor_tensor(P_b[:, (ct % 2) * T:(ct % 2) * T + T],
                                           h_b[:, ho:ho + T],
                                           cm_sb[:, bmo:bmo + T], ALU.mult)
                step('vector', scn,
                     waits=[('s', sv)] + (wz(f'P{ct % 2}') if n > 0 else []),
                     incs=(('v', 1),))
                dep[f'dA{ct % 2}'] = ('v', now('v'))
                pend_h[0] = (col, ho, now('v'))
                if n > 0:
                    def post(g_, ct=ct):
                        return g_.tensor_tensor(vv_t[:, ct * T:(ct + 1) * T],
                                                vv_t[:, ct * T:(ct + 1) * T],
                                                P_b[:, (ct % 2) * T:(ct % 2) * T + T],
                                                ALU.add)
                    step('gpsimd', post, waits=(('v', now('v')),), incs=(('g', 1),))
                    dep[f'P{ct % 2}'] = ('g', now('g'))
                    dep[f'acc{ct}'] = ('g', now('g'))
                else:
                    dep[f'acc{ct}'] = ('v', now('v'))
                if ct % 2 == 0 and n + 1 < NST:
                    k = ct // 2
                    nb, ncb = (4, 5) if (n + 1) % 2 == 0 else (0, 1)
                    w = [('v', now('v'))]
                    if k == 0:
                        w += rowv[n + 1] + wait_bank(nb) + wait_bank(ncb) + \
                             wz(f'bm{(n + 1) % 2}')
                    bc_piece(n + 1, k, w)
                if last_n and ct >= 2:
                    emit_fin(ct - 2)
            dep[f'bm{n % 2}'] = ('g', now('g'))
            if last_n:
                emit_fin(CT - 2)
                emit_fin(CT - 1)
                flush_vo()
        flush_h()
        dep['hstate'] = ('g', now('g'))
        def epo(v, r_=None):
            r_ = rbuf if r_ is None else r_
            last = None
            for tt in range(TSUB):
                for dc in range(2):
                    sl = slice(tt * DM + dc * 512, tt * DM + (dc + 1) * 512)
                    last = v.tensor_tensor(r_[:, sl], bankF[SETA[2 * tt + dc]],
                                           r_[:, sl], ALU.add)
            return last
        step('vector', epo, waits=(('t', now('t')),), incs=(('v', 1),))
        for b in SETA:
            bank_free[b] = ('v', now('v'))
        dep['r'] = ('v', now('v'))
        dep['xln'] = ('v', now('v'))

    bufs = (r_t, r_t2)
    prev_ov = [None]

    def prefetch(tile):
        if tile + 1 >= ntiles:
            return None, None
        nb = bufs[(tile + 1) % 2]
        w = [prev_ov[0]] if prev_ov[0] else []
        return nb, dma('gpsimd', nb[:], x_ap(d_x, tile + 1), 'dx', waits=w)

    ix_next = ffn_head(0, rwaits=[dep['rbuf0']], rbuf=bufs[0])
    for tile in range(ntiles):
        rb = bufs[tile % 2]
        ffn_body(ix_next, 0, rb)
        ix_next = None
        if stages >= 2:
            mamba(rb)
        nb, ldv = prefetch(tile)
        if stages >= 3:
            ix2 = ffn_head(24, rbuf=rb)
            if nb is not None:
                hold = {}
                def hook(nb=nb, ldv=ldv, hold=hold):
                    hold['ix'] = ffn_head(0, rwaits=[ldv], rbuf=nb)
                ffn_body(ix2, 24, rb, pre_epi=hook)
                ix_next = hold['ix']
            else:
                ffn_body(ix2, 24, rb)
        prev_ov[0] = dma('gpsimd', x_ap(d_out, tile), rb[:], 'dx', waits=wz('r'))
        if stages < 3 and nb is not None:
            ix_next = ffn_head(0, rwaits=[ldv], rbuf=nb)
    if debug:
        dbg_specs = {"sc_bc": sc_bc, "stat": stat, "xln": xln, "Ix": Ix,
                     "IxT": IxT, "gu": gu, "Igu": Igu, "r_t": r_t,
                     "sps": sps, "u16": u16, "wl0": wlane[0], "IguT": IguT,
                     "xinp": xinp, "z_t": z_t, "xa_t": xa_t, "dt_t": dt_t,
                     "vv_t": vv_t, "ssm_sb": ssm_sb, "h_state": h_state,
                     "a_neg": a_neg, "brow": brow, "crow": crow}
        wall = [(s_, v_) for s_, v_ in cnt.items()]
        first_dbg = True
        for nm, tns in dbg_specs.items():
            shp = list(tns.shape) if hasattr(tns, 'shape') else None
            dt_ = tns.dtype
            dd = nc.dram_tensor("dbg_" + nm, shp, dt_, kind="ExternalOutput").ap()
            dma('gpsimd', dd, tns[:], 'dx', waits=tuple(wall) if first_dbg else ())
            first_dbg = False
    step('gpsimd', lambda g: g.memset(srow[0:1, 6:8], 0.0),
         waits=tuple(qdone('dx')) + tuple(qdone('dw')))

    with ExitStack() as st:
        for n_ in sorted(cnt.keys()):
            sems[n_] = st.enter_context(nc.semaphore(n_))
        block = st.enter_context(nc.Block())

        def section(engname):
            def body(eng):
                for e, waits, fn, incs in ops:
                    if e != engname:
                        continue
                    for sname, val in waits:
                        eng.wait_ge(sems[sname], val)
                    inst = fn(eng)
                    for sname, by in incs:
                        inst.then_inc(sems[sname], by)
            return body

        block.sync(section('sync'))
        block.gpsimd(section('gpsimd'))
        block.vector(section('vector'))
        block.scalar(section('scalar'))
        block.tensor(section('tensor'))
    return nc


_NC_CACHE = {}


def kernel(L0_attn_x, L0_attn_W_in, L0_attn_conv_w, L0_attn_conv_b, L0_attn_W_x,
           L0_attn_W_dt, L0_attn_b_dt, L0_attn_A_log, L0_attn_D, L0_attn_W_out,
           L0_ffn_w_gate, L0_ffn_w_up, L0_ffn_w_down):
    f32 = np.float32

    def pm(w, atiles):
        r, c = w.shape
        assert r == atiles * 128, (r, atiles)
        return np.ascontiguousarray(
            np.asarray(w, f32).reshape(atiles, 128, c).transpose(1, 0, 2)
            .reshape(128, atiles * c))

    wgT = pm(np.asarray(L0_ffn_w_gate, f32).T, AD)
    wuT = pm(np.asarray(L0_ffn_w_up, f32).T, AD)
    wdT = pm(np.asarray(L0_ffn_w_down, f32).T, AE)
    win = pm(L0_attn_W_in, AD)
    wout = pm(L0_attn_W_out, CT)
    wx = pm(L0_attn_W_x, CT)
    alog = pm(L0_attn_A_log, CT)
    convw = pm(L0_attn_conv_w, CT)
    convb = pm(np.asarray(L0_attn_conv_b, f32).reshape(DI, 1), CT)
    dd = pm(np.asarray(L0_attn_D, f32).reshape(DI, 1), CT)
    bdt = pm(np.asarray(L0_attn_b_dt, f32).reshape(DI, 1), CT)
    wdt = np.ascontiguousarray(np.asarray(L0_attn_W_dt, f32))

    key = (_STAGES, _NTILES)
    if key not in _NC_CACHE:
        _NC_CACHE[key] = build_program(_STAGES, _NTILES)
    nc = _NC_CACHE[key]

    x = np.asarray(L0_attn_x, f32)
    in_maps = []
    for b in range(8):
        in_maps.append({
            "x": np.ascontiguousarray(x[b]),
            "wgT": wgT, "wuT": wuT, "wdT": wdT,
            "win": win, "wout": wout, "wx": wx, "wdt": wdt,
            "alog": alog, "convw": convw, "convb": convb,
            "dvec": dd, "bdt": bdt,
        })
    res = run_bass_kernel_spmd(nc, in_maps, core_ids=list(range(8)))
    return np.stack([np.asarray(r["out"], f32) for r in res.results], axis=0)


# version 33
# speedup vs baseline: 1.0346x; 1.0346x over previous
import numpy as np
import ml_dtypes
from contextlib import ExitStack
import concourse.bass as bass
import concourse.mybir as mybir
from concourse.bass_utils import run_bass_kernel_spmd

F32 = mybir.dt.float32
BF16 = mybir.dt.bfloat16
F16 = mybir.dt.float16
ALU = mybir.AluOpType
ACT = mybir.ActivationFunctionType
AX = mybir.AxisListType

S, DM, E, DI, NST = 4096, 1024, 4096, 2048, 16
T = 256
NT = S // T
TSUB = T // 128
CT = DI // 128
AD = DM // 128
AE = E // 128
XP = T + 4
_STAGES = 3
_NTILES = NT
_DEBUG = False

C_SUM, C_SQ, C_MEAN, C_MSQ, C_VAR, C_VE, C_S0, C_T2 = 0, 2, 4, 6, 8, 10, 12, 14
C_MX, C_SX, C_ISX, C_CG, C_CU, C_CD, C_T3, C_ACC = 16, 18, 20, 22, 24, 26, 28, 32


def build_program(stages=3, ntiles=None, debug=None):
    ntiles = ntiles or _NTILES
    debug = _DEBUG if debug is None else debug
    nc = bass.Bass("TRN2", target_bir_lowering=False, detect_race_conditions=False)
    ops = []
    cnt = {}
    sems = {}

    def step(eng, fn, waits=(), incs=()):
        ops.append((eng, tuple(waits), fn, tuple(incs)))
        for s_, by in incs:
            cnt[s_] = cnt.get(s_, 0) + by

    def now(s_):
        return cnt.get(s_, 0)

    d_x = nc.dram_tensor("x", [S, DM], F32, kind="ExternalInput").ap()
    d_out = nc.dram_tensor("out", [S, DM], F32, kind="ExternalOutput").ap()
    d_wg = nc.dram_tensor("wgT", [128, AD * E], F32, kind="ExternalInput").ap()
    d_wu = nc.dram_tensor("wuT", [128, AD * E], F32, kind="ExternalInput").ap()
    d_wd = nc.dram_tensor("wdT", [128, AE * DM], F32, kind="ExternalInput").ap()
    d_win = nc.dram_tensor("win", [128, AD * E], F32, kind="ExternalInput").ap()
    d_wout = nc.dram_tensor("wout", [128, CT * DM], F32, kind="ExternalInput").ap()
    d_wx = nc.dram_tensor("wx", [128, CT * 96], F32, kind="ExternalInput").ap()
    d_wdt = nc.dram_tensor("wdt", [64, DI], F32, kind="ExternalInput").ap()
    d_alog = nc.dram_tensor("alog", [128, CT * NST], F32, kind="ExternalInput").ap()
    d_convw = nc.dram_tensor("convw", [128, CT * 4], F32, kind="ExternalInput").ap()
    d_convb = nc.dram_tensor("convb", [128, CT], F32, kind="ExternalInput").ap()
    d_dd = nc.dram_tensor("dvec", [128, CT], F32, kind="ExternalInput").ap()
    d_bdt = nc.dram_tensor("bdt", [128, CT], F32, kind="ExternalInput").ap()
    s_gu = nc.dram_tensor("s_gu", [128, AD * 2 * E], BF16, kind="Internal").ap()
    s_d = nc.dram_tensor("s_d", [128, AE * DM], BF16, kind="Internal").ap()
    s_wih = nc.dram_tensor("s_wih", [128, AD * E], BF16, kind="Internal").ap()
    s_wil = nc.dram_tensor("s_wil", [128, AD * E], BF16, kind="Internal").ap()
    s_woh = nc.dram_tensor("s_woh", [128, CT * DM], BF16, kind="Internal").ap()
    s_wol = nc.dram_tensor("s_wol", [128, CT * DM], BF16, kind="Internal").ap()

    A = nc.alloc_sbuf_tensor
    ident_bf = A("ident_bf", [128, 128], BF16)
    ident_f = A("ident_f", [128, 128], F32)
    ones_f = A("ones_f", [1, 128], F32)
    iotp = A("iotp", [128, 1], F32)
    iotf = A("iotf", [128, 128], F32)
    a_neg = A("a_neg", [128, CT * NST], F32)
    convw = A("convw_s", [128, CT * 4], F32)
    convb = A("convb_s", [128, CT], F32)
    dvec = A("dvec_s", [128, CT], F32)
    bdt = A("bdt_s", [128, CT], F32)
    wx_h = A("wx_h", [128, CT * 96], BF16)
    wx_l = A("wx_l", [128, CT * 96], BF16)
    wdt_h = A("wdt_h", [64, DI], BF16)
    wdt_l = A("wdt_l", [64, DI], BF16)
    h_state = A("h_state", [128, CT * NST], F32)
    stat = A("stat", [128, 64], F32)
    sc_bc = A("sc_bc", [128, 8], F32)
    srow = A("srow", [1, 8], F32)
    brow = A("brow", [1, 2 * T], F32)
    crow = A("crow", [1, 2 * T], F32)
    wlane = [A(f"wlane{i}", [128, 4096], BF16) for i in range(4)]
    r_t = A("r_t", [128, TSUB * DM], F32)
    xln = A("xln", [128, TSUB * DM], F32)

    Ix = A("Ix", [128, TSUB * DM], BF16)
    IxT = A("IxT", [128, AD * T], BF16)
    xlh = A("xlh", [128, AD * T], BF16)
    xll = A("xll", [128, AD * T], BF16)
    dA_b = A("dA_b", [128, 2 * T], F32)
    dBx_b = A("dBx_b", [128, 2 * T], F32)
    h_b = A("h_b", [128, 4 * T], F32)
    u16 = nc.alloc_sbuf_tensor_at("u16", [128, 2048], F16,
                                  offset=nc.lookup_mloc(h_b).addr)
    P_b = A("P_b", [128, 2 * T], F32)
    bm_sb = A("bm_sb", [128, 2 * T], F32)
    cm_sb = A("cm_sb", [128, 2 * T], F32)
    r_t2 = A("r_t2", [128, TSUB * DM], F32)
    sz_b = A("sz_b", [128, 2 * T], F32)
    ssm_sb = A("ssm_sb", [96, T], F32)
    dtr_h = A("dtr_h", [64, T], BF16)
    dtr_l = A("dtr_l", [64, T], BF16)
    va_h = A("va_h", [128, 2 * T], BF16)
    va_l = A("va_l", [128, 2 * T], BF16)
    vo_h = A("vo_h", [128, 8 * T], BF16)
    vo_l = A("vo_l", [128, 8 * T], BF16)
    sps = A("sps", [128, 1024], F32)
    sqd = nc.alloc_sbuf_tensor_at("sqd", [128, DM], F32,
                                  offset=nc.lookup_mloc(sps).addr)
    halo = A("halo", [128, CT * 4], F32)
    AR = 82176
    arena = A("arena", [128, AR], mybir.dt.uint8)
    base = nc.lookup_mloc(arena).addr

    def at(name, shape, dtype, off):
        return nc.alloc_sbuf_tensor_at(name, shape, dtype, offset=base + off)

    xinp = at("xinp", [128, CT * XP], F32, 0)
    z_t = at("z_t", [128, CT * T], F32, 16640)
    xa_t = at("xa_t", [128, CT * T], F32, 33024)
    dt_t = at("dt_t", [128, CT * T], F32, 49408)
    vv_t = at("vv_t", [128, CT * T], F32, 65792)
    gu = at("gu", [128, TSUB * E], F32, 0)
    Igu = at("Igu", [128, TSUB * E], BF16, 32768)
    IguT = at("IguT", [128, AE * T], BF16, 49152)
    wq_in = at("wq_in", [128, 4096], F32, 0)
    wq_t1 = at("wq_t1", [128, 4096], F32, 16384)
    wq_t2 = at("wq_t2", [128, 4096], F32, 32768)
    wq_o1 = at("wq_o1", [128, 4096], BF16, 49152)
    wq_o2 = at("wq_o2", [128, 4096], BF16, 57344)
    wq_t3 = at("wq_t3", [128, 4096], F32, 65792)

    rem = nc.sbuf_bytes_remaining
    assert rem >= 0, f"sbuf over budget: {rem}"

    pT0 = nc.alloc_psum_tensor("pT0", [128, 1024], BF16)
    pT1 = nc.alloc_psum_tensor("pT1", [128, 1024], BF16)
    pF = [nc.alloc_psum_tensor(f"pF{i}", [128, 512], F32) for i in range(6)]
    bankF = [pT0[:, 0:1024].bitcast(F32), pT1[:, 0:1024].bitcast(F32)] + \
            [pF[i][:, 0:512] for i in range(6)]
    SETA, SETB = (2, 3, 4, 5), (6, 7, 0, 1)

    bank_free = {i: None for i in range(8)}
    lane_free = {i: None for i in range(4)}
    dep = {}

    def wait_bank(b):
        return [bank_free[b]] if bank_free[b] else []

    def wait_lane(i):
        return [lane_free[i]] if lane_free[i] else []

    def wz(*names):
        seen = {}
        for n_ in names:
            if n_ in dep:
                s_, v_ = dep[n_]
                seen[s_] = max(seen.get(s_, 0), v_)
        return [(s_, v_) for s_, v_ in seen.items()]

    QKS = {'dw': 8, 'dx': 4}
    qstate = {q: [0, [0] * k] for q, k in QKS.items()}

    def dma(qeng, out_ap, in_ap, q, waits=()):
        st_ = qstate[q]
        k = st_[0] % QKS[q]
        st_[0] += 1
        sem = f'{q}{k}'
        w = list(waits)
        if st_[1][k]:
            w.append((sem, st_[1][k]))
        step(qeng, lambda e, o=out_ap, i=in_ap: e.dma_start(o, i),
             waits=w, incs=((sem, 16),))
        st_[1][k] = now(sem)
        return (sem, st_[1][k])

    def qdone(q):
        return [(f'{q}{k}', v) for k, v in enumerate(qstate[q][1]) if v]

    SC = lambda c: stat[:, c:c + 1]
    SC2 = lambda c: stat[:, c:c + 2]

    def vchain(fns, waits=()):
        for i_, fn_ in enumerate(fns):
            w = list(waits) if i_ == 0 else []
            w.append(('v', now('v')))
            step('vector', fn_, waits=w, incs=(('v', 1),))

    step('gpsimd', lambda g: g.iota(iotp[:], [[0, 1]], channel_multiplier=1,
                                    allow_small_or_imprecise_dtypes=True))
    step('gpsimd', lambda g: g.iota(iotf[:], [[1, 128]], channel_multiplier=0,
                                    allow_small_or_imprecise_dtypes=True),
         incs=(('g', 1),))
    for dst, src in ((a_neg[:], d_alog), (convw[:], d_convw), (convb[:], d_convb),
                     (dvec[:], d_dd), (bdt[:], d_bdt),
                     (wq_t1[:, 0:CT * 96], d_wx), (wq_t2[0:64, 0:DI], d_wdt)):
        dma('gpsimd', dst, src, 'dx')
    v_small = qdone('dx')
    def x_ap(d, tile):
        return bass.AP(d.tensor, tile * T * DM,
                       [[DM, 128], [128 * DM, TSUB], [1, DM]])
    dep['rbuf0'] = dma('gpsimd', r_t[:], x_ap(d_x, 0), 'dx')

    def vsetup(v):
        v.tensor_scalar(ident_bf[:], iotf[:], iotp[:, 0:1], None, op0=ALU.is_equal)
        v.tensor_scalar(ident_f[:], iotf[:], iotp[:, 0:1], None, op0=ALU.is_equal)
        v.memset(ones_f[:], 1.0)
        v.memset(h_state[:], 0.0)
        v.memset(stat[:, C_ACC + 3:C_ACC + 4], 0.0)
        v.memset(halo[:], 0.0)
        v.tensor_copy(wx_h[:], wq_t1[:, 0:CT * 96])
        v.tensor_tensor(wx_l[:], wq_t1[:, 0:CT * 96], wx_h[:], ALU.subtract)
        v.tensor_copy(wdt_h[:], wq_t2[0:64, 0:DI])
        return v.tensor_tensor(wdt_l[:], wq_t2[0:64, 0:DI], wdt_h[:], ALU.subtract)
    step('vector', vsetup, waits=[('g', 1)] + v_small, incs=(('v', 1),))
    dep['wq0'] = ('v', now('v'))
    dep['wq1'] = ('v', now('v'))
    ldbuf = (wq_in, wq_t1, wq_t3)
    ldi = [0]

    def ld_next():
        i_ = ldi[0] % 3
        ldi[0] += 1
        return ldbuf[i_], f'wq{i_}'
    step('scalar', lambda s: s.activation(a_neg[:], a_neg[:], ACT.Exp,
                                          bias=0.0, scale=1.0),
         waits=v_small, incs=(('s', 1),))
    step('vector', lambda v: v.tensor_single_scalar(a_neg[:], a_neg[:], -1.0, ALU.mult),
         waits=(('s', now('s')),), incs=(('v', 1),))

    raws = [d_wg, d_wu, d_wd]
    for j in range(3):
        for c in range(8):
            buf, gn = ld_next()
            wv = dma('sync', buf[:], raws[j][:, c * 4096:(c + 1) * 4096], 'dw',
                     waits=wz(gn))
            def red1(v, b=buf):
                return v.reduce_sum(SC(C_T3), b[:], axis=AX.X,
                                    apply_absolute_value=True)
            def red2(v, j=j, c=c):
                if c == 0:
                    return v.tensor_copy(SC(C_ACC + j), SC(C_T3))
                return v.tensor_tensor(SC(C_ACC + j), SC(C_ACC + j), SC(C_T3), ALU.add)
            vchain([red1, red2], waits=(wv,))
            dep[gn] = ('v', now('v'))
    step('tensor', lambda t: t.transpose(pF[4][0:4, 0:128], stat[:, C_ACC:C_ACC + 4],
                                         ident_f[:]),
         waits=wz('wq0', 'wq1', 'wq2'), incs=(('t', 1),))
    vchain([
        lambda v: v.reduce_sum(stat[0:4, 36:37], pF[4][0:4, 0:128], axis=AX.X),
        lambda v: v.tensor_scalar(stat[0:4, 37:38], stat[0:4, 36:37],
                                  1.0 / (DM * E), None, op0=ALU.mult),
        lambda v: v.tensor_scalar(stat[0:4, 38:39], stat[0:4, 37:38], 1e-5, None,
                                  op0=ALU.max),
        lambda v: v.reciprocal(stat[0:4, 39:40], stat[0:4, 38:39]),
        lambda v: v.reciprocal(stat[0:4, 40:41], stat[0:4, 39:40]),
    ], waits=(('t', now('t')),))
    bank_free[6] = ('v', now('v'))
    rv1 = dma('gpsimd', srow[0:1, 0:3], stat[0:3, 39:40], 'dx', waits=(('v', now('v')),))
    rv2 = dma('gpsimd', srow[0:1, 3:6], stat[0:3, 40:41], 'dx')
    step('tensor', lambda t: t.matmul(pF[5][:, 0:6], ones_f[:], srow[0:1, 0:6],
                                      start=True, stop=True),
         waits=[rv1, rv2] + wait_bank(7), incs=(('t', 1),))
    step('vector', lambda v: v.tensor_copy(sc_bc[:, 0:6], pF[5][:, 0:6]),
         waits=(('t', now('t')),), incs=(('v', 1),))
    bank_free[7] = ('v', now('v'))
    dep['sc_bc'] = ('v', now('v'))

    for j in range(3):
        for c in range(8):
            pin, gn = ld_next()
            out = wq_o1 if c % 2 == 0 else wq_o2
            for h in range(2):
                sl = slice(h * 2048, (h + 1) * 2048)
                wv = dma('sync', pin[:, sl],
                         raws[j][:, c * 4096 + h * 2048:
                                 c * 4096 + (h + 1) * 2048], 'dw',
                         waits=wz(gn))
                def quant(v, pin=pin, out=out, j=j, sl=sl):
                    v.tensor_scalar(wq_t2[:, sl], pin[:, sl], sc_bc[:, j:j + 1],
                                    None, op0=ALU.mult)
                    v.tensor_single_scalar(out[:, sl], wq_t2[:, sl], -0.5,
                                           ALU.is_le)
                    return v.scalar_tensor_tensor(out[:, sl], wq_t2[:, sl],
                                                  0.5, out[:, sl],
                                                  op0=ALU.is_ge,
                                                  op1=ALU.subtract)
                step('vector', quant,
                     waits=[wv] + (wz('sc_bc', f'wqo{c % 2}')
                                   if h == 0 else []), incs=(('v', 1),))
            dep[gn] = ('v', now('v'))
            vq = now('v')
            if j < 2:
                dst = bass.AP(s_gu.tensor, c * 8192 + j * 512,
                              [[AD * 2 * E, 128], [1024, 8], [1, 512]])
            else:
                dst = s_d[:, c * 4096:(c + 1) * 4096]
            dep[f'wqo{c % 2}'] = dma('sync', dst, out[:], 'dw', waits=(('v', vq),))
    for (raw, nch, dh, dl, grouped) in ((d_win, 8, s_wih, s_wil, True),
                                        (d_wout, 4, s_woh, s_wol, False)):
        for c in range(nch):
            pin, gn = ld_next()
            for h in range(2):
                sl = slice(h * 2048, (h + 1) * 2048)
                wv = dma('sync', pin[:, sl],
                         raw[:, c * 4096 + h * 2048:
                             c * 4096 + (h + 1) * 2048], 'dw',
                         waits=wz(gn))
                def split(v, pin=pin, sl=sl):
                    v.tensor_copy(wq_o1[:, sl], pin[:, sl])
                    return v.tensor_tensor(wq_o2[:, sl], pin[:, sl],
                                           wq_o1[:, sl], ALU.subtract)
                step('vector', split,
                     waits=[wv] + (wz('wqo0', 'wqo1') if h == 0 else []),
                     incs=(('v', 1),))
            dep[gn] = ('v', now('v'))
            vq = now('v')
            if grouped:
                dsth = bass.AP(s_wih.tensor, c * 512,
                               [[AD * E, 128], [4096, 8], [1, 512]])
                dstl = bass.AP(s_wil.tensor, c * 512,
                               [[AD * E, 128], [4096, 8], [1, 512]])
            else:
                dsth = s_woh[:, c * 4096:(c + 1) * 4096]
                dstl = s_wol[:, c * 4096:(c + 1) * 4096]
            dma('sync', dsth, wq_o1[:], 'dw', waits=(('v', vq),))
            ov = dma('sync', dstl, wq_o2[:], 'dw')
            dep['wqo0'] = ov
            dep['wqo1'] = ov
    for i_, pv in enumerate(qdone('dw')):
        dep[f'prol{i_}'] = pv

    cur_r = [r_t]

    def layer_norm(rwaits=None, rbuf=None):
        r_ = rbuf
        def ln_s(s):
            last = None
            for tt in range(TSUB):
                s.activation(xln[:, tt * DM:(tt + 1) * DM],
                             r_[:, tt * DM:(tt + 1) * DM],
                             ACT.Identity, bias=0.0, scale=1.0,
                             accum_out=stat[:, C_SUM + tt:C_SUM + tt + 1])
                last = s.activation(sqd[:], r_[:, tt * DM:(tt + 1) * DM],
                                    ACT.Square, bias=0.0, scale=1.0,
                                    accum_out=stat[:, C_SQ + tt:C_SQ + tt + 1])
            return last
        step('scalar', ln_s,
             waits=(wz('r') if rwaits is None else list(rwaits)) +
                   wz('xln', 'sps0', 'sps1'), incs=(('s', 1),))
        sv = now('s')
        vchain([
            lambda v: (v.tensor_scalar(SC2(C_MEAN), SC2(C_SUM), 1.0 / DM, None,
                                       op0=ALU.mult),
                       v.tensor_scalar(SC2(C_MSQ), SC2(C_SQ), 1.0 / DM, None,
                                       op0=ALU.mult))[-1],
            lambda v: v.tensor_tensor(SC2(C_VAR), SC2(C_MEAN), SC2(C_MEAN), ALU.mult),
            lambda v: v.tensor_tensor(SC2(C_VAR), SC2(C_MSQ), SC2(C_VAR), ALU.subtract),
            lambda v: v.tensor_scalar(SC2(C_VE), SC2(C_VAR), 1e-5, None, op0=ALU.add),
        ], waits=(('s', sv),))
        def sqrt_ln(s):
            s.activation(SC2(C_T2), SC2(C_VE), ACT.Ln, bias=0.0, scale=1.0)
            return s.activation(SC2(C_S0), SC2(C_T2), ACT.Exp, bias=0.0, scale=0.5)
        step('scalar', sqrt_ln, waits=(('v', now('v')),), incs=(('s', 1),))
        sv = now('s')
        def ln_apply(v):
            last = None
            for tt in range(TSUB):
                sl = slice(tt * DM, (tt + 1) * DM)
                last = v.tensor_scalar(xln[:, sl], xln[:, sl],
                                       stat[:, C_MEAN + tt:C_MEAN + tt + 1],
                                       stat[:, C_T2 + tt:C_T2 + tt + 1],
                                       op0=ALU.subtract, op1=ALU.mult)
            return last
        vchain([lambda v: v.reciprocal(SC2(C_T2), SC2(C_S0)), ln_apply],
               waits=(('s', sv),))
        dep['xln_w'] = ('v', now('v'))

    def ffn_head(so, rwaits=None, rbuf=None):
        layer_norm(rwaits=rwaits, rbuf=rbuf)
        def qtail(v):
            last = None
            for tt in range(TSUB):
                sl = slice(tt * DM, (tt + 1) * DM)
                v.tensor_scalar(u16[:, 0:DM], xln[:, sl],
                                stat[:, so + C_SX + tt:so + C_SX + tt + 1],
                                1536.0, op0=ALU.mult, op1=ALU.add)
                last = v.tensor_single_scalar(Ix[:, sl], u16[:, 0:DM], 1536.0,
                                              ALU.subtract)
            return last
        vchain([
            lambda v: (v.reduce_max(SC(so + C_MX), xln[:, 0:DM], axis=AX.X,
                                    apply_absolute_value=True),
                       v.reduce_max(SC(so + C_MX + 1), xln[:, DM:2 * DM],
                                    axis=AX.X,
                                    apply_absolute_value=True))[-1],
            lambda v: v.tensor_scalar(SC2(so + C_MX), SC2(so + C_MX), 1e-5,
                                      None, op0=ALU.max),
            lambda v: v.reciprocal(SC2(so + C_SX), SC2(so + C_MX)),
            lambda v: v.tensor_scalar(SC2(so + C_SX), SC2(so + C_SX), 127.0,
                                      None, op0=ALU.mult),
            lambda v: v.reciprocal(SC2(so + C_ISX), SC2(so + C_SX)),
            lambda v: (v.tensor_scalar(SC2(so + C_CG), SC2(so + C_ISX),
                                       sc_bc[:, 3:4], None, op0=ALU.mult),
                       v.tensor_scalar(SC2(so + C_CU), SC2(so + C_ISX),
                                       sc_bc[:, 4:5], None, op0=ALU.mult))[-1],
            qtail,
        ], waits=wz('sc_bc', 'Ix_free'))
        dep['Ix_w'] = ('v', now('v'))
        dep['xln'] = ('v', now('v'))
        ixt_a = []
        for a in range(AD):
            pb, bi = (pT0, 0) if a % 2 == 0 else (pT1, 1)
            def tr(t, a=a, pb=pb):
                t.transpose(pb[:, 0:128],
                            Ix[:, a * 128:a * 128 + 128], ident_bf[:])
                return t.transpose(pb[:, 128:256],
                                   Ix[:, DM + a * 128:DM + a * 128 + 128],
                                   ident_bf[:])
            step('tensor', tr,
                 waits=(wz('Ix_w', 'IxT_free') if a < 2 else []) + wait_bank(bi),
                 incs=(('t', 1),))
            tv = now('t')
            def cp(v, pb=pb, a=a):
                dst = bass.AP(IxT, a * T, [[AD * T, 128], [128, 2], [1, 128]])
                src = bass.AP(pb, 0, [[1024, 128], [128, 2], [1, 128]])
                return v.tensor_copy(dst, src)
            step('vector', cp, waits=(('t', tv),), incs=(('v', 1),))
            bank_free[bi] = ('v', now('v'))
            ixt_a.append(now('v'))
        dep['IxT'] = ('v', now('v'))
        dep['Ix_free'] = ('t', now('t'))
        return ixt_a

    def ffn_body(ixt_a, so, rbuf, pre_epi=None):
        for seg in range(8):
            lanes = [(seg % 2) * 2, (seg % 2) * 2 + 1]
            ch_done = []
            for a in range(AD):
                ln_ = lanes[a // 4]
                off = (a % 4) * 1024
                wv = dma('sync', wlane[ln_][:, off:off + 1024],
                         s_gu[:, a * 8192 + seg * 1024:a * 8192 + (seg + 1) * 1024],
                         'dw', waits=wait_lane(ln_) if a % 4 == 0 else ())
                ch_done.append(wv)
            bs = SETA if seg % 2 == 0 else SETB
            def mm(t, lanes=lanes, ch_done=ch_done, bs=bs):
                last = None
                for a in range(AD):
                    t.wait_ge(sems[ch_done[a][0]], ch_done[a][1])
                    t.wait_ge(sems['v'], ixt_a[a])
                    ln_ = wlane[lanes[a // 4]]
                    off = (a % 4) * 1024
                    for tt in range(TSUB):
                        lhs = IxT[:, a * T + tt * 128:a * T + tt * 128 + 128]
                        last = t.matmul(bankF[bs[2 * tt]], lhs,
                                        ln_[:, off:off + 512],
                                        start=(a == 0), stop=(a == AD - 1))
                        last = t.matmul(bankF[bs[2 * tt + 1]], lhs,
                                        ln_[:, off + 512:off + 1024],
                                        start=(a == 0), stop=(a == AD - 1))
                return last
            step('tensor', mm,
                 waits=[w for b in bs for w in wait_bank(b)],
                 incs=(('t', 1),))
            tv = now('t')
            for l_ in lanes:
                lane_free[l_] = ('t', tv)
            for tt in range(TSUB):
                spo = tt * 512
                def ev_s(s, tt=tt, spo=spo, bs=bs):
                    return s.activation(sps[:, spo:spo + 512], bankF[bs[2 * tt]],
                                        ACT.Silu, bias=0.0,
                                        scale=stat[:, so + C_CG + tt:
                                                   so + C_CG + tt + 1])
                step('scalar', ev_s, waits=[('t', tv)] + wz(f'sps{tt}'),
                     incs=(('s', 1),))
                sv = now('s')
                bank_free[bs[2 * tt]] = ('s', sv)
                def ev_v(v, tt=tt, seg=seg, spo=spo, bs=bs):
                    dst = gu[:, tt * E + seg * 512:tt * E + (seg + 1) * 512]
                    return v.scalar_tensor_tensor(dst, bankF[bs[2 * tt + 1]],
                                                  stat[:, so + C_CU + tt:
                                                       so + C_CU + tt + 1],
                                                  sps[:, spo:spo + 512],
                                                  op0=ALU.mult, op1=ALU.mult)
                def ev_m1(v, tt=tt, seg=seg):
                    sl = slice(tt * E + seg * 512, tt * E + (seg + 1) * 512)
                    return v.reduce_max(SC(C_T3), gu[:, sl], axis=AX.X,
                                        apply_absolute_value=True)
                def ev_m2(v, tt=tt, seg=seg):
                    mx = slice(so + C_MX + tt, so + C_MX + tt + 1)
                    if seg == 0:
                        return v.tensor_copy(stat[:, mx], SC(C_T3))
                    return v.tensor_tensor(stat[:, mx], stat[:, mx],
                                           SC(C_T3), ALU.max)
                step('vector', ev_v,
                     waits=[('s', sv)] +
                           (wz('wq0', 'wq1') if seg == 0 else []),
                     incs=(('v', 1),))
                vchain([ev_m1, ev_m2])
                dep[f'sps{tt}'] = ('v', now('v'))
                bank_free[bs[2 * tt + 1]] = ('v', now('v'))
        vchain([
            lambda v: v.tensor_scalar(SC2(so + C_MX), SC2(so + C_MX), 1e-5,
                                      None, op0=ALU.max),
            lambda v: v.reciprocal(SC2(so + C_SX), SC2(so + C_MX)),
            lambda v: v.tensor_scalar(SC2(so + C_SX), SC2(so + C_SX), 127.0,
                                      None, op0=ALU.mult),
            lambda v: v.reciprocal(SC2(so + C_ISX), SC2(so + C_SX)),
            lambda v: v.tensor_scalar(SC2(so + C_CD), SC2(so + C_ISX),
                                      sc_bc[:, 5:6], None, op0=ALU.mult),
            lambda v: v.tensor_scalar(SC2(so + C_CD), SC2(so + C_CD), 0.5,
                                      None, op0=ALU.mult),
        ])
        igut_b = {}
        for tt in range(TSUB):
            def qgut(v, tt=tt):
                last = None
                for hf in range(2):
                    sl = slice(tt * E + hf * 2048, tt * E + (hf + 1) * 2048)
                    v.tensor_scalar(u16[:], gu[:, sl],
                                    stat[:, so + C_SX + tt:so + C_SX + tt + 1],
                                    1536.0, op0=ALU.mult, op1=ALU.add)
                    last = v.tensor_single_scalar(Igu[:, sl], u16[:], 1536.0,
                                                  ALU.subtract)
                return last
            step('vector', qgut, waits=(('v', now('v')),), incs=(('v', 1),))
            q_done = now('v')
            for b0 in range(0, AE, 8):
                pb, bi = (pT0, 0) if (b0 // 8) % 2 == 0 else (pT1, 1)
                def tr(t, tt=tt, b0=b0, pb=pb):
                    last = None
                    for k in range(8):
                        c0 = tt * E + (b0 + k) * 128
                        last = t.transpose(pb[:, k * 128:(k + 1) * 128],
                                           Igu[:, c0:c0 + 128], ident_bf[:])
                    return last
                step('tensor', tr,
                     waits=[('v', q_done)] + wz('IguT_free') + wait_bank(bi),
                     incs=(('t', 1),))
                tv = now('t')
                def cp(v, pb=pb, tt=tt, b0=b0):
                    dst = bass.AP(IguT, tt * 128 + b0 * T,
                                  [[AE * T, 128], [T, 8], [1, 128]])
                    src = bass.AP(pb, 0, [[1024, 128], [128, 8], [1, 128]])
                    return v.tensor_copy(dst, src)
                step('vector', cp, waits=[('t', tv)] + wz('wqo0', 'wqo1'),
                     incs=(('v', 1),))
                bank_free[bi] = ('v', now('v'))
                igut_b[(tt, b0 // 8)] = now('v')
        dep['IguT'] = ('v', now('v'))
        for c in range(8):
            l_ = c % 4
            wv = dma('sync', wlane[l_][:], s_d[:, c * 4096:(c + 1) * 4096], 'dw',
                     waits=wait_lane(l_))
            def mmd(t, c=c, l_=l_, wv=wv):
                t.wait_ge(sems[wv[0]], wv[1])
                b_ = (4 * c) // 8
                t.wait_ge(sems['v'], max(igut_b[(0, b_)], igut_b[(1, b_)]))
                last = None
                for al in range(4):
                    a = c * 4 + al
                    for tt in range(TSUB):
                        lhs = IguT[:, a * T + tt * 128:a * T + tt * 128 + 128]
                        for dc in range(2):
                            last = t.matmul(
                                bankF[SETA[2 * tt + dc]], lhs,
                                wlane[l_][:, al * 1024 + dc * 512:al * 1024 + (dc + 1) * 512],
                                start=(a == 0), stop=(a == 31))
                return last
            step('tensor', mmd,
                 waits=([w for b in SETA for w in wait_bank(b)])
                       if c == 0 else (),
                 incs=(('t', 1),))
            lane_free[l_] = ('t', now('t'))
        tv = now('t')
        dep['IguT_free'] = ('t', tv)
        dep['IxT_free'] = ('t', tv)
        if pre_epi is not None:
            pre_epi()
        def epi(v, r_=rbuf):
            last = None
            for tt in range(TSUB):
                for dc in range(2):
                    sl = slice(tt * DM + dc * 512, tt * DM + (dc + 1) * 512)
                    last = v.scalar_tensor_tensor(r_[:, sl],
                                                  bankF[SETA[2 * tt + dc]],
                                                  stat[:, so + C_CD + tt:
                                                       so + C_CD + tt + 1],
                                                  r_[:, sl],
                                                  op0=ALU.mult, op1=ALU.add)
            return last
        step('vector', epi, waits=(('t', tv),), incs=(('v', 1),))
        for b in SETA:
            bank_free[b] = ('v', now('v'))
        dep['r'] = ('v', now('v'))

    def mamba(rbuf):
        layer_norm(rbuf=rbuf)
        for tt in range(TSUB):
            for b0 in range(0, AD, 4):
                bi = 6 + (b0 // 4) % 2
                pb = bankF[bi]
                def trf(t, b0=b0, pb=pb, tt=tt):
                    last = None
                    for k in range(4):
                        c0 = tt * DM + (b0 + k) * 128
                        last = t.transpose(pb[:, k * 128:(k + 1) * 128],
                                           xln[:, c0:c0 + 128], ident_f[:])
                    return last
                step('tensor', trf, waits=wz('xln_w', 'xlT_free') + wait_bank(bi),
                     incs=(('t', 1),))
                tv = now('t')
                def hilo(v, pb=pb, b0=b0, tt=tt):
                    dsth = bass.AP(xlh, tt * 128 + b0 * T,
                                   [[AD * T, 128], [T, 4], [1, 128]])
                    dstl = bass.AP(xll, tt * 128 + b0 * T,
                                   [[AD * T, 128], [T, 4], [1, 128]])
                    src = pb.rearrange("p (b c) -> p b c", b=4)
                    v.tensor_copy(dsth, src)
                    return v.tensor_tensor(dstl, src, dsth, ALU.subtract)
                step('vector', hilo, waits=(('t', tv),), incs=(('v', 1),))
                bank_free[bi] = ('v', now('v'))
        dep['xlT'] = ('v', now('v'))
        for g in range(8):
            lh, ll = (g % 2) * 2, (g % 2) * 2 + 1
            wvh = dma('sync', wlane[lh][:], s_wih[:, g * 4096:(g + 1) * 4096], 'dw',
                      waits=wait_lane(lh))
            wvl = dma('sync', wlane[ll][:], s_wil[:, g * 4096:(g + 1) * 4096], 'dw',
                      waits=wait_lane(ll))
            bs = SETA if g % 2 == 0 else SETB
            def mmz(t, lh=lh, ll=ll, wvh=wvh, wvl=wvl, bs=bs):
                t.wait_ge(sems[wvh[0]], wvh[1])
                t.wait_ge(sems[wvl[0]], wvl[1])
                last = None
                for a in range(AD):
                    for el in range(4):
                        for p_, (wl_, xr_) in enumerate(((lh, xlh), (lh, xll),
                                                         (ll, xlh))):
                            rhs = xr_[:, a * T:(a + 1) * T]
                            lhs = wlane[wl_][:, a * 512 + el * 128:a * 512 + (el + 1) * 128]
                            last = t.matmul(bankF[bs[el]][:, 0:T], lhs, rhs,
                                            start=(a == 0 and p_ == 0),
                                            stop=(a == AD - 1 and p_ == 2))
                return last
            step('tensor', mmz,
                 waits=wz('xlT') + [w for b in bs for w in wait_bank(b)],
                 incs=(('t', 1),))
            tv = now('t')
            lane_free[lh] = ('t', tv)
            lane_free[ll] = ('t', tv)
            def evz(v, g=g, bs=bs):
                last = None
                for el in range(4):
                    e = 4 * g + el
                    if e < CT:
                        dst = xinp[:, e * XP + 3:e * XP + 3 + T]
                    else:
                        dst = z_t[:, (e - CT) * T:(e - CT + 1) * T]
                    last = v.tensor_copy(dst, bankF[bs[el]][:, 0:T])
                return last
            step('vector', evz, waits=(('t', tv),), incs=(('v', 1),))
            for b in bs:
                bank_free[b] = ('v', now('v'))
            if g < 4:
                def restg(v, g=g):
                    last = None
                    for ct in range(4 * g, 4 * g + 4):
                        last = v.tensor_copy(xinp[:, ct * XP:ct * XP + 3],
                                             halo[:, ct * 4:ct * 4 + 3])
                    return last
                def convg(v, g=g):
                    last = None
                    for ct in range(4 * g, 4 * g + 4):
                        p0 = ct * XP
                        dst = xa_t[:, ct * T:(ct + 1) * T]
                        v.tensor_scalar(dst, xinp[:, p0:p0 + T],
                                        convw[:, ct * 4:ct * 4 + 1], None,
                                        op0=ALU.mult)
                        for k in range(1, 4):
                            v.scalar_tensor_tensor(dst, xinp[:, p0 + k:p0 + k + T],
                                                   convw[:, ct * 4 + k:ct * 4 + k + 1],
                                                   dst, op0=ALU.mult, op1=ALU.add)
                        last = v.tensor_copy(halo[:, ct * 4:ct * 4 + 3],
                                             xinp[:, p0 + T:p0 + T + 3])
                    return last
                vchain([restg, convg])
                def silug(s, g=g):
                    last = None
                    for ct in range(4 * g, 4 * g + 4):
                        sl = slice(ct * T, (ct + 1) * T)
                        last = s.activation(xa_t[:, sl], xa_t[:, sl], ACT.Silu,
                                            bias=convb[:, ct:ct + 1], scale=1.0)
                    return last
                step('scalar', silug, waits=(('v', now('v')),), incs=(('s', 1),))
        dep['xa'] = ('s', now('s'))
        dep['xlT_free'] = ('t', now('t'))
        for ct in range(CT):
            def spl_s(s, ct=ct):
                o = (ct % 2) * T
                return s.activation(va_h[:, o:o + T], xa_t[:, ct * T:(ct + 1) * T],
                                    ACT.Copy, bias=0.0, scale=1.0)
            step('scalar', spl_s, waits=wz('xa', f'va{ct % 2}'), incs=(('s', 1),))
            ssv = now('s')
            def spl(v, ct=ct):
                o = (ct % 2) * T
                return v.tensor_tensor(va_l[:, o:o + T], xa_t[:, ct * T:(ct + 1) * T],
                                       va_h[:, o:o + T], ALU.subtract)
            step('vector', spl, waits=(('s', ssv),), incs=(('v', 1),))
            vv_ = now('v')
            def mmp(t, ct=ct):
                o = (ct % 2) * T
                last = None
                for p_, (wl_, xr_) in enumerate(((wx_h, va_h), (wx_h, va_l),
                                                 (wx_l, va_h))):
                    last = t.matmul(bankF[6][0:96, 0:T],
                                    wl_[:, ct * 96:(ct + 1) * 96], xr_[:, o:o + T],
                                    start=(ct == 0 and p_ == 0),
                                    stop=(ct == CT - 1 and p_ == 2))
                return last
            step('tensor', mmp,
                 waits=[('v', vv_)] + (wait_bank(6) if ct == 0 else []),
                 incs=(('t', 1),))
            dep[f'va{ct % 2}'] = ('t', now('t'))
        tv = now('t')
        step('scalar',
             lambda s: s.activation(ssm_sb[:], bankF[6][0:96, 0:T],
                                    ACT.Copy, bias=0.0, scale=1.0),
             waits=(('t', tv),), incs=(('s', 1),))
        bank_free[6] = ('s', now('s'))
        vssm = ('s', now('s'))
        def spl_dtr(v):
            v.tensor_copy(dtr_h[:], ssm_sb[0:64, :])
            return v.tensor_tensor(dtr_l[:], ssm_sb[0:64, :], dtr_h[:], ALU.subtract)
        step('vector', spl_dtr, waits=(vssm,), incs=(('v', 1),))
        vdtr = now('v')
        for ct in range(CT):
            bi = 6 + ct % 2
            pb = bankF[bi]
            def mmdt(t, ct=ct, pb=pb):
                last = None
                for p_, (wl_, xr_) in enumerate(((wdt_h, dtr_h), (wdt_h, dtr_l),
                                                 (wdt_l, dtr_h))):
                    last = t.matmul(pb[:, 0:T], wl_[:, ct * 128:(ct + 1) * 128],
                                    xr_[:], start=(p_ == 0), stop=(p_ == 2))
                return last
            step('tensor', mmdt, waits=[('v', vdtr)] + wait_bank(bi),
                 incs=(('t', 1),))
            tv = now('t')
            def sp_s(s, ct=ct, pb=pb):
                o = (ct % 2) * 512
                s.activation(sps[:, o:o + T], pb[:, 0:T], ACT.Exp,
                             bias=bdt[:, ct:ct + 1], scale=1.0)
                return s.activation(dt_t[:, ct * T:(ct + 1) * T], sps[:, o:o + T],
                                    ACT.Ln, bias=1.0, scale=1.0)
            step('scalar', sp_s, waits=[('t', tv)] + wz(f'sps{ct % 2}'),
                 incs=(('s', 1),))
            dep[f'sps{ct % 2}'] = ('s', now('s'))
            bank_free[bi] = ('s', now('s'))
        dep['dt'] = ('s', now('s'))
        step('vector', lambda v: v.tensor_tensor(xa_t[:], dt_t[:], xa_t[:], ALU.mult),
             waits=wz('dt'), incs=(('v', 1),))
        dep['u'] = ('v', now('v'))
        wch = []
        def wout_dma(c):
            lh, ll = (c % 2) * 2, (c % 2) * 2 + 1
            wvh = dma('sync', wlane[lh][:], s_woh[:, c * 4096:(c + 1) * 4096], 'dw',
                      waits=wait_lane(lh))
            wvl = dma('sync', wlane[ll][:], s_wol[:, c * 4096:(c + 1) * 4096], 'dw',
                      waits=wait_lane(ll))
            wch.append((lh, ll, wvh, wvl))
        wout_dma(0)
        wout_dma(1)
        vo_ready = {}
        def rowdma(n):
            bv1 = dma('gpsimd', brow[0:1, (n % 2) * T:(n % 2 + 1) * T],
                      ssm_sb[64 + n:65 + n, :], 'dx',
                      waits=[vssm] + wz(f'row{n % 2}'))
            bv2 = dma('gpsimd', crow[0:1, (n % 2) * T:(n % 2 + 1) * T],
                      ssm_sb[80 + n:81 + n, :], 'dx')
            return [bv1, bv2]
        rowv = [rowdma(0)]
        pend_h = [None]

        def flush_h(s_=None):
            if pend_h[0] is None:
                return
            col_, ho_, vv_ = pend_h[0]
            pend_h[0] = None
            def hcp(s, col_=col_, ho_=ho_):
                return s.activation(h_state[:, col_:col_ + 1],
                                    h_b[:, ho_ + T - 1:ho_ + T],
                                    ACT.Copy, bias=0.0, scale=1.0)
            step('scalar', hcp, waits=(('v', vv_),), incs=(('s', 1),))
        bc_done = {}
        def bc_piece(n, k, waits):
            bb, bc_ = (4, 5) if n % 2 == 0 else (0, 1)
            o = (n % 2) * T
            bank = bb if k < 4 else bc_
            row = brow if k < 4 else crow
            c0 = (k % 4) * (T // 4)
            def pc(t, bank=bank, row=row, o=o, c0=c0):
                return t.matmul(bankF[bank][:, c0:c0 + T // 4], ones_f[:],
                                row[0:1, o + c0:o + c0 + T // 4],
                                start=True, stop=True)
            step('tensor', pc, waits=waits, incs=(('t', 1),))
            bc_done[n] = now('t')
        for k in range(8):
            w = (rowv[0] + wait_bank(4) + wait_bank(5)) if k == 0 else ()
            bc_piece(0, k, w)
        for n in range(NST):
            bmo = (n % 2) * T
            tbc = bc_done[n]
            dep[f'row{n % 2}'] = ('t', tbc)
            if n + 1 < NST:
                rowv.append(rowdma(n + 1))
            bb, bc_ = (4, 5) if n % 2 == 0 else (0, 1)
            def evbc(s, bb=bb, bc_=bc_, bmo=bmo):
                s.activation(bm_sb[:, bmo:bmo + T], bankF[bb][:, 0:T],
                             ACT.Copy, bias=0.0, scale=1.0)
                return s.activation(cm_sb[:, bmo:bmo + T], bankF[bc_][:, 0:T],
                                    ACT.Copy, bias=0.0, scale=1.0)
            step('scalar', evbc, waits=[('t', tbc)] + wz(f'bm{n % 2}'),
                 incs=(('s', 1),))
            bank_free[bb] = ('s', now('s'))
            bank_free[bc_] = ('s', now('s'))
            last_n = (n == NST - 1)

            def emit_fin(ct):
                o = (ct % 2) * T
                def sz_s(s, ct=ct, o=o):
                    return s.activation(sz_b[:, o:o + T],
                                        z_t[:, ct * T:(ct + 1) * T],
                                        ACT.Silu, bias=0.0, scale=1.0)
                step('scalar', sz_s, waits=wz(f'sz{ct % 2}'), incs=(('s', 1),))
                svz = now('s')
                def fin(v, ct=ct, o=o):
                    dst = vv_t[:, ct * T:(ct + 1) * T]
                    v.scalar_tensor_tensor(dst,
                                           xinp[:, ct * XP + 3:ct * XP + 3 + T],
                                           dvec[:, ct:ct + 1], dst,
                                           op0=ALU.mult, op1=ALU.add)
                    v.tensor_tensor(dst, dst, sz_b[:, o:o + T], ALU.mult)
                    ov_ = (ct % 8) * T
                    v.tensor_copy(vo_h[:, ov_:ov_ + T], dst)
                    return v.tensor_tensor(vo_l[:, ov_:ov_ + T], dst,
                                           vo_h[:, ov_:ov_ + T], ALU.subtract)
                step('vector', fin,
                     waits=[('s', svz)] + wz(f'acc{ct}',
                                             f'voT{(ct // 4) % 2}'),
                     incs=(('v', 1),))
                dep[f'sz{ct % 2}'] = ('v', now('v'))
                vo_ready[ct] = now('v')
                if ct % 4 == 3:
                    c = ct // 4
                    lh, ll, wvh, wvl = wch[c]
                    def mmo(t, c=c, lh=lh, ll=ll, wvh=wvh, wvl=wvl,
                            vd=now('v')):
                        t.wait_ge(sems[wvh[0]], wvh[1])
                        t.wait_ge(sems[wvl[0]], wvl[1])
                        t.wait_ge(sems['v'], vd)
                        last = None
                        for cti in range(4 * c, 4 * c + 4):
                            ov_ = (cti % 8) * T
                            co = (cti % 4) * 1024
                            for tt in range(TSUB):
                                for p_, (vb, wl_) in enumerate(
                                        ((vo_h, lh), (vo_l, lh), (vo_h, ll))):
                                    lhs = vb[:, ov_ + tt * 128:ov_ + tt * 128 + 128]
                                    for dc in range(2):
                                        last = t.matmul(
                                            bankF[SETA[2 * tt + dc]], lhs,
                                            wlane[wl_][:, co + dc * 512:co + (dc + 1) * 512],
                                            start=(c == 0 and cti == 0 and p_ == 0),
                                            stop=(c == 3 and cti == 15 and p_ == 2))
                        return last
                    step('tensor', mmo,
                         waits=([w for b in SETA for w in wait_bank(b)]
                                if c == 0 else ()),
                         incs=(('t', 1),))
                    dep[f'voT{c % 2}'] = ('t', now('t'))
                    lane_free[lh] = ('t', now('t'))
                    lane_free[ll] = ('t', now('t'))
                    if c + 2 < 4:
                        wout_dma(c + 2)

            for ct in range(CT):
                o = (ct % 2) * T
                ho = (ct % 4) * T
                col = ct * NST + n
                def mk_dA(s, ct=ct, o=o, col=col):
                    return s.activation(dA_b[:, o:o + T],
                                        dt_t[:, ct * T:(ct + 1) * T],
                                        ACT.Exp, bias=0.0,
                                        scale=a_neg[:, col:col + 1])
                step('scalar', mk_dA, waits=wz(f'dA{ct % 2}'), incs=(('s', 1),))
                sv = now('s')
                flush_h()
                def scn(v, ct=ct, o=o, ho=ho, col=col, n=n, bmo=bmo):
                    v.tensor_tensor(dBx_b[:, o:o + T],
                                    xa_t[:, ct * T:(ct + 1) * T],
                                    bm_sb[:, bmo:bmo + T], ALU.mult)
                    v.tensor_tensor_scan(h_b[:, ho:ho + T], dA_b[:, o:o + T],
                                         dBx_b[:, o:o + T],
                                         h_state[:, col:col + 1],
                                         op0=ALU.mult, op1=ALU.add)
                    if n == 0:
                        return v.tensor_tensor(vv_t[:, ct * T:(ct + 1) * T],
                                               h_b[:, ho:ho + T],
                                               cm_sb[:, bmo:bmo + T], ALU.mult)
                    return v.tensor_tensor(P_b[:, (ct % 2) * T:(ct % 2) * T + T],
                                           h_b[:, ho:ho + T],
                                           cm_sb[:, bmo:bmo + T], ALU.mult)
                step('vector', scn,
                     waits=[('s', sv)] + (wz(f'P{ct % 2}') if n > 0 else []),
                     incs=(('v', 1),))
                dep[f'dA{ct % 2}'] = ('v', now('v'))
                pend_h[0] = (col, ho, now('v'))
                if n > 0:
                    def post(g_, ct=ct):
                        return g_.tensor_tensor(vv_t[:, ct * T:(ct + 1) * T],
                                                vv_t[:, ct * T:(ct + 1) * T],
                                                P_b[:, (ct % 2) * T:(ct % 2) * T + T],
                                                ALU.add)
                    step('gpsimd', post, waits=(('v', now('v')),), incs=(('g', 1),))
                    dep[f'P{ct % 2}'] = ('g', now('g'))
                    dep[f'acc{ct}'] = ('g', now('g'))
                else:
                    dep[f'acc{ct}'] = ('v', now('v'))
                if ct % 2 == 0 and n + 1 < NST:
                    k = ct // 2
                    nb, ncb = (4, 5) if (n + 1) % 2 == 0 else (0, 1)
                    w = [('v', now('v'))]
                    if k == 0:
                        w += rowv[n + 1] + wait_bank(nb) + wait_bank(ncb) + \
                             wz(f'bm{(n + 1) % 2}')
                    bc_piece(n + 1, k, w)
                if last_n and ct >= 2:
                    emit_fin(ct - 2)
            dep[f'bm{n % 2}'] = ('g', now('g'))
            if last_n:
                emit_fin(CT - 2)
                emit_fin(CT - 1)
        flush_h()
        dep['hstate'] = ('g', now('g'))
        def epo(v, r_=None):
            r_ = rbuf if r_ is None else r_
            last = None
            for tt in range(TSUB):
                for dc in range(2):
                    sl = slice(tt * DM + dc * 512, tt * DM + (dc + 1) * 512)
                    last = v.tensor_tensor(r_[:, sl], bankF[SETA[2 * tt + dc]],
                                           r_[:, sl], ALU.add)
            return last
        step('vector', epo, waits=(('t', now('t')),), incs=(('v', 1),))
        for b in SETA:
            bank_free[b] = ('v', now('v'))
        dep['r'] = ('v', now('v'))
        dep['xln'] = ('v', now('v'))

    bufs = (r_t, r_t2)
    prev_ov = [None]

    def prefetch(tile):
        if tile + 1 >= ntiles:
            return None, None
        nb = bufs[(tile + 1) % 2]
        w = [prev_ov[0]] if prev_ov[0] else []
        return nb, dma('gpsimd', nb[:], x_ap(d_x, tile + 1), 'dx', waits=w)

    ix_next = ffn_head(0, rwaits=[dep['rbuf0']], rbuf=bufs[0])
    for tile in range(ntiles):
        rb = bufs[tile % 2]
        ffn_body(ix_next, 0, rb)
        ix_next = None
        if stages >= 2:
            mamba(rb)
        nb, ldv = prefetch(tile)
        if stages >= 3:
            ix2 = ffn_head(24, rbuf=rb)
            if nb is not None:
                hold = {}
                def hook(nb=nb, ldv=ldv, hold=hold):
                    hold['ix'] = ffn_head(0, rwaits=[ldv], rbuf=nb)
                ffn_body(ix2, 24, rb, pre_epi=hook)
                ix_next = hold['ix']
            else:
                ffn_body(ix2, 24, rb)
        prev_ov[0] = dma('gpsimd', x_ap(d_out, tile), rb[:], 'dx', waits=wz('r'))
        if stages < 3 and nb is not None:
            ix_next = ffn_head(0, rwaits=[ldv], rbuf=nb)
    if debug:
        dbg_specs = {"sc_bc": sc_bc, "stat": stat, "xln": xln, "Ix": Ix,
                     "IxT": IxT, "gu": gu, "Igu": Igu, "r_t": r_t,
                     "sps": sps, "u16": u16, "wl0": wlane[0], "IguT": IguT,
                     "xinp": xinp, "z_t": z_t, "xa_t": xa_t, "dt_t": dt_t,
                     "vv_t": vv_t, "ssm_sb": ssm_sb, "h_state": h_state,
                     "a_neg": a_neg, "brow": brow, "crow": crow}
        wall = [(s_, v_) for s_, v_ in cnt.items()]
        first_dbg = True
        for nm, tns in dbg_specs.items():
            shp = list(tns.shape) if hasattr(tns, 'shape') else None
            dt_ = tns.dtype
            dd = nc.dram_tensor("dbg_" + nm, shp, dt_, kind="ExternalOutput").ap()
            dma('gpsimd', dd, tns[:], 'dx', waits=tuple(wall) if first_dbg else ())
            first_dbg = False
    step('gpsimd', lambda g: g.memset(srow[0:1, 6:8], 0.0),
         waits=tuple(qdone('dx')) + tuple(qdone('dw')))

    with ExitStack() as st:
        for n_ in sorted(cnt.keys()):
            sems[n_] = st.enter_context(nc.semaphore(n_))
        block = st.enter_context(nc.Block())

        def section(engname):
            def body(eng):
                for e, waits, fn, incs in ops:
                    if e != engname:
                        continue
                    for sname, val in waits:
                        eng.wait_ge(sems[sname], val)
                    inst = fn(eng)
                    for sname, by in incs:
                        inst.then_inc(sems[sname], by)
            return body

        block.sync(section('sync'))
        block.gpsimd(section('gpsimd'))
        block.vector(section('vector'))
        block.scalar(section('scalar'))
        block.tensor(section('tensor'))
    return nc


_NC_CACHE = {}


def kernel(L0_attn_x, L0_attn_W_in, L0_attn_conv_w, L0_attn_conv_b, L0_attn_W_x,
           L0_attn_W_dt, L0_attn_b_dt, L0_attn_A_log, L0_attn_D, L0_attn_W_out,
           L0_ffn_w_gate, L0_ffn_w_up, L0_ffn_w_down):
    f32 = np.float32

    def pm(w, atiles):
        r, c = w.shape
        assert r == atiles * 128, (r, atiles)
        return np.ascontiguousarray(
            np.asarray(w, f32).reshape(atiles, 128, c).transpose(1, 0, 2)
            .reshape(128, atiles * c))

    wgT = pm(np.asarray(L0_ffn_w_gate, f32).T, AD)
    wuT = pm(np.asarray(L0_ffn_w_up, f32).T, AD)
    wdT = pm(np.asarray(L0_ffn_w_down, f32).T, AE)
    win = pm(L0_attn_W_in, AD)
    wout = pm(L0_attn_W_out, CT)
    wx = pm(L0_attn_W_x, CT)
    alog = pm(L0_attn_A_log, CT)
    convw = pm(L0_attn_conv_w, CT)
    convb = pm(np.asarray(L0_attn_conv_b, f32).reshape(DI, 1), CT)
    dd = pm(np.asarray(L0_attn_D, f32).reshape(DI, 1), CT)
    bdt = pm(np.asarray(L0_attn_b_dt, f32).reshape(DI, 1), CT)
    wdt = np.ascontiguousarray(np.asarray(L0_attn_W_dt, f32))

    key = (_STAGES, _NTILES)
    if key not in _NC_CACHE:
        _NC_CACHE[key] = build_program(_STAGES, _NTILES)
    nc = _NC_CACHE[key]

    x = np.asarray(L0_attn_x, f32)
    in_maps = []
    for b in range(8):
        in_maps.append({
            "x": np.ascontiguousarray(x[b]),
            "wgT": wgT, "wuT": wuT, "wdT": wdT,
            "win": win, "wout": wout, "wx": wx, "wdt": wdt,
            "alog": alog, "convw": convw, "convb": convb,
            "dvec": dd, "bdt": bdt,
        })
    res = run_bass_kernel_spmd(nc, in_maps, core_ids=list(range(8)))
    return np.stack([np.asarray(r["out"], f32) for r in res.results], axis=0)


# version 34
# speedup vs baseline: 1.0346x; 1.0000x over previous
import numpy as np
import ml_dtypes
from contextlib import ExitStack
import concourse.bass as bass
import concourse.mybir as mybir
from concourse.bass_utils import run_bass_kernel_spmd

F32 = mybir.dt.float32
BF16 = mybir.dt.bfloat16
F16 = mybir.dt.float16
ALU = mybir.AluOpType
ACT = mybir.ActivationFunctionType
AX = mybir.AxisListType

S, DM, E, DI, NST = 4096, 1024, 4096, 2048, 16
T = 256
NT = S // T
TSUB = T // 128
CT = DI // 128
AD = DM // 128
AE = E // 128
XP = T + 4
_STAGES = 3
_NTILES = NT
_DEBUG = False

C_SUM, C_SQ, C_MEAN, C_MSQ, C_VAR, C_VE, C_S0, C_T2 = 0, 2, 4, 6, 8, 10, 12, 14
C_MX, C_SX, C_ISX, C_CG, C_CU, C_CD, C_T3, C_ACC = 16, 18, 20, 22, 24, 26, 28, 32


def build_program(stages=3, ntiles=None, debug=None):
    ntiles = ntiles or _NTILES
    debug = _DEBUG if debug is None else debug
    nc = bass.Bass("TRN2", target_bir_lowering=False, detect_race_conditions=False)
    ops = []
    cnt = {}
    sems = {}

    def step(eng, fn, waits=(), incs=()):
        ops.append((eng, tuple(waits), fn, tuple(incs)))
        for s_, by in incs:
            cnt[s_] = cnt.get(s_, 0) + by

    def now(s_):
        return cnt.get(s_, 0)

    d_x = nc.dram_tensor("x", [S, DM], F32, kind="ExternalInput").ap()
    d_out = nc.dram_tensor("out", [S, DM], F32, kind="ExternalOutput").ap()
    d_wg = nc.dram_tensor("wgT", [128, AD * E], F32, kind="ExternalInput").ap()
    d_wu = nc.dram_tensor("wuT", [128, AD * E], F32, kind="ExternalInput").ap()
    d_wd = nc.dram_tensor("wdT", [128, AE * DM], F32, kind="ExternalInput").ap()
    d_win = nc.dram_tensor("win", [128, AD * E], F32, kind="ExternalInput").ap()
    d_wout = nc.dram_tensor("wout", [128, CT * DM], F32, kind="ExternalInput").ap()
    d_wx = nc.dram_tensor("wx", [128, CT * 96], F32, kind="ExternalInput").ap()
    d_wdt = nc.dram_tensor("wdt", [64, DI], F32, kind="ExternalInput").ap()
    d_alog = nc.dram_tensor("alog", [128, CT * NST], F32, kind="ExternalInput").ap()
    d_convw = nc.dram_tensor("convw", [128, CT * 4], F32, kind="ExternalInput").ap()
    d_convb = nc.dram_tensor("convb", [128, CT], F32, kind="ExternalInput").ap()
    d_dd = nc.dram_tensor("dvec", [128, CT], F32, kind="ExternalInput").ap()
    d_bdt = nc.dram_tensor("bdt", [128, CT], F32, kind="ExternalInput").ap()
    s_gu = nc.dram_tensor("s_gu", [128, AD * 2 * E], BF16, kind="Internal").ap()
    s_d = nc.dram_tensor("s_d", [128, AE * DM], BF16, kind="Internal").ap()
    s_wih = nc.dram_tensor("s_wih", [128, AD * E], BF16, kind="Internal").ap()
    s_wil = nc.dram_tensor("s_wil", [128, AD * E], BF16, kind="Internal").ap()
    s_woh = nc.dram_tensor("s_woh", [128, CT * DM], BF16, kind="Internal").ap()
    s_wol = nc.dram_tensor("s_wol", [128, CT * DM], BF16, kind="Internal").ap()

    A = nc.alloc_sbuf_tensor
    ident_bf = A("ident_bf", [128, 128], BF16)
    ident_f = A("ident_f", [128, 128], F32)
    ones_f = A("ones_f", [1, 128], F32)
    iotp = A("iotp", [128, 1], F32)
    iotf = A("iotf", [128, 128], F32)
    a_neg = A("a_neg", [128, CT * NST], F32)
    convw = A("convw_s", [128, CT * 4], F32)
    convb = A("convb_s", [128, CT], F32)
    dvec = A("dvec_s", [128, CT], F32)
    bdt = A("bdt_s", [128, CT], F32)
    wx_h = A("wx_h", [128, CT * 96], BF16)
    wx_l = A("wx_l", [128, CT * 96], BF16)
    wdt_h = A("wdt_h", [64, DI], BF16)
    wdt_l = A("wdt_l", [64, DI], BF16)
    h_state = A("h_state", [128, CT * NST], F32)
    stat = A("stat", [128, 64], F32)
    sc_bc = A("sc_bc", [128, 8], F32)
    srow = A("srow", [1, 8], F32)
    brow = A("brow", [1, 2 * T], F32)
    crow = A("crow", [1, 2 * T], F32)
    wlane = [A(f"wlane{i}", [128, 4096], BF16) for i in range(4)]
    r_t = A("r_t", [128, TSUB * DM], F32)
    xln = A("xln", [128, TSUB * DM], F32)

    Ix = A("Ix", [128, TSUB * DM], BF16)
    IxT = A("IxT", [128, AD * T], BF16)
    xlh = A("xlh", [128, AD * T], BF16)
    xll = A("xll", [128, AD * T], BF16)
    dA_b = A("dA_b", [128, 2 * T], F32)
    dBx_b = A("dBx_b", [128, 2 * T], F32)
    h_b = A("h_b", [128, 4 * T], F32)
    u16 = nc.alloc_sbuf_tensor_at("u16", [128, 2048], F16,
                                  offset=nc.lookup_mloc(h_b).addr)
    P_b = A("P_b", [128, 2 * T], F32)
    bm_sb = A("bm_sb", [128, 2 * T], F32)
    cm_sb = A("cm_sb", [128, 2 * T], F32)
    r_t2 = A("r_t2", [128, TSUB * DM], F32)
    sz_b = A("sz_b", [128, 2 * T], F32)
    ssm_sb = A("ssm_sb", [96, T], F32)
    dtr_h = A("dtr_h", [64, T], BF16)
    dtr_l = A("dtr_l", [64, T], BF16)
    va_h = A("va_h", [128, 2 * T], BF16)
    va_l = A("va_l", [128, 2 * T], BF16)
    vo_h = A("vo_h", [128, 8 * T], BF16)
    vo_l = A("vo_l", [128, 8 * T], BF16)
    sps = A("sps", [128, 1024], F32)
    sqd = nc.alloc_sbuf_tensor_at("sqd", [128, DM], F32,
                                  offset=nc.lookup_mloc(sps).addr)
    halo = A("halo", [128, CT * 4], F32)
    AR = 82176
    arena = A("arena", [128, AR], mybir.dt.uint8)
    base = nc.lookup_mloc(arena).addr

    def at(name, shape, dtype, off):
        return nc.alloc_sbuf_tensor_at(name, shape, dtype, offset=base + off)

    xinp = at("xinp", [128, CT * XP], F32, 0)
    z_t = at("z_t", [128, CT * T], F32, 16640)
    xa_t = at("xa_t", [128, CT * T], F32, 33024)
    dt_t = at("dt_t", [128, CT * T], F32, 49408)
    vv_t = at("vv_t", [128, CT * T], F32, 65792)
    gu = at("gu", [128, TSUB * E], F32, 0)
    Igu = at("Igu", [128, TSUB * E], BF16, 32768)
    IguT = at("IguT", [128, AE * T], BF16, 49152)
    wq_in = at("wq_in", [128, 4096], F32, 0)
    wq_t1 = at("wq_t1", [128, 4096], F32, 16384)
    wq_t2 = at("wq_t2", [128, 4096], F32, 32768)
    wq_o1 = at("wq_o1", [128, 4096], BF16, 49152)
    wq_o2 = at("wq_o2", [128, 4096], BF16, 57344)
    wq_t3 = at("wq_t3", [128, 4096], F32, 65792)

    rem = nc.sbuf_bytes_remaining
    assert rem >= 0, f"sbuf over budget: {rem}"

    pT0 = nc.alloc_psum_tensor("pT0", [128, 1024], BF16)
    pT1 = nc.alloc_psum_tensor("pT1", [128, 1024], BF16)
    pF = [nc.alloc_psum_tensor(f"pF{i}", [128, 512], F32) for i in range(6)]
    bankF = [pT0[:, 0:1024].bitcast(F32), pT1[:, 0:1024].bitcast(F32)] + \
            [pF[i][:, 0:512] for i in range(6)]
    SETA, SETB = (2, 3, 4, 5), (6, 7, 0, 1)

    bank_free = {i: None for i in range(8)}
    lane_free = {i: None for i in range(4)}
    dep = {}

    def wait_bank(b):
        return [bank_free[b]] if bank_free[b] else []

    def wait_lane(i):
        return [lane_free[i]] if lane_free[i] else []

    def wz(*names):
        seen = {}
        for n_ in names:
            if n_ in dep:
                s_, v_ = dep[n_]
                seen[s_] = max(seen.get(s_, 0), v_)
        return [(s_, v_) for s_, v_ in seen.items()]

    QKS = {'dw': 12, 'dx': 6}
    qstate = {q: [0, [0] * k] for q, k in QKS.items()}

    def dma(qeng, out_ap, in_ap, q, waits=()):
        st_ = qstate[q]
        k = st_[0] % QKS[q]
        st_[0] += 1
        sem = f'{q}{k}'
        w = list(waits)
        if st_[1][k]:
            w.append((sem, st_[1][k]))
        step(qeng, lambda e, o=out_ap, i=in_ap: e.dma_start(o, i),
             waits=w, incs=((sem, 16),))
        st_[1][k] = now(sem)
        return (sem, st_[1][k])

    def qdone(q):
        return [(f'{q}{k}', v) for k, v in enumerate(qstate[q][1]) if v]

    SC = lambda c: stat[:, c:c + 1]
    SC2 = lambda c: stat[:, c:c + 2]

    def vchain(fns, waits=()):
        for i_, fn_ in enumerate(fns):
            w = list(waits) if i_ == 0 else []
            w.append(('v', now('v')))
            step('vector', fn_, waits=w, incs=(('v', 1),))

    step('gpsimd', lambda g: g.iota(iotp[:], [[0, 1]], channel_multiplier=1,
                                    allow_small_or_imprecise_dtypes=True))
    step('gpsimd', lambda g: g.iota(iotf[:], [[1, 128]], channel_multiplier=0,
                                    allow_small_or_imprecise_dtypes=True),
         incs=(('g', 1),))
    for dst, src in ((a_neg[:], d_alog), (convw[:], d_convw), (convb[:], d_convb),
                     (dvec[:], d_dd), (bdt[:], d_bdt),
                     (wq_t1[:, 0:CT * 96], d_wx), (wq_t2[0:64, 0:DI], d_wdt)):
        dma('gpsimd', dst, src, 'dx')
    v_small = qdone('dx')
    def x_ap(d, tile):
        return bass.AP(d.tensor, tile * T * DM,
                       [[DM, 128], [128 * DM, TSUB], [1, DM]])
    dep['rbuf0'] = dma('gpsimd', r_t[:], x_ap(d_x, 0), 'dx')

    def vsetup(v):
        v.tensor_scalar(ident_bf[:], iotf[:], iotp[:, 0:1], None, op0=ALU.is_equal)
        v.tensor_scalar(ident_f[:], iotf[:], iotp[:, 0:1], None, op0=ALU.is_equal)
        v.memset(ones_f[:], 1.0)
        v.memset(h_state[:], 0.0)
        v.memset(stat[:, C_ACC + 3:C_ACC + 4], 0.0)
        v.memset(halo[:], 0.0)
        v.tensor_copy(wx_h[:], wq_t1[:, 0:CT * 96])
        v.tensor_tensor(wx_l[:], wq_t1[:, 0:CT * 96], wx_h[:], ALU.subtract)
        v.tensor_copy(wdt_h[:], wq_t2[0:64, 0:DI])
        return v.tensor_tensor(wdt_l[:], wq_t2[0:64, 0:DI], wdt_h[:], ALU.subtract)
    step('vector', vsetup, waits=[('g', 1)] + v_small, incs=(('v', 1),))
    dep['wq0'] = ('v', now('v'))
    dep['wq1'] = ('v', now('v'))
    ldbuf = (wq_in, wq_t1, wq_t3)
    ldi = [0]

    def ld_next():
        i_ = ldi[0] % 3
        ldi[0] += 1
        return ldbuf[i_], f'wq{i_}'
    step('scalar', lambda s: s.activation(a_neg[:], a_neg[:], ACT.Exp,
                                          bias=0.0, scale=1.0),
         waits=v_small, incs=(('s', 1),))
    step('vector', lambda v: v.tensor_single_scalar(a_neg[:], a_neg[:], -1.0, ALU.mult),
         waits=(('s', now('s')),), incs=(('v', 1),))

    raws = [d_wg, d_wu, d_wd]
    for j in range(3):
        for c in range(8):
            buf, gn = ld_next()
            wv = dma('sync', buf[:], raws[j][:, c * 4096:(c + 1) * 4096], 'dw',
                     waits=wz(gn))
            def red1(v, b=buf):
                return v.reduce_sum(SC(C_T3), b[:], axis=AX.X,
                                    apply_absolute_value=True)
            def red2(v, j=j, c=c):
                if c == 0:
                    return v.tensor_copy(SC(C_ACC + j), SC(C_T3))
                return v.tensor_tensor(SC(C_ACC + j), SC(C_ACC + j), SC(C_T3), ALU.add)
            vchain([red1, red2], waits=(wv,))
            dep[gn] = ('v', now('v'))
    step('tensor', lambda t: t.transpose(pF[4][0:4, 0:128], stat[:, C_ACC:C_ACC + 4],
                                         ident_f[:]),
         waits=wz('wq0', 'wq1', 'wq2'), incs=(('t', 1),))
    vchain([
        lambda v: v.reduce_sum(stat[0:4, 36:37], pF[4][0:4, 0:128], axis=AX.X),
        lambda v: v.tensor_scalar(stat[0:4, 37:38], stat[0:4, 36:37],
                                  1.0 / (DM * E), None, op0=ALU.mult),
        lambda v: v.tensor_scalar(stat[0:4, 38:39], stat[0:4, 37:38], 1e-5, None,
                                  op0=ALU.max),
        lambda v: v.reciprocal(stat[0:4, 39:40], stat[0:4, 38:39]),
        lambda v: v.reciprocal(stat[0:4, 40:41], stat[0:4, 39:40]),
    ], waits=(('t', now('t')),))
    bank_free[6] = ('v', now('v'))
    rv1 = dma('gpsimd', srow[0:1, 0:3], stat[0:3, 39:40], 'dx', waits=(('v', now('v')),))
    rv2 = dma('gpsimd', srow[0:1, 3:6], stat[0:3, 40:41], 'dx')
    step('tensor', lambda t: t.matmul(pF[5][:, 0:6], ones_f[:], srow[0:1, 0:6],
                                      start=True, stop=True),
         waits=[rv1, rv2] + wait_bank(7), incs=(('t', 1),))
    step('vector', lambda v: v.tensor_copy(sc_bc[:, 0:6], pF[5][:, 0:6]),
         waits=(('t', now('t')),), incs=(('v', 1),))
    bank_free[7] = ('v', now('v'))
    dep['sc_bc'] = ('v', now('v'))

    for j in range(3):
        for c in range(8):
            pin, gn = ld_next()
            out = wq_o1 if c % 2 == 0 else wq_o2
            for h in range(2):
                sl = slice(h * 2048, (h + 1) * 2048)
                wv = dma('sync', pin[:, sl],
                         raws[j][:, c * 4096 + h * 2048:
                                 c * 4096 + (h + 1) * 2048], 'dw',
                         waits=wz(gn))
                def quant(v, pin=pin, out=out, j=j, sl=sl):
                    v.tensor_scalar(wq_t2[:, sl], pin[:, sl], sc_bc[:, j:j + 1],
                                    None, op0=ALU.mult)
                    v.tensor_single_scalar(out[:, sl], wq_t2[:, sl], -0.5,
                                           ALU.is_le)
                    return v.scalar_tensor_tensor(out[:, sl], wq_t2[:, sl],
                                                  0.5, out[:, sl],
                                                  op0=ALU.is_ge,
                                                  op1=ALU.subtract)
                step('vector', quant,
                     waits=[wv] + (wz('sc_bc', f'wqo{c % 2}')
                                   if h == 0 else []), incs=(('v', 1),))
            dep[gn] = ('v', now('v'))
            vq = now('v')
            if j < 2:
                dst = bass.AP(s_gu.tensor, c * 8192 + j * 512,
                              [[AD * 2 * E, 128], [1024, 8], [1, 512]])
            else:
                dst = s_d[:, c * 4096:(c + 1) * 4096]
            dep[f'wqo{c % 2}'] = dma('sync', dst, out[:], 'dw', waits=(('v', vq),))
    for (raw, nch, dh, dl, grouped) in ((d_win, 8, s_wih, s_wil, True),
                                        (d_wout, 4, s_woh, s_wol, False)):
        for c in range(nch):
            pin, gn = ld_next()
            for h in range(2):
                sl = slice(h * 2048, (h + 1) * 2048)
                wv = dma('sync', pin[:, sl],
                         raw[:, c * 4096 + h * 2048:
                             c * 4096 + (h + 1) * 2048], 'dw',
                         waits=wz(gn))
                def split(v, pin=pin, sl=sl):
                    v.tensor_copy(wq_o1[:, sl], pin[:, sl])
                    return v.tensor_tensor(wq_o2[:, sl], pin[:, sl],
                                           wq_o1[:, sl], ALU.subtract)
                step('vector', split,
                     waits=[wv] + (wz('wqo0', 'wqo1') if h == 0 else []),
                     incs=(('v', 1),))
            dep[gn] = ('v', now('v'))
            vq = now('v')
            if grouped:
                dsth = bass.AP(s_wih.tensor, c * 512,
                               [[AD * E, 128], [4096, 8], [1, 512]])
                dstl = bass.AP(s_wil.tensor, c * 512,
                               [[AD * E, 128], [4096, 8], [1, 512]])
            else:
                dsth = s_woh[:, c * 4096:(c + 1) * 4096]
                dstl = s_wol[:, c * 4096:(c + 1) * 4096]
            dma('sync', dsth, wq_o1[:], 'dw', waits=(('v', vq),))
            ov = dma('sync', dstl, wq_o2[:], 'dw')
            dep['wqo0'] = ov
            dep['wqo1'] = ov
    for i_, pv in enumerate(qdone('dw')):
        dep[f'prol{i_}'] = pv

    cur_r = [r_t]

    def layer_norm(rwaits=None, rbuf=None):
        r_ = rbuf
        def ln_s(s):
            last = None
            for tt in range(TSUB):
                s.activation(xln[:, tt * DM:(tt + 1) * DM],
                             r_[:, tt * DM:(tt + 1) * DM],
                             ACT.Identity, bias=0.0, scale=1.0,
                             accum_out=stat[:, C_SUM + tt:C_SUM + tt + 1])
                last = s.activation(sqd[:], r_[:, tt * DM:(tt + 1) * DM],
                                    ACT.Square, bias=0.0, scale=1.0,
                                    accum_out=stat[:, C_SQ + tt:C_SQ + tt + 1])
            return last
        step('scalar', ln_s,
             waits=(wz('r') if rwaits is None else list(rwaits)) +
                   wz('xln', 'sps0', 'sps1'), incs=(('s', 1),))
        sv = now('s')
        vchain([
            lambda v: (v.tensor_scalar(SC2(C_MEAN), SC2(C_SUM), 1.0 / DM, None,
                                       op0=ALU.mult),
                       v.tensor_scalar(SC2(C_MSQ), SC2(C_SQ), 1.0 / DM, None,
                                       op0=ALU.mult))[-1],
            lambda v: v.tensor_tensor(SC2(C_VAR), SC2(C_MEAN), SC2(C_MEAN), ALU.mult),
            lambda v: v.tensor_tensor(SC2(C_VAR), SC2(C_MSQ), SC2(C_VAR), ALU.subtract),
            lambda v: v.tensor_scalar(SC2(C_VE), SC2(C_VAR), 1e-5, None, op0=ALU.add),
        ], waits=(('s', sv),))
        def sqrt_ln(s):
            s.activation(SC2(C_T2), SC2(C_VE), ACT.Ln, bias=0.0, scale=1.0)
            return s.activation(SC2(C_S0), SC2(C_T2), ACT.Exp, bias=0.0, scale=0.5)
        step('scalar', sqrt_ln, waits=(('v', now('v')),), incs=(('s', 1),))
        sv = now('s')
        def ln_apply(v):
            last = None
            for tt in range(TSUB):
                sl = slice(tt * DM, (tt + 1) * DM)
                last = v.tensor_scalar(xln[:, sl], xln[:, sl],
                                       stat[:, C_MEAN + tt:C_MEAN + tt + 1],
                                       stat[:, C_T2 + tt:C_T2 + tt + 1],
                                       op0=ALU.subtract, op1=ALU.mult)
            return last
        vchain([lambda v: v.reciprocal(SC2(C_T2), SC2(C_S0)), ln_apply],
               waits=(('s', sv),))
        dep['xln_w'] = ('v', now('v'))

    def ffn_head(so, rwaits=None, rbuf=None):
        layer_norm(rwaits=rwaits, rbuf=rbuf)
        def qtail(v):
            last = None
            for tt in range(TSUB):
                sl = slice(tt * DM, (tt + 1) * DM)
                v.tensor_scalar(u16[:, 0:DM], xln[:, sl],
                                stat[:, so + C_SX + tt:so + C_SX + tt + 1],
                                1536.0, op0=ALU.mult, op1=ALU.add)
                last = v.tensor_single_scalar(Ix[:, sl], u16[:, 0:DM], 1536.0,
                                              ALU.subtract)
            return last
        vchain([
            lambda v: (v.reduce_max(SC(so + C_MX), xln[:, 0:DM], axis=AX.X,
                                    apply_absolute_value=True),
                       v.reduce_max(SC(so + C_MX + 1), xln[:, DM:2 * DM],
                                    axis=AX.X,
                                    apply_absolute_value=True))[-1],
            lambda v: v.tensor_scalar(SC2(so + C_MX), SC2(so + C_MX), 1e-5,
                                      None, op0=ALU.max),
            lambda v: v.reciprocal(SC2(so + C_SX), SC2(so + C_MX)),
            lambda v: v.tensor_scalar(SC2(so + C_SX), SC2(so + C_SX), 127.0,
                                      None, op0=ALU.mult),
            lambda v: v.reciprocal(SC2(so + C_ISX), SC2(so + C_SX)),
            lambda v: (v.tensor_scalar(SC2(so + C_CG), SC2(so + C_ISX),
                                       sc_bc[:, 3:4], None, op0=ALU.mult),
                       v.tensor_scalar(SC2(so + C_CU), SC2(so + C_ISX),
                                       sc_bc[:, 4:5], None, op0=ALU.mult))[-1],
            qtail,
        ], waits=wz('sc_bc', 'Ix_free'))
        dep['Ix_w'] = ('v', now('v'))
        dep['xln'] = ('v', now('v'))
        ixt_a = []
        for a in range(AD):
            pb, bi = (pT0, 0) if a % 2 == 0 else (pT1, 1)
            def tr(t, a=a, pb=pb):
                t.transpose(pb[:, 0:128],
                            Ix[:, a * 128:a * 128 + 128], ident_bf[:])
                return t.transpose(pb[:, 128:256],
                                   Ix[:, DM + a * 128:DM + a * 128 + 128],
                                   ident_bf[:])
            step('tensor', tr,
                 waits=(wz('Ix_w', 'IxT_free') if a < 2 else []) + wait_bank(bi),
                 incs=(('t', 1),))
            tv = now('t')
            def cp(v, pb=pb, a=a):
                dst = bass.AP(IxT, a * T, [[AD * T, 128], [128, 2], [1, 128]])
                src = bass.AP(pb, 0, [[1024, 128], [128, 2], [1, 128]])
                return v.tensor_copy(dst, src)
            step('vector', cp, waits=(('t', tv),), incs=(('v', 1),))
            bank_free[bi] = ('v', now('v'))
            ixt_a.append(now('v'))
        dep['IxT'] = ('v', now('v'))
        dep['Ix_free'] = ('t', now('t'))
        return ixt_a

    def ffn_body(ixt_a, so, rbuf, pre_epi=None):
        for seg in range(8):
            lanes = [(seg % 2) * 2, (seg % 2) * 2 + 1]
            ch_done = []
            for a in range(AD):
                ln_ = lanes[a // 4]
                off = (a % 4) * 1024
                wv = dma('sync', wlane[ln_][:, off:off + 1024],
                         s_gu[:, a * 8192 + seg * 1024:a * 8192 + (seg + 1) * 1024],
                         'dw', waits=wait_lane(ln_) if a % 4 == 0 else ())
                ch_done.append(wv)
            bs = SETA if seg % 2 == 0 else SETB
            def mm(t, lanes=lanes, ch_done=ch_done, bs=bs):
                last = None
                for a in range(AD):
                    t.wait_ge(sems[ch_done[a][0]], ch_done[a][1])
                    t.wait_ge(sems['v'], ixt_a[a])
                    ln_ = wlane[lanes[a // 4]]
                    off = (a % 4) * 1024
                    for tt in range(TSUB):
                        lhs = IxT[:, a * T + tt * 128:a * T + tt * 128 + 128]
                        last = t.matmul(bankF[bs[2 * tt]], lhs,
                                        ln_[:, off:off + 512],
                                        start=(a == 0), stop=(a == AD - 1))
                        last = t.matmul(bankF[bs[2 * tt + 1]], lhs,
                                        ln_[:, off + 512:off + 1024],
                                        start=(a == 0), stop=(a == AD - 1))
                return last
            step('tensor', mm,
                 waits=[w for b in bs for w in wait_bank(b)],
                 incs=(('t', 1),))
            tv = now('t')
            for l_ in lanes:
                lane_free[l_] = ('t', tv)
            for tt in range(TSUB):
                spo = tt * 512
                def ev_s(s, tt=tt, spo=spo, bs=bs):
                    return s.activation(sps[:, spo:spo + 512], bankF[bs[2 * tt]],
                                        ACT.Silu, bias=0.0,
                                        scale=stat[:, so + C_CG + tt:
                                                   so + C_CG + tt + 1])
                step('scalar', ev_s, waits=[('t', tv)] + wz(f'sps{tt}'),
                     incs=(('s', 1),))
                sv = now('s')
                bank_free[bs[2 * tt]] = ('s', sv)
                def ev_v(v, tt=tt, seg=seg, spo=spo, bs=bs):
                    dst = gu[:, tt * E + seg * 512:tt * E + (seg + 1) * 512]
                    return v.scalar_tensor_tensor(dst, bankF[bs[2 * tt + 1]],
                                                  stat[:, so + C_CU + tt:
                                                       so + C_CU + tt + 1],
                                                  sps[:, spo:spo + 512],
                                                  op0=ALU.mult, op1=ALU.mult)
                def ev_m1(v, tt=tt, seg=seg):
                    sl = slice(tt * E + seg * 512, tt * E + (seg + 1) * 512)
                    return v.reduce_max(SC(C_T3), gu[:, sl], axis=AX.X,
                                        apply_absolute_value=True)
                def ev_m2(v, tt=tt, seg=seg):
                    mx = slice(so + C_MX + tt, so + C_MX + tt + 1)
                    if seg == 0:
                        return v.tensor_copy(stat[:, mx], SC(C_T3))
                    return v.tensor_tensor(stat[:, mx], stat[:, mx],
                                           SC(C_T3), ALU.max)
                step('vector', ev_v,
                     waits=[('s', sv)] +
                           (wz('wq0', 'wq1') if seg == 0 else []),
                     incs=(('v', 1),))
                vchain([ev_m1, ev_m2])
                dep[f'sps{tt}'] = ('v', now('v'))
                bank_free[bs[2 * tt + 1]] = ('v', now('v'))
        vchain([
            lambda v: v.tensor_scalar(SC2(so + C_MX), SC2(so + C_MX), 1e-5,
                                      None, op0=ALU.max),
            lambda v: v.reciprocal(SC2(so + C_SX), SC2(so + C_MX)),
            lambda v: v.tensor_scalar(SC2(so + C_SX), SC2(so + C_SX), 127.0,
                                      None, op0=ALU.mult),
            lambda v: v.reciprocal(SC2(so + C_ISX), SC2(so + C_SX)),
            lambda v: v.tensor_scalar(SC2(so + C_CD), SC2(so + C_ISX),
                                      sc_bc[:, 5:6], None, op0=ALU.mult),
            lambda v: v.tensor_scalar(SC2(so + C_CD), SC2(so + C_CD), 0.5,
                                      None, op0=ALU.mult),
        ])
        igut_b = {}
        for tt in range(TSUB):
            def qgut(v, tt=tt):
                last = None
                for hf in range(2):
                    sl = slice(tt * E + hf * 2048, tt * E + (hf + 1) * 2048)
                    v.tensor_scalar(u16[:], gu[:, sl],
                                    stat[:, so + C_SX + tt:so + C_SX + tt + 1],
                                    1536.0, op0=ALU.mult, op1=ALU.add)
                    last = v.tensor_single_scalar(Igu[:, sl], u16[:], 1536.0,
                                                  ALU.subtract)
                return last
            step('vector', qgut, waits=(('v', now('v')),), incs=(('v', 1),))
            q_done = now('v')
            for b0 in range(0, AE, 8):
                pb, bi = (pT0, 0) if (b0 // 8) % 2 == 0 else (pT1, 1)
                def tr(t, tt=tt, b0=b0, pb=pb):
                    last = None
                    for k in range(8):
                        c0 = tt * E + (b0 + k) * 128
                        last = t.transpose(pb[:, k * 128:(k + 1) * 128],
                                           Igu[:, c0:c0 + 128], ident_bf[:])
                    return last
                step('tensor', tr,
                     waits=[('v', q_done)] + wz('IguT_free') + wait_bank(bi),
                     incs=(('t', 1),))
                tv = now('t')
                def cp(v, pb=pb, tt=tt, b0=b0):
                    dst = bass.AP(IguT, tt * 128 + b0 * T,
                                  [[AE * T, 128], [T, 8], [1, 128]])
                    src = bass.AP(pb, 0, [[1024, 128], [128, 8], [1, 128]])
                    return v.tensor_copy(dst, src)
                step('vector', cp, waits=[('t', tv)] + wz('wqo0', 'wqo1'),
                     incs=(('v', 1),))
                bank_free[bi] = ('v', now('v'))
                igut_b[(tt, b0 // 8)] = now('v')
        dep['IguT'] = ('v', now('v'))
        for c in range(8):
            l_ = c % 4
            wv = dma('sync', wlane[l_][:], s_d[:, c * 4096:(c + 1) * 4096], 'dw',
                     waits=wait_lane(l_))
            def mmd(t, c=c, l_=l_, wv=wv):
                t.wait_ge(sems[wv[0]], wv[1])
                b_ = (4 * c) // 8
                t.wait_ge(sems['v'], max(igut_b[(0, b_)], igut_b[(1, b_)]))
                last = None
                for al in range(4):
                    a = c * 4 + al
                    for tt in range(TSUB):
                        lhs = IguT[:, a * T + tt * 128:a * T + tt * 128 + 128]
                        for dc in range(2):
                            last = t.matmul(
                                bankF[SETA[2 * tt + dc]], lhs,
                                wlane[l_][:, al * 1024 + dc * 512:al * 1024 + (dc + 1) * 512],
                                start=(a == 0), stop=(a == 31))
                return last
            step('tensor', mmd,
                 waits=([w for b in SETA for w in wait_bank(b)])
                       if c == 0 else (),
                 incs=(('t', 1),))
            lane_free[l_] = ('t', now('t'))
        tv = now('t')
        dep['IguT_free'] = ('t', tv)
        dep['IxT_free'] = ('t', tv)
        if pre_epi is not None:
            pre_epi()
        def epi(v, r_=rbuf):
            last = None
            for tt in range(TSUB):
                for dc in range(2):
                    sl = slice(tt * DM + dc * 512, tt * DM + (dc + 1) * 512)
                    last = v.scalar_tensor_tensor(r_[:, sl],
                                                  bankF[SETA[2 * tt + dc]],
                                                  stat[:, so + C_CD + tt:
                                                       so + C_CD + tt + 1],
                                                  r_[:, sl],
                                                  op0=ALU.mult, op1=ALU.add)
            return last
        step('vector', epi, waits=(('t', tv),), incs=(('v', 1),))
        for b in SETA:
            bank_free[b] = ('v', now('v'))
        dep['r'] = ('v', now('v'))

    def mamba(rbuf):
        layer_norm(rbuf=rbuf)
        for tt in range(TSUB):
            for b0 in range(0, AD, 4):
                bi = 6 + (b0 // 4) % 2
                pb = bankF[bi]
                def trf(t, b0=b0, pb=pb, tt=tt):
                    last = None
                    for k in range(4):
                        c0 = tt * DM + (b0 + k) * 128
                        last = t.transpose(pb[:, k * 128:(k + 1) * 128],
                                           xln[:, c0:c0 + 128], ident_f[:])
                    return last
                step('tensor', trf, waits=wz('xln_w', 'xlT_free') + wait_bank(bi),
                     incs=(('t', 1),))
                tv = now('t')
                def hilo(v, pb=pb, b0=b0, tt=tt):
                    dsth = bass.AP(xlh, tt * 128 + b0 * T,
                                   [[AD * T, 128], [T, 4], [1, 128]])
                    dstl = bass.AP(xll, tt * 128 + b0 * T,
                                   [[AD * T, 128], [T, 4], [1, 128]])
                    src = pb.rearrange("p (b c) -> p b c", b=4)
                    v.tensor_copy(dsth, src)
                    return v.tensor_tensor(dstl, src, dsth, ALU.subtract)
                step('vector', hilo, waits=(('t', tv),), incs=(('v', 1),))
                bank_free[bi] = ('v', now('v'))
        dep['xlT'] = ('v', now('v'))
        for g in range(8):
            lh, ll = (g % 2) * 2, (g % 2) * 2 + 1
            wvh = dma('sync', wlane[lh][:], s_wih[:, g * 4096:(g + 1) * 4096], 'dw',
                      waits=wait_lane(lh))
            wvl = dma('sync', wlane[ll][:], s_wil[:, g * 4096:(g + 1) * 4096], 'dw',
                      waits=wait_lane(ll))
            bs = SETA if g % 2 == 0 else SETB
            def mmz(t, lh=lh, ll=ll, wvh=wvh, wvl=wvl, bs=bs):
                t.wait_ge(sems[wvh[0]], wvh[1])
                t.wait_ge(sems[wvl[0]], wvl[1])
                last = None
                for a in range(AD):
                    for el in range(4):
                        for p_, (wl_, xr_) in enumerate(((lh, xlh), (lh, xll),
                                                         (ll, xlh))):
                            rhs = xr_[:, a * T:(a + 1) * T]
                            lhs = wlane[wl_][:, a * 512 + el * 128:a * 512 + (el + 1) * 128]
                            last = t.matmul(bankF[bs[el]][:, 0:T], lhs, rhs,
                                            start=(a == 0 and p_ == 0),
                                            stop=(a == AD - 1 and p_ == 2))
                return last
            step('tensor', mmz,
                 waits=wz('xlT') + [w for b in bs for w in wait_bank(b)],
                 incs=(('t', 1),))
            tv = now('t')
            lane_free[lh] = ('t', tv)
            lane_free[ll] = ('t', tv)
            def evz(v, g=g, bs=bs):
                last = None
                for el in range(4):
                    e = 4 * g + el
                    if e < CT:
                        dst = xinp[:, e * XP + 3:e * XP + 3 + T]
                    else:
                        dst = z_t[:, (e - CT) * T:(e - CT + 1) * T]
                    last = v.tensor_copy(dst, bankF[bs[el]][:, 0:T])
                return last
            step('vector', evz, waits=(('t', tv),), incs=(('v', 1),))
            for b in bs:
                bank_free[b] = ('v', now('v'))
            if g < 4:
                def restg(v, g=g):
                    last = None
                    for ct in range(4 * g, 4 * g + 4):
                        last = v.tensor_copy(xinp[:, ct * XP:ct * XP + 3],
                                             halo[:, ct * 4:ct * 4 + 3])
                    return last
                def convg(v, g=g):
                    last = None
                    for ct in range(4 * g, 4 * g + 4):
                        p0 = ct * XP
                        dst = xa_t[:, ct * T:(ct + 1) * T]
                        v.tensor_scalar(dst, xinp[:, p0:p0 + T],
                                        convw[:, ct * 4:ct * 4 + 1], None,
                                        op0=ALU.mult)
                        for k in range(1, 4):
                            v.scalar_tensor_tensor(dst, xinp[:, p0 + k:p0 + k + T],
                                                   convw[:, ct * 4 + k:ct * 4 + k + 1],
                                                   dst, op0=ALU.mult, op1=ALU.add)
                        last = v.tensor_copy(halo[:, ct * 4:ct * 4 + 3],
                                             xinp[:, p0 + T:p0 + T + 3])
                    return last
                vchain([restg, convg])
                def silug(s, g=g):
                    last = None
                    for ct in range(4 * g, 4 * g + 4):
                        sl = slice(ct * T, (ct + 1) * T)
                        last = s.activation(xa_t[:, sl], xa_t[:, sl], ACT.Silu,
                                            bias=convb[:, ct:ct + 1], scale=1.0)
                    return last
                step('scalar', silug, waits=(('v', now('v')),), incs=(('s', 1),))
        dep['xa'] = ('s', now('s'))
        dep['xlT_free'] = ('t', now('t'))
        for ct in range(CT):
            def spl(v, ct=ct):
                o = (ct % 2) * T
                sl = slice(ct * T, (ct + 1) * T)
                v.tensor_copy(va_h[:, o:o + T], xa_t[:, sl])
                return v.tensor_tensor(va_l[:, o:o + T], xa_t[:, sl],
                                       va_h[:, o:o + T], ALU.subtract)
            step('vector', spl, waits=wz('xa', f'va{ct % 2}'), incs=(('v', 1),))
            vv_ = now('v')
            def mmp(t, ct=ct):
                o = (ct % 2) * T
                last = None
                for p_, (wl_, xr_) in enumerate(((wx_h, va_h), (wx_h, va_l),
                                                 (wx_l, va_h))):
                    last = t.matmul(bankF[6][0:96, 0:T],
                                    wl_[:, ct * 96:(ct + 1) * 96], xr_[:, o:o + T],
                                    start=(ct == 0 and p_ == 0),
                                    stop=(ct == CT - 1 and p_ == 2))
                return last
            step('tensor', mmp,
                 waits=[('v', vv_)] + (wait_bank(6) if ct == 0 else []),
                 incs=(('t', 1),))
            dep[f'va{ct % 2}'] = ('t', now('t'))
        tv = now('t')
        step('vector', lambda v: v.tensor_copy(ssm_sb[:], bankF[6][0:96, 0:T]),
             waits=(('t', tv),), incs=(('v', 1),))
        bank_free[6] = ('v', now('v'))
        vssm = now('v')
        def spl_dtr(v):
            v.tensor_copy(dtr_h[:], ssm_sb[0:64, :])
            return v.tensor_tensor(dtr_l[:], ssm_sb[0:64, :], dtr_h[:], ALU.subtract)
        step('vector', spl_dtr, incs=(('v', 1),))
        vdtr = now('v')
        for ct in range(CT):
            bi = 6 + ct % 2
            pb = bankF[bi]
            def mmdt(t, ct=ct, pb=pb):
                last = None
                for p_, (wl_, xr_) in enumerate(((wdt_h, dtr_h), (wdt_h, dtr_l),
                                                 (wdt_l, dtr_h))):
                    last = t.matmul(pb[:, 0:T], wl_[:, ct * 128:(ct + 1) * 128],
                                    xr_[:], start=(p_ == 0), stop=(p_ == 2))
                return last
            step('tensor', mmdt, waits=[('v', vdtr)] + wait_bank(bi),
                 incs=(('t', 1),))
            tv = now('t')
            def sp_s(s, ct=ct, pb=pb):
                o = (ct % 2) * 512
                s.activation(sps[:, o:o + T], pb[:, 0:T], ACT.Exp,
                             bias=bdt[:, ct:ct + 1], scale=1.0)
                return s.activation(dt_t[:, ct * T:(ct + 1) * T], sps[:, o:o + T],
                                    ACT.Ln, bias=1.0, scale=1.0)
            step('scalar', sp_s, waits=[('t', tv)] + wz(f'sps{ct % 2}'),
                 incs=(('s', 1),))
            dep[f'sps{ct % 2}'] = ('s', now('s'))
            bank_free[bi] = ('s', now('s'))
        dep['dt'] = ('s', now('s'))
        step('vector', lambda v: v.tensor_tensor(xa_t[:], dt_t[:], xa_t[:], ALU.mult),
             waits=wz('dt'), incs=(('v', 1),))
        dep['u'] = ('v', now('v'))
        wch = []
        def wout_dma(c):
            lh, ll = (c % 2) * 2, (c % 2) * 2 + 1
            wvh = dma('sync', wlane[lh][:], s_woh[:, c * 4096:(c + 1) * 4096], 'dw',
                      waits=wait_lane(lh))
            wvl = dma('sync', wlane[ll][:], s_wol[:, c * 4096:(c + 1) * 4096], 'dw',
                      waits=wait_lane(ll))
            wch.append((lh, ll, wvh, wvl))
        wout_dma(0)
        wout_dma(1)
        vo_ready = {}
        def rowdma(n):
            bv1 = dma('gpsimd', brow[0:1, (n % 2) * T:(n % 2 + 1) * T],
                      ssm_sb[64 + n:65 + n, :], 'dx',
                      waits=[('v', vssm)] + wz(f'row{n % 2}'))
            bv2 = dma('gpsimd', crow[0:1, (n % 2) * T:(n % 2 + 1) * T],
                      ssm_sb[80 + n:81 + n, :], 'dx')
            return [bv1, bv2]
        rowv = [rowdma(0)]
        pend_h = [None]

        def flush_h(s_=None):
            if pend_h[0] is None:
                return
            col_, ho_, vv_ = pend_h[0]
            pend_h[0] = None
            def hcp(s, col_=col_, ho_=ho_):
                return s.activation(h_state[:, col_:col_ + 1],
                                    h_b[:, ho_ + T - 1:ho_ + T],
                                    ACT.Copy, bias=0.0, scale=1.0)
            step('scalar', hcp, waits=(('v', vv_),), incs=(('s', 1),))
        bc_done = {}
        def bc_piece(n, k, waits):
            bb, bc_ = (4, 5) if n % 2 == 0 else (0, 1)
            o = (n % 2) * T
            bank = bb if k < 4 else bc_
            row = brow if k < 4 else crow
            c0 = (k % 4) * (T // 4)
            def pc(t, bank=bank, row=row, o=o, c0=c0):
                return t.matmul(bankF[bank][:, c0:c0 + T // 4], ones_f[:],
                                row[0:1, o + c0:o + c0 + T // 4],
                                start=True, stop=True)
            step('tensor', pc, waits=waits, incs=(('t', 1),))
            bc_done[n] = now('t')
        for k in range(8):
            w = (rowv[0] + wait_bank(4) + wait_bank(5)) if k == 0 else ()
            bc_piece(0, k, w)
        for n in range(NST):
            bmo = (n % 2) * T
            tbc = bc_done[n]
            dep[f'row{n % 2}'] = ('t', tbc)
            if n + 1 < NST:
                rowv.append(rowdma(n + 1))
            bb, bc_ = (4, 5) if n % 2 == 0 else (0, 1)
            def evbc(s, bb=bb, bc_=bc_, bmo=bmo):
                s.activation(bm_sb[:, bmo:bmo + T], bankF[bb][:, 0:T],
                             ACT.Copy, bias=0.0, scale=1.0)
                return s.activation(cm_sb[:, bmo:bmo + T], bankF[bc_][:, 0:T],
                                    ACT.Copy, bias=0.0, scale=1.0)
            step('scalar', evbc, waits=[('t', tbc)] + wz(f'bm{n % 2}'),
                 incs=(('s', 1),))
            bank_free[bb] = ('s', now('s'))
            bank_free[bc_] = ('s', now('s'))
            last_n = (n == NST - 1)

            def emit_fin(ct):
                o = (ct % 2) * T
                def sz_s(s, ct=ct, o=o):
                    return s.activation(sz_b[:, o:o + T],
                                        z_t[:, ct * T:(ct + 1) * T],
                                        ACT.Silu, bias=0.0, scale=1.0)
                step('scalar', sz_s, waits=wz(f'sz{ct % 2}'), incs=(('s', 1),))
                svz = now('s')
                def fin(v, ct=ct, o=o):
                    dst = vv_t[:, ct * T:(ct + 1) * T]
                    v.scalar_tensor_tensor(dst,
                                           xinp[:, ct * XP + 3:ct * XP + 3 + T],
                                           dvec[:, ct:ct + 1], dst,
                                           op0=ALU.mult, op1=ALU.add)
                    v.tensor_tensor(dst, dst, sz_b[:, o:o + T], ALU.mult)
                    ov_ = (ct % 8) * T
                    v.tensor_copy(vo_h[:, ov_:ov_ + T], dst)
                    return v.tensor_tensor(vo_l[:, ov_:ov_ + T], dst,
                                           vo_h[:, ov_:ov_ + T], ALU.subtract)
                step('vector', fin,
                     waits=[('s', svz)] + wz(f'acc{ct}',
                                             f'voT{(ct // 4) % 2}'),
                     incs=(('v', 1),))
                dep[f'sz{ct % 2}'] = ('v', now('v'))
                vo_ready[ct] = now('v')
                if ct % 4 == 3:
                    c = ct // 4
                    lh, ll, wvh, wvl = wch[c]
                    def mmo(t, c=c, lh=lh, ll=ll, wvh=wvh, wvl=wvl,
                            vd=now('v')):
                        t.wait_ge(sems[wvh[0]], wvh[1])
                        t.wait_ge(sems[wvl[0]], wvl[1])
                        t.wait_ge(sems['v'], vd)
                        last = None
                        for cti in range(4 * c, 4 * c + 4):
                            ov_ = (cti % 8) * T
                            co = (cti % 4) * 1024
                            for tt in range(TSUB):
                                for p_, (vb, wl_) in enumerate(
                                        ((vo_h, lh), (vo_l, lh), (vo_h, ll))):
                                    lhs = vb[:, ov_ + tt * 128:ov_ + tt * 128 + 128]
                                    for dc in range(2):
                                        last = t.matmul(
                                            bankF[SETA[2 * tt + dc]], lhs,
                                            wlane[wl_][:, co + dc * 512:co + (dc + 1) * 512],
                                            start=(c == 0 and cti == 0 and p_ == 0),
                                            stop=(c == 3 and cti == 15 and p_ == 2))
                        return last
                    step('tensor', mmo,
                         waits=([w for b in SETA for w in wait_bank(b)]
                                if c == 0 else ()),
                         incs=(('t', 1),))
                    dep[f'voT{c % 2}'] = ('t', now('t'))
                    lane_free[lh] = ('t', now('t'))
                    lane_free[ll] = ('t', now('t'))
                    if c + 2 < 4:
                        wout_dma(c + 2)

            for ct in range(CT):
                o = (ct % 2) * T
                ho = (ct % 4) * T
                col = ct * NST + n
                def mk_dA(s, ct=ct, o=o, col=col):
                    return s.activation(dA_b[:, o:o + T],
                                        dt_t[:, ct * T:(ct + 1) * T],
                                        ACT.Exp, bias=0.0,
                                        scale=a_neg[:, col:col + 1])
                step('scalar', mk_dA, waits=wz(f'dA{ct % 2}'), incs=(('s', 1),))
                sv = now('s')
                flush_h()
                def scn(v, ct=ct, o=o, ho=ho, col=col, n=n, bmo=bmo):
                    v.tensor_tensor(dBx_b[:, o:o + T],
                                    xa_t[:, ct * T:(ct + 1) * T],
                                    bm_sb[:, bmo:bmo + T], ALU.mult)
                    v.tensor_tensor_scan(h_b[:, ho:ho + T], dA_b[:, o:o + T],
                                         dBx_b[:, o:o + T],
                                         h_state[:, col:col + 1],
                                         op0=ALU.mult, op1=ALU.add)
                    if n == 0:
                        return v.tensor_tensor(vv_t[:, ct * T:(ct + 1) * T],
                                               h_b[:, ho:ho + T],
                                               cm_sb[:, bmo:bmo + T], ALU.mult)
                    return v.tensor_tensor(P_b[:, (ct % 2) * T:(ct % 2) * T + T],
                                           h_b[:, ho:ho + T],
                                           cm_sb[:, bmo:bmo + T], ALU.mult)
                step('vector', scn,
                     waits=[('s', sv)] + (wz(f'P{ct % 2}') if n > 0 else []),
                     incs=(('v', 1),))
                dep[f'dA{ct % 2}'] = ('v', now('v'))
                pend_h[0] = (col, ho, now('v'))
                if n > 0:
                    def post(g_, ct=ct):
                        return g_.tensor_tensor(vv_t[:, ct * T:(ct + 1) * T],
                                                vv_t[:, ct * T:(ct + 1) * T],
                                                P_b[:, (ct % 2) * T:(ct % 2) * T + T],
                                                ALU.add)
                    step('gpsimd', post, waits=(('v', now('v')),), incs=(('g', 1),))
                    dep[f'P{ct % 2}'] = ('g', now('g'))
                    dep[f'acc{ct}'] = ('g', now('g'))
                else:
                    dep[f'acc{ct}'] = ('v', now('v'))
                if ct % 2 == 0 and n + 1 < NST:
                    k = ct // 2
                    nb, ncb = (4, 5) if (n + 1) % 2 == 0 else (0, 1)
                    w = [('v', now('v'))]
                    if k == 0:
                        w += rowv[n + 1] + wait_bank(nb) + wait_bank(ncb) + \
                             wz(f'bm{(n + 1) % 2}')
                    bc_piece(n + 1, k, w)
                if last_n and ct >= 2:
                    emit_fin(ct - 2)
            dep[f'bm{n % 2}'] = ('g', now('g'))
            if last_n:
                emit_fin(CT - 2)
                emit_fin(CT - 1)
        flush_h()
        dep['hstate'] = ('g', now('g'))
        def epo(v, r_=None):
            r_ = rbuf if r_ is None else r_
            last = None
            for tt in range(TSUB):
                for dc in range(2):
                    sl = slice(tt * DM + dc * 512, tt * DM + (dc + 1) * 512)
                    last = v.tensor_tensor(r_[:, sl], bankF[SETA[2 * tt + dc]],
                                           r_[:, sl], ALU.add)
            return last
        step('vector', epo, waits=(('t', now('t')),), incs=(('v', 1),))
        for b in SETA:
            bank_free[b] = ('v', now('v'))
        dep['r'] = ('v', now('v'))
        dep['xln'] = ('v', now('v'))

    bufs = (r_t, r_t2)
    prev_ov = [None]

    def prefetch(tile):
        if tile + 1 >= ntiles:
            return None, None
        nb = bufs[(tile + 1) % 2]
        w = [prev_ov[0]] if prev_ov[0] else []
        return nb, dma('gpsimd', nb[:], x_ap(d_x, tile + 1), 'dx', waits=w)

    ix_next = ffn_head(0, rwaits=[dep['rbuf0']], rbuf=bufs[0])
    for tile in range(ntiles):
        rb = bufs[tile % 2]
        ffn_body(ix_next, 0, rb)
        ix_next = None
        if stages >= 2:
            mamba(rb)
        nb, ldv = prefetch(tile)
        if stages >= 3:
            ix2 = ffn_head(24, rbuf=rb)
            if nb is not None:
                hold = {}
                def hook(nb=nb, ldv=ldv, hold=hold):
                    hold['ix'] = ffn_head(0, rwaits=[ldv], rbuf=nb)
                ffn_body(ix2, 24, rb, pre_epi=hook)
                ix_next = hold['ix']
            else:
                ffn_body(ix2, 24, rb)
        prev_ov[0] = dma('gpsimd', x_ap(d_out, tile), rb[:], 'dx', waits=wz('r'))
        if stages < 3 and nb is not None:
            ix_next = ffn_head(0, rwaits=[ldv], rbuf=nb)
    if debug:
        dbg_specs = {"sc_bc": sc_bc, "stat": stat, "xln": xln, "Ix": Ix,
                     "IxT": IxT, "gu": gu, "Igu": Igu, "r_t": r_t,
                     "sps": sps, "u16": u16, "wl0": wlane[0], "IguT": IguT,
                     "xinp": xinp, "z_t": z_t, "xa_t": xa_t, "dt_t": dt_t,
                     "vv_t": vv_t, "ssm_sb": ssm_sb, "h_state": h_state,
                     "a_neg": a_neg, "brow": brow, "crow": crow}
        wall = [(s_, v_) for s_, v_ in cnt.items()]
        first_dbg = True
        for nm, tns in dbg_specs.items():
            shp = list(tns.shape) if hasattr(tns, 'shape') else None
            dt_ = tns.dtype
            dd = nc.dram_tensor("dbg_" + nm, shp, dt_, kind="ExternalOutput").ap()
            dma('gpsimd', dd, tns[:], 'dx', waits=tuple(wall) if first_dbg else ())
            first_dbg = False
    step('gpsimd', lambda g: g.memset(srow[0:1, 6:8], 0.0),
         waits=tuple(qdone('dx')) + tuple(qdone('dw')))

    with ExitStack() as st:
        for n_ in sorted(cnt.keys()):
            sems[n_] = st.enter_context(nc.semaphore(n_))
        block = st.enter_context(nc.Block())

        def section(engname):
            def body(eng):
                for e, waits, fn, incs in ops:
                    if e != engname:
                        continue
                    for sname, val in waits:
                        eng.wait_ge(sems[sname], val)
                    inst = fn(eng)
                    for sname, by in incs:
                        inst.then_inc(sems[sname], by)
            return body

        block.sync(section('sync'))
        block.gpsimd(section('gpsimd'))
        block.vector(section('vector'))
        block.scalar(section('scalar'))
        block.tensor(section('tensor'))
    return nc


_NC_CACHE = {}


def kernel(L0_attn_x, L0_attn_W_in, L0_attn_conv_w, L0_attn_conv_b, L0_attn_W_x,
           L0_attn_W_dt, L0_attn_b_dt, L0_attn_A_log, L0_attn_D, L0_attn_W_out,
           L0_ffn_w_gate, L0_ffn_w_up, L0_ffn_w_down):
    f32 = np.float32

    def pm(w, atiles):
        r, c = w.shape
        assert r == atiles * 128, (r, atiles)
        return np.ascontiguousarray(
            np.asarray(w, f32).reshape(atiles, 128, c).transpose(1, 0, 2)
            .reshape(128, atiles * c))

    wgT = pm(np.asarray(L0_ffn_w_gate, f32).T, AD)
    wuT = pm(np.asarray(L0_ffn_w_up, f32).T, AD)
    wdT = pm(np.asarray(L0_ffn_w_down, f32).T, AE)
    win = pm(L0_attn_W_in, AD)
    wout = pm(L0_attn_W_out, CT)
    wx = pm(L0_attn_W_x, CT)
    alog = pm(L0_attn_A_log, CT)
    convw = pm(L0_attn_conv_w, CT)
    convb = pm(np.asarray(L0_attn_conv_b, f32).reshape(DI, 1), CT)
    dd = pm(np.asarray(L0_attn_D, f32).reshape(DI, 1), CT)
    bdt = pm(np.asarray(L0_attn_b_dt, f32).reshape(DI, 1), CT)
    wdt = np.ascontiguousarray(np.asarray(L0_attn_W_dt, f32))

    key = (_STAGES, _NTILES)
    if key not in _NC_CACHE:
        _NC_CACHE[key] = build_program(_STAGES, _NTILES)
    nc = _NC_CACHE[key]

    x = np.asarray(L0_attn_x, f32)
    in_maps = []
    for b in range(8):
        in_maps.append({
            "x": np.ascontiguousarray(x[b]),
            "wgT": wgT, "wuT": wuT, "wdT": wdT,
            "win": win, "wout": wout, "wx": wx, "wdt": wdt,
            "alog": alog, "convw": convw, "convb": convb,
            "dvec": dd, "bdt": bdt,
        })
    res = run_bass_kernel_spmd(nc, in_maps, core_ids=list(range(8)))
    return np.stack([np.asarray(r["out"], f32) for r in res.results], axis=0)


# version 35
# speedup vs baseline: 1.0522x; 1.0170x over previous
import numpy as np
import ml_dtypes
from contextlib import ExitStack
import concourse.bass as bass
import concourse.mybir as mybir
from concourse.bass_utils import run_bass_kernel_spmd

F32 = mybir.dt.float32
BF16 = mybir.dt.bfloat16
F16 = mybir.dt.float16
ALU = mybir.AluOpType
ACT = mybir.ActivationFunctionType
AX = mybir.AxisListType

S, DM, E, DI, NST = 4096, 1024, 4096, 2048, 16
T = 256
NT = S // T
TSUB = T // 128
CT = DI // 128
AD = DM // 128
AE = E // 128
XP = T + 4
_STAGES = 3
_NTILES = NT
_DEBUG = False

C_SUM, C_SQ, C_MEAN, C_MSQ, C_VAR, C_VE, C_S0, C_T2 = 0, 2, 4, 6, 8, 10, 12, 14
C_MX, C_SX, C_ISX, C_CG, C_CU, C_CD, C_T3, C_ACC = 16, 18, 20, 22, 24, 26, 28, 32


def build_program(stages=3, ntiles=None, debug=None):
    ntiles = ntiles or _NTILES
    debug = _DEBUG if debug is None else debug
    nc = bass.Bass("TRN2", target_bir_lowering=False, detect_race_conditions=False)
    ops = []
    cnt = {}
    sems = {}

    def step(eng, fn, waits=(), incs=()):
        ops.append((eng, tuple(waits), fn, tuple(incs)))
        for s_, by in incs:
            cnt[s_] = cnt.get(s_, 0) + by

    def now(s_):
        return cnt.get(s_, 0)

    d_x = nc.dram_tensor("x", [S, DM], F32, kind="ExternalInput").ap()
    d_out = nc.dram_tensor("out", [S, DM], F32, kind="ExternalOutput").ap()
    d_wg = nc.dram_tensor("wgT", [128, AD * E], F32, kind="ExternalInput").ap()
    d_wu = nc.dram_tensor("wuT", [128, AD * E], F32, kind="ExternalInput").ap()
    d_wd = nc.dram_tensor("wdT", [128, AE * DM], F32, kind="ExternalInput").ap()
    d_win = nc.dram_tensor("win", [128, AD * E], F32, kind="ExternalInput").ap()
    d_wout = nc.dram_tensor("wout", [128, CT * DM], F32, kind="ExternalInput").ap()
    d_wx = nc.dram_tensor("wx", [128, CT * 96], F32, kind="ExternalInput").ap()
    d_wdt = nc.dram_tensor("wdt", [64, DI], F32, kind="ExternalInput").ap()
    d_alog = nc.dram_tensor("alog", [128, CT * NST], F32, kind="ExternalInput").ap()
    d_convw = nc.dram_tensor("convw", [128, CT * 4], F32, kind="ExternalInput").ap()
    d_convb = nc.dram_tensor("convb", [128, CT], F32, kind="ExternalInput").ap()
    d_dd = nc.dram_tensor("dvec", [128, CT], F32, kind="ExternalInput").ap()
    d_bdt = nc.dram_tensor("bdt", [128, CT], F32, kind="ExternalInput").ap()
    s_gu = nc.dram_tensor("s_gu", [128, AD * 2 * E], BF16, kind="Internal").ap()
    s_d = nc.dram_tensor("s_d", [128, AE * DM], BF16, kind="Internal").ap()
    s_wih = nc.dram_tensor("s_wih", [128, AD * E], BF16, kind="Internal").ap()
    s_wil = nc.dram_tensor("s_wil", [128, AD * E], BF16, kind="Internal").ap()
    s_woh = nc.dram_tensor("s_woh", [128, CT * DM], BF16, kind="Internal").ap()
    s_wol = nc.dram_tensor("s_wol", [128, CT * DM], BF16, kind="Internal").ap()

    A = nc.alloc_sbuf_tensor
    ident_bf = A("ident_bf", [128, 128], BF16)
    ident_f = A("ident_f", [128, 128], F32)
    ones_f = A("ones_f", [1, 128], F32)
    iotp = A("iotp", [128, 1], F32)
    iotf = A("iotf", [128, 128], F32)
    a_neg = A("a_neg", [128, CT * NST], F32)
    convw = A("convw_s", [128, CT * 4], F32)
    convb = A("convb_s", [128, CT], F32)
    dvec = A("dvec_s", [128, CT], F32)
    bdt = A("bdt_s", [128, CT], F32)
    wx_h = A("wx_h", [128, CT * 96], BF16)
    wx_l = A("wx_l", [128, CT * 96], BF16)
    wdt_h = A("wdt_h", [64, DI], BF16)
    wdt_l = A("wdt_l", [64, DI], BF16)
    h_state = A("h_state", [128, CT * NST], F32)
    stat = A("stat", [128, 64], F32)
    sc_bc = A("sc_bc", [128, 8], F32)
    srow = A("srow", [1, 8], F32)
    brow = A("brow", [1, 2 * T], F32)
    crow = A("crow", [1, 2 * T], F32)
    wlane = [A(f"wlane{i}", [128, 4096], BF16) for i in range(4)]
    r_t = A("r_t", [128, TSUB * DM], F32)
    xln = A("xln", [128, TSUB * DM], F32)

    Ix = A("Ix", [128, TSUB * DM], BF16)
    IxT = A("IxT", [128, AD * T], BF16)
    xlh = A("xlh", [128, AD * T], BF16)
    xll = A("xll", [128, AD * T], BF16)
    dA_b = A("dA_b", [128, 2 * T], F32)
    dBx_b = A("dBx_b", [128, 2 * T], F32)
    h_b = A("h_b", [128, 4 * T], F32)
    u16 = nc.alloc_sbuf_tensor_at("u16", [128, 2048], F16,
                                  offset=nc.lookup_mloc(h_b).addr)
    P_b = A("P_b", [128, 2 * T], F32)
    bm_sb = A("bm_sb", [128, 2 * T], F32)
    cm_sb = A("cm_sb", [128, 2 * T], F32)
    r_t2 = A("r_t2", [128, TSUB * DM], F32)
    sz_b = A("sz_b", [128, 2 * T], F32)
    ssm_sb = A("ssm_sb", [96, T], F32)
    dtr_h = A("dtr_h", [64, T], BF16)
    dtr_l = A("dtr_l", [64, T], BF16)
    va_h = A("va_h", [128, 2 * T], BF16)
    va_l = A("va_l", [128, 2 * T], BF16)
    vo_h = A("vo_h", [128, 8 * T], BF16)
    vo_l = A("vo_l", [128, 8 * T], BF16)
    sps = A("sps", [128, 1024], F32)
    sqd = nc.alloc_sbuf_tensor_at("sqd", [128, DM], F32,
                                  offset=nc.lookup_mloc(sps).addr)
    halo = A("halo", [128, CT * 4], F32)
    AR = 82176
    arena = A("arena", [128, AR], mybir.dt.uint8)
    base = nc.lookup_mloc(arena).addr

    def at(name, shape, dtype, off):
        return nc.alloc_sbuf_tensor_at(name, shape, dtype, offset=base + off)

    xinp = at("xinp", [128, CT * XP], F32, 0)
    z_t = at("z_t", [128, CT * T], F32, 16640)
    xa_t = at("xa_t", [128, CT * T], F32, 33024)
    dt_t = at("dt_t", [128, CT * T], F32, 49408)
    vv_t = at("vv_t", [128, CT * T], F32, 65792)
    gu = at("gu", [128, TSUB * E], F32, 0)
    Igu = at("Igu", [128, TSUB * E], BF16, 32768)
    IguT = at("IguT", [128, AE * T], BF16, 49152)
    wq_in = at("wq_in", [128, 4096], F32, 0)
    wq_t1 = at("wq_t1", [128, 4096], F32, 16384)
    wq_t2 = at("wq_t2", [128, 4096], F32, 32768)
    wq_o1 = at("wq_o1", [128, 4096], BF16, 49152)
    wq_o2 = at("wq_o2", [128, 4096], BF16, 57344)
    wq_t3 = at("wq_t3", [128, 4096], F32, 65792)

    rem = nc.sbuf_bytes_remaining
    assert rem >= 0, f"sbuf over budget: {rem}"

    pT0 = nc.alloc_psum_tensor("pT0", [128, 1024], BF16)
    pT1 = nc.alloc_psum_tensor("pT1", [128, 1024], BF16)
    pF = [nc.alloc_psum_tensor(f"pF{i}", [128, 512], F32) for i in range(6)]
    bankF = [pT0[:, 0:1024].bitcast(F32), pT1[:, 0:1024].bitcast(F32)] + \
            [pF[i][:, 0:512] for i in range(6)]
    SETA, SETB = (2, 3, 4, 5), (6, 7, 0, 1)

    bank_free = {i: None for i in range(8)}
    lane_free = {i: None for i in range(4)}
    dep = {}

    def wait_bank(b):
        return [bank_free[b]] if bank_free[b] else []

    def wait_lane(i):
        return [lane_free[i]] if lane_free[i] else []

    def wz(*names):
        seen = {}
        for n_ in names:
            if n_ in dep:
                s_, v_ = dep[n_]
                seen[s_] = max(seen.get(s_, 0), v_)
        return [(s_, v_) for s_, v_ in seen.items()]

    QKS = {'dw': 8, 'dx': 4}
    qstate = {q: [0, [0] * k] for q, k in QKS.items()}

    def dma(qeng, out_ap, in_ap, q, waits=()):
        st_ = qstate[q]
        k = st_[0] % QKS[q]
        st_[0] += 1
        sem = f'{q}{k}'
        w = list(waits)
        if st_[1][k]:
            w.append((sem, st_[1][k]))
        step(qeng, lambda e, o=out_ap, i=in_ap: e.dma_start(o, i),
             waits=w, incs=((sem, 16),))
        st_[1][k] = now(sem)
        return (sem, st_[1][k])

    def qdone(q):
        return [(f'{q}{k}', v) for k, v in enumerate(qstate[q][1]) if v]

    SC = lambda c: stat[:, c:c + 1]
    SC2 = lambda c: stat[:, c:c + 2]

    def vchain(fns, waits=()):
        for i_, fn_ in enumerate(fns):
            w = list(waits) if i_ == 0 else []
            w.append(('v', now('v')))
            step('vector', fn_, waits=w, incs=(('v', 1),))

    step('gpsimd', lambda g: g.iota(iotp[:], [[0, 1]], channel_multiplier=1,
                                    allow_small_or_imprecise_dtypes=True))
    step('gpsimd', lambda g: g.iota(iotf[:], [[1, 128]], channel_multiplier=0,
                                    allow_small_or_imprecise_dtypes=True),
         incs=(('g', 1),))
    for dst, src in ((a_neg[:], d_alog), (convw[:], d_convw), (convb[:], d_convb),
                     (dvec[:], d_dd), (bdt[:], d_bdt),
                     (wq_t1[:, 0:CT * 96], d_wx), (wq_t2[0:64, 0:DI], d_wdt)):
        dma('gpsimd', dst, src, 'dx')
    v_small = qdone('dx')
    def x_ap(d, tile):
        return bass.AP(d.tensor, tile * T * DM,
                       [[DM, 128], [128 * DM, TSUB], [1, DM]])
    dep['rbuf0'] = dma('gpsimd', r_t[:], x_ap(d_x, 0), 'dx')

    def vsetup(v):
        v.tensor_scalar(ident_bf[:], iotf[:], iotp[:, 0:1], None, op0=ALU.is_equal)
        v.tensor_scalar(ident_f[:], iotf[:], iotp[:, 0:1], None, op0=ALU.is_equal)
        v.memset(ones_f[:], 1.0)
        v.memset(h_state[:], 0.0)
        v.memset(stat[:, C_ACC + 3:C_ACC + 4], 0.0)
        v.memset(halo[:], 0.0)
        v.tensor_copy(wx_h[:], wq_t1[:, 0:CT * 96])
        v.tensor_tensor(wx_l[:], wq_t1[:, 0:CT * 96], wx_h[:], ALU.subtract)
        v.tensor_copy(wdt_h[:], wq_t2[0:64, 0:DI])
        return v.tensor_tensor(wdt_l[:], wq_t2[0:64, 0:DI], wdt_h[:], ALU.subtract)
    step('vector', vsetup, waits=[('g', 1)] + v_small, incs=(('v', 1),))
    dep['wq0'] = ('v', now('v'))
    dep['wq1'] = ('v', now('v'))
    ldbuf = (wq_in, wq_t1, wq_t3)
    ldi = [0]

    def ld_next():
        i_ = ldi[0] % 3
        ldi[0] += 1
        return ldbuf[i_], f'wq{i_}'
    step('scalar', lambda s: s.activation(a_neg[:], a_neg[:], ACT.Exp,
                                          bias=0.0, scale=1.0),
         waits=v_small, incs=(('s', 1),))
    step('vector', lambda v: v.tensor_single_scalar(a_neg[:], a_neg[:], -1.0, ALU.mult),
         waits=(('s', now('s')),), incs=(('v', 1),))

    raws = [d_wg, d_wu, d_wd]
    for j in range(3):
        for c in range(8):
            buf, gn = ld_next()
            wv = dma('sync', buf[:], raws[j][:, c * 4096:(c + 1) * 4096], 'dw',
                     waits=wz(gn))
            def red1(v, b=buf):
                return v.reduce_sum(SC(C_T3), b[:], axis=AX.X,
                                    apply_absolute_value=True)
            def red2(v, j=j, c=c):
                if c == 0:
                    return v.tensor_copy(SC(C_ACC + j), SC(C_T3))
                return v.tensor_tensor(SC(C_ACC + j), SC(C_ACC + j), SC(C_T3), ALU.add)
            vchain([red1, red2], waits=(wv,))
            dep[gn] = ('v', now('v'))
    step('tensor', lambda t: t.transpose(pF[4][0:4, 0:128], stat[:, C_ACC:C_ACC + 4],
                                         ident_f[:]),
         waits=wz('wq0', 'wq1', 'wq2'), incs=(('t', 1),))
    vchain([
        lambda v: v.reduce_sum(stat[0:4, 36:37], pF[4][0:4, 0:128], axis=AX.X),
        lambda v: v.tensor_scalar(stat[0:4, 37:38], stat[0:4, 36:37],
                                  1.0 / (DM * E), None, op0=ALU.mult),
        lambda v: v.tensor_scalar(stat[0:4, 38:39], stat[0:4, 37:38], 1e-5, None,
                                  op0=ALU.max),
        lambda v: v.reciprocal(stat[0:4, 39:40], stat[0:4, 38:39]),
        lambda v: v.reciprocal(stat[0:4, 40:41], stat[0:4, 39:40]),
    ], waits=(('t', now('t')),))
    bank_free[6] = ('v', now('v'))
    rv1 = dma('gpsimd', srow[0:1, 0:3], stat[0:3, 39:40], 'dx', waits=(('v', now('v')),))
    rv2 = dma('gpsimd', srow[0:1, 3:6], stat[0:3, 40:41], 'dx')
    step('tensor', lambda t: t.matmul(pF[5][:, 0:6], ones_f[:], srow[0:1, 0:6],
                                      start=True, stop=True),
         waits=[rv1, rv2] + wait_bank(7), incs=(('t', 1),))
    step('vector', lambda v: v.tensor_copy(sc_bc[:, 0:6], pF[5][:, 0:6]),
         waits=(('t', now('t')),), incs=(('v', 1),))
    bank_free[7] = ('v', now('v'))
    dep['sc_bc'] = ('v', now('v'))

    for j in range(3):
        for c in range(8):
            pin, gn = ld_next()
            out = wq_o1 if c % 2 == 0 else wq_o2
            for h in range(2):
                sl = slice(h * 2048, (h + 1) * 2048)
                wv = dma('sync', pin[:, sl],
                         raws[j][:, c * 4096 + h * 2048:
                                 c * 4096 + (h + 1) * 2048], 'dw',
                         waits=wz(gn))
                def quant(v, pin=pin, out=out, j=j, sl=sl):
                    v.tensor_scalar(wq_t2[:, sl], pin[:, sl], sc_bc[:, j:j + 1],
                                    None, op0=ALU.mult)
                    v.tensor_single_scalar(out[:, sl], wq_t2[:, sl], -0.5,
                                           ALU.is_le)
                    return v.scalar_tensor_tensor(out[:, sl], wq_t2[:, sl],
                                                  0.5, out[:, sl],
                                                  op0=ALU.is_ge,
                                                  op1=ALU.subtract)
                step('vector', quant,
                     waits=[wv] + (wz('sc_bc', f'wqo{c % 2}')
                                   if h == 0 else []), incs=(('v', 1),))
            dep[gn] = ('v', now('v'))
            vq = now('v')
            if j < 2:
                dst = bass.AP(s_gu.tensor, c * 8192 + j * 512,
                              [[AD * 2 * E, 128], [1024, 8], [1, 512]])
            else:
                dst = s_d[:, c * 4096:(c + 1) * 4096]
            dep[f'wqo{c % 2}'] = dma('sync', dst, out[:], 'dw', waits=(('v', vq),))
    for (raw, nch, dh, dl, grouped) in ((d_win, 8, s_wih, s_wil, True),
                                        (d_wout, 4, s_woh, s_wol, False)):
        for c in range(nch):
            pin, gn = ld_next()
            for h in range(2):
                sl = slice(h * 2048, (h + 1) * 2048)
                wv = dma('sync', pin[:, sl],
                         raw[:, c * 4096 + h * 2048:
                             c * 4096 + (h + 1) * 2048], 'dw',
                         waits=wz(gn))
                def split(v, pin=pin, sl=sl):
                    v.tensor_copy(wq_o1[:, sl], pin[:, sl])
                    return v.tensor_tensor(wq_o2[:, sl], pin[:, sl],
                                           wq_o1[:, sl], ALU.subtract)
                step('vector', split,
                     waits=[wv] + (wz('wqo0', 'wqo1') if h == 0 else []),
                     incs=(('v', 1),))
            dep[gn] = ('v', now('v'))
            vq = now('v')
            if grouped:
                dsth = bass.AP(s_wih.tensor, c * 512,
                               [[AD * E, 128], [4096, 8], [1, 512]])
                dstl = bass.AP(s_wil.tensor, c * 512,
                               [[AD * E, 128], [4096, 8], [1, 512]])
            else:
                dsth = s_woh[:, c * 4096:(c + 1) * 4096]
                dstl = s_wol[:, c * 4096:(c + 1) * 4096]
            dma('sync', dsth, wq_o1[:], 'dw', waits=(('v', vq),))
            ov = dma('sync', dstl, wq_o2[:], 'dw')
            dep['wqo0'] = ov
            dep['wqo1'] = ov
    for i_, pv in enumerate(qdone('dw')):
        dep[f'prol{i_}'] = pv

    cur_r = [r_t]

    def layer_norm(rwaits=None, rbuf=None):
        r_ = rbuf
        def ln_s(s):
            last = None
            for tt in range(TSUB):
                s.activation(xln[:, tt * DM:(tt + 1) * DM],
                             r_[:, tt * DM:(tt + 1) * DM],
                             ACT.Identity, bias=0.0, scale=1.0,
                             accum_out=stat[:, C_SUM + tt:C_SUM + tt + 1])
                last = s.activation(sqd[:], r_[:, tt * DM:(tt + 1) * DM],
                                    ACT.Square, bias=0.0, scale=1.0,
                                    accum_out=stat[:, C_SQ + tt:C_SQ + tt + 1])
            return last
        step('scalar', ln_s,
             waits=(wz('r') if rwaits is None else list(rwaits)) +
                   wz('xln', 'sps0', 'sps1'), incs=(('s', 1),))
        sv = now('s')
        vchain([
            lambda v: (v.tensor_scalar(SC2(C_MEAN), SC2(C_SUM), 1.0 / DM, None,
                                       op0=ALU.mult),
                       v.tensor_scalar(SC2(C_MSQ), SC2(C_SQ), 1.0 / DM, None,
                                       op0=ALU.mult))[-1],
            lambda v: v.tensor_tensor(SC2(C_VAR), SC2(C_MEAN), SC2(C_MEAN), ALU.mult),
            lambda v: v.tensor_tensor(SC2(C_VAR), SC2(C_MSQ), SC2(C_VAR), ALU.subtract),
            lambda v: v.tensor_scalar(SC2(C_VE), SC2(C_VAR), 1e-5, None, op0=ALU.add),
        ], waits=(('s', sv),))
        def sqrt_ln(s):
            s.activation(SC2(C_T2), SC2(C_VE), ACT.Ln, bias=0.0, scale=1.0)
            return s.activation(SC2(C_S0), SC2(C_T2), ACT.Exp, bias=0.0, scale=0.5)
        step('scalar', sqrt_ln, waits=(('v', now('v')),), incs=(('s', 1),))
        sv = now('s')
        def ln_apply(v):
            last = None
            for tt in range(TSUB):
                sl = slice(tt * DM, (tt + 1) * DM)
                last = v.tensor_scalar(xln[:, sl], xln[:, sl],
                                       stat[:, C_MEAN + tt:C_MEAN + tt + 1],
                                       stat[:, C_T2 + tt:C_T2 + tt + 1],
                                       op0=ALU.subtract, op1=ALU.mult)
            return last
        vchain([lambda v: v.reciprocal(SC2(C_T2), SC2(C_S0)), ln_apply],
               waits=(('s', sv),))
        dep['xln_w'] = ('v', now('v'))

    def ffn_head(so, rwaits=None, rbuf=None):
        layer_norm(rwaits=rwaits, rbuf=rbuf)
        def qtail(v):
            last = None
            for tt in range(TSUB):
                sl = slice(tt * DM, (tt + 1) * DM)
                v.tensor_scalar(u16[:, 0:DM], xln[:, sl],
                                stat[:, so + C_SX + tt:so + C_SX + tt + 1],
                                1536.0, op0=ALU.mult, op1=ALU.add)
                last = v.tensor_single_scalar(Ix[:, sl], u16[:, 0:DM], 1536.0,
                                              ALU.subtract)
            return last
        vchain([
            lambda v: (v.reduce_max(SC(so + C_MX), xln[:, 0:DM], axis=AX.X,
                                    apply_absolute_value=True),
                       v.reduce_max(SC(so + C_MX + 1), xln[:, DM:2 * DM],
                                    axis=AX.X,
                                    apply_absolute_value=True))[-1],
            lambda v: v.tensor_scalar(SC2(so + C_MX), SC2(so + C_MX), 1e-5,
                                      None, op0=ALU.max),
            lambda v: v.reciprocal(SC2(so + C_SX), SC2(so + C_MX)),
            lambda v: v.tensor_scalar(SC2(so + C_SX), SC2(so + C_SX), 127.0,
                                      None, op0=ALU.mult),
            lambda v: v.reciprocal(SC2(so + C_ISX), SC2(so + C_SX)),
            lambda v: (v.tensor_scalar(SC2(so + C_CG), SC2(so + C_ISX),
                                       sc_bc[:, 3:4], None, op0=ALU.mult),
                       v.tensor_scalar(SC2(so + C_CU), SC2(so + C_ISX),
                                       sc_bc[:, 4:5], None, op0=ALU.mult))[-1],
            qtail,
        ], waits=wz('sc_bc', 'Ix_free'))
        dep['Ix_w'] = ('v', now('v'))
        dep['xln'] = ('v', now('v'))
        ixt_a = []
        for a in range(AD):
            pb, bi = (pT0, 0) if a % 2 == 0 else (pT1, 1)
            def tr(t, a=a, pb=pb):
                t.transpose(pb[:, 0:128],
                            Ix[:, a * 128:a * 128 + 128], ident_bf[:])
                return t.transpose(pb[:, 128:256],
                                   Ix[:, DM + a * 128:DM + a * 128 + 128],
                                   ident_bf[:])
            step('tensor', tr,
                 waits=(wz('Ix_w', 'IxT_free') if a < 2 else []) + wait_bank(bi),
                 incs=(('t', 1),))
            tv = now('t')
            def cp(v, pb=pb, a=a):
                dst = bass.AP(IxT, a * T, [[AD * T, 128], [128, 2], [1, 128]])
                src = bass.AP(pb, 0, [[1024, 128], [128, 2], [1, 128]])
                return v.tensor_copy(dst, src)
            step('vector', cp, waits=(('t', tv),), incs=(('v', 1),))
            bank_free[bi] = ('v', now('v'))
            ixt_a.append(now('v'))
        dep['IxT'] = ('v', now('v'))
        dep['Ix_free'] = ('t', now('t'))
        return ixt_a

    def ffn_body(ixt_a, so, rbuf, pre_epi=None):
        for seg in range(8):
            lanes = [(seg % 2) * 2, (seg % 2) * 2 + 1]
            ch_done = []
            for a in range(AD):
                ln_ = lanes[a // 4]
                off = (a % 4) * 1024
                wv = dma('sync', wlane[ln_][:, off:off + 1024],
                         s_gu[:, a * 8192 + seg * 1024:a * 8192 + (seg + 1) * 1024],
                         'dw', waits=wait_lane(ln_) if a % 4 == 0 else ())
                ch_done.append(wv)
            bs = SETA if seg % 2 == 0 else SETB
            def mm(t, lanes=lanes, ch_done=ch_done, bs=bs):
                last = None
                for a in range(AD):
                    t.wait_ge(sems[ch_done[a][0]], ch_done[a][1])
                    t.wait_ge(sems['v'], ixt_a[a])
                    ln_ = wlane[lanes[a // 4]]
                    off = (a % 4) * 1024
                    for tt in range(TSUB):
                        lhs = IxT[:, a * T + tt * 128:a * T + tt * 128 + 128]
                        last = t.matmul(bankF[bs[2 * tt]], lhs,
                                        ln_[:, off:off + 512],
                                        start=(a == 0), stop=(a == AD - 1))
                        last = t.matmul(bankF[bs[2 * tt + 1]], lhs,
                                        ln_[:, off + 512:off + 1024],
                                        start=(a == 0), stop=(a == AD - 1))
                return last
            step('tensor', mm,
                 waits=[w for b in bs for w in wait_bank(b)],
                 incs=(('t', 1),))
            tv = now('t')
            for l_ in lanes:
                lane_free[l_] = ('t', tv)
            for tt in range(TSUB):
                spo = tt * 512
                def ev_s(s, tt=tt, spo=spo, bs=bs):
                    return s.activation(sps[:, spo:spo + 512], bankF[bs[2 * tt]],
                                        ACT.Silu, bias=0.0,
                                        scale=stat[:, so + C_CG + tt:
                                                   so + C_CG + tt + 1])
                step('scalar', ev_s, waits=[('t', tv)] + wz(f'sps{tt}'),
                     incs=(('s', 1),))
                sv = now('s')
                bank_free[bs[2 * tt]] = ('s', sv)
                def ev_v(v, tt=tt, seg=seg, spo=spo, bs=bs):
                    dst = gu[:, tt * E + seg * 512:tt * E + (seg + 1) * 512]
                    return v.scalar_tensor_tensor(dst, bankF[bs[2 * tt + 1]],
                                                  stat[:, so + C_CU + tt:
                                                       so + C_CU + tt + 1],
                                                  sps[:, spo:spo + 512],
                                                  op0=ALU.mult, op1=ALU.mult)
                def ev_m1(v, tt=tt, seg=seg):
                    sl = slice(tt * E + seg * 512, tt * E + (seg + 1) * 512)
                    return v.reduce_max(SC(C_T3), gu[:, sl], axis=AX.X,
                                        apply_absolute_value=True)
                def ev_m2(v, tt=tt, seg=seg):
                    mx = slice(so + C_MX + tt, so + C_MX + tt + 1)
                    if seg == 0:
                        return v.tensor_copy(stat[:, mx], SC(C_T3))
                    return v.tensor_tensor(stat[:, mx], stat[:, mx],
                                           SC(C_T3), ALU.max)
                step('vector', ev_v,
                     waits=[('s', sv)] +
                           (wz('wq0', 'wq1') if seg == 0 else []),
                     incs=(('v', 1),))
                vchain([ev_m1, ev_m2])
                dep[f'sps{tt}'] = ('v', now('v'))
                bank_free[bs[2 * tt + 1]] = ('v', now('v'))
        vchain([
            lambda v: v.tensor_scalar(SC2(so + C_MX), SC2(so + C_MX), 1e-5,
                                      None, op0=ALU.max),
            lambda v: v.reciprocal(SC2(so + C_SX), SC2(so + C_MX)),
            lambda v: v.tensor_scalar(SC2(so + C_SX), SC2(so + C_SX), 127.0,
                                      None, op0=ALU.mult),
            lambda v: v.reciprocal(SC2(so + C_ISX), SC2(so + C_SX)),
            lambda v: v.tensor_scalar(SC2(so + C_CD), SC2(so + C_ISX),
                                      sc_bc[:, 5:6], None, op0=ALU.mult),
            lambda v: v.tensor_scalar(SC2(so + C_CD), SC2(so + C_CD), 0.5,
                                      None, op0=ALU.mult),
        ])
        igut_b = {}
        for tt in range(TSUB):
            def qgut(v, tt=tt):
                last = None
                for hf in range(2):
                    sl = slice(tt * E + hf * 2048, tt * E + (hf + 1) * 2048)
                    v.tensor_scalar(u16[:], gu[:, sl],
                                    stat[:, so + C_SX + tt:so + C_SX + tt + 1],
                                    1536.0, op0=ALU.mult, op1=ALU.add)
                    last = v.tensor_single_scalar(Igu[:, sl], u16[:], 1536.0,
                                                  ALU.subtract)
                return last
            step('vector', qgut, waits=(('v', now('v')),), incs=(('v', 1),))
            q_done = now('v')
            for b0 in range(0, AE, 8):
                pb, bi = (pT0, 0) if (b0 // 8) % 2 == 0 else (pT1, 1)
                def tr(t, tt=tt, b0=b0, pb=pb):
                    last = None
                    for k in range(8):
                        c0 = tt * E + (b0 + k) * 128
                        last = t.transpose(pb[:, k * 128:(k + 1) * 128],
                                           Igu[:, c0:c0 + 128], ident_bf[:])
                    return last
                step('tensor', tr,
                     waits=[('v', q_done)] + wz('IguT_free') + wait_bank(bi),
                     incs=(('t', 1),))
                tv = now('t')
                def cp(v, pb=pb, tt=tt, b0=b0):
                    dst = bass.AP(IguT, tt * 128 + b0 * T,
                                  [[AE * T, 128], [T, 8], [1, 128]])
                    src = bass.AP(pb, 0, [[1024, 128], [128, 8], [1, 128]])
                    return v.tensor_copy(dst, src)
                step('vector', cp, waits=[('t', tv)] + wz('wqo0', 'wqo1'),
                     incs=(('v', 1),))
                bank_free[bi] = ('v', now('v'))
                igut_b[(tt, b0 // 8)] = now('v')
        dep['IguT'] = ('v', now('v'))
        for c in range(8):
            l_ = c % 4
            wv = dma('sync', wlane[l_][:], s_d[:, c * 4096:(c + 1) * 4096], 'dw',
                     waits=wait_lane(l_))
            def mmd(t, c=c, l_=l_, wv=wv):
                t.wait_ge(sems[wv[0]], wv[1])
                b_ = (4 * c) // 8
                t.wait_ge(sems['v'], max(igut_b[(0, b_)], igut_b[(1, b_)]))
                last = None
                for al in range(4):
                    a = c * 4 + al
                    for tt in range(TSUB):
                        lhs = IguT[:, a * T + tt * 128:a * T + tt * 128 + 128]
                        for dc in range(2):
                            last = t.matmul(
                                bankF[SETA[2 * tt + dc]], lhs,
                                wlane[l_][:, al * 1024 + dc * 512:al * 1024 + (dc + 1) * 512],
                                start=(a == 0), stop=(a == 31))
                return last
            step('tensor', mmd,
                 waits=([w for b in SETA for w in wait_bank(b)])
                       if c == 0 else (),
                 incs=(('t', 1),))
            lane_free[l_] = ('t', now('t'))
        tv = now('t')
        dep['IguT_free'] = ('t', tv)
        dep['IxT_free'] = ('t', tv)
        if pre_epi is not None:
            pre_epi()
        def epi(v, r_=rbuf):
            last = None
            for tt in range(TSUB):
                for dc in range(2):
                    sl = slice(tt * DM + dc * 512, tt * DM + (dc + 1) * 512)
                    last = v.scalar_tensor_tensor(r_[:, sl],
                                                  bankF[SETA[2 * tt + dc]],
                                                  stat[:, so + C_CD + tt:
                                                       so + C_CD + tt + 1],
                                                  r_[:, sl],
                                                  op0=ALU.mult, op1=ALU.add)
            return last
        step('vector', epi, waits=(('t', tv),), incs=(('v', 1),))
        for b in SETA:
            bank_free[b] = ('v', now('v'))
        dep['r'] = ('v', now('v'))

    def mamba(rbuf):
        layer_norm(rbuf=rbuf)
        for tt in range(TSUB):
            for b0 in range(0, AD, 4):
                bi = 6 + (b0 // 4) % 2
                pb = bankF[bi]
                def trf(t, b0=b0, pb=pb, tt=tt):
                    last = None
                    for k in range(4):
                        c0 = tt * DM + (b0 + k) * 128
                        last = t.transpose(pb[:, k * 128:(k + 1) * 128],
                                           xln[:, c0:c0 + 128], ident_f[:])
                    return last
                step('tensor', trf, waits=wz('xln_w', 'xlT_free') + wait_bank(bi),
                     incs=(('t', 1),))
                tv = now('t')
                def hilo(v, pb=pb, b0=b0, tt=tt):
                    dsth = bass.AP(xlh, tt * 128 + b0 * T,
                                   [[AD * T, 128], [T, 4], [1, 128]])
                    dstl = bass.AP(xll, tt * 128 + b0 * T,
                                   [[AD * T, 128], [T, 4], [1, 128]])
                    src = pb.rearrange("p (b c) -> p b c", b=4)
                    v.tensor_copy(dsth, src)
                    return v.tensor_tensor(dstl, src, dsth, ALU.subtract)
                step('vector', hilo, waits=(('t', tv),), incs=(('v', 1),))
                bank_free[bi] = ('v', now('v'))
        dep['xlT'] = ('v', now('v'))
        for g in range(8):
            lh, ll = (g % 2) * 2, (g % 2) * 2 + 1
            wvh = dma('sync', wlane[lh][:], s_wih[:, g * 4096:(g + 1) * 4096], 'dw',
                      waits=wait_lane(lh))
            wvl = dma('sync', wlane[ll][:], s_wil[:, g * 4096:(g + 1) * 4096], 'dw',
                      waits=wait_lane(ll))
            bs = SETA if g % 2 == 0 else SETB
            def mmz(t, lh=lh, ll=ll, wvh=wvh, wvl=wvl, bs=bs):
                t.wait_ge(sems[wvh[0]], wvh[1])
                t.wait_ge(sems[wvl[0]], wvl[1])
                last = None
                for a in range(AD):
                    for el in range(4):
                        for p_, (wl_, xr_) in enumerate(((lh, xlh), (lh, xll),
                                                         (ll, xlh))):
                            rhs = xr_[:, a * T:(a + 1) * T]
                            lhs = wlane[wl_][:, a * 512 + el * 128:a * 512 + (el + 1) * 128]
                            last = t.matmul(bankF[bs[el]][:, 0:T], lhs, rhs,
                                            start=(a == 0 and p_ == 0),
                                            stop=(a == AD - 1 and p_ == 2))
                return last
            step('tensor', mmz,
                 waits=wz('xlT') + [w for b in bs for w in wait_bank(b)],
                 incs=(('t', 1),))
            tv = now('t')
            lane_free[lh] = ('t', tv)
            lane_free[ll] = ('t', tv)
            def evz(v, g=g, bs=bs):
                last = None
                for el in range(4):
                    e = 4 * g + el
                    if e < CT:
                        dst = xinp[:, e * XP + 3:e * XP + 3 + T]
                    else:
                        dst = z_t[:, (e - CT) * T:(e - CT + 1) * T]
                    last = v.tensor_copy(dst, bankF[bs[el]][:, 0:T])
                return last
            step('vector', evz, waits=(('t', tv),), incs=(('v', 1),))
            for b in bs:
                bank_free[b] = ('v', now('v'))
            if g < 4:
                def restg(v, g=g):
                    last = None
                    for ct in range(4 * g, 4 * g + 4):
                        last = v.tensor_copy(xinp[:, ct * XP:ct * XP + 3],
                                             halo[:, ct * 4:ct * 4 + 3])
                    return last
                def convg(v, g=g):
                    last = None
                    for ct in range(4 * g, 4 * g + 4):
                        p0 = ct * XP
                        dst = xa_t[:, ct * T:(ct + 1) * T]
                        v.tensor_scalar(dst, xinp[:, p0:p0 + T],
                                        convw[:, ct * 4:ct * 4 + 1], None,
                                        op0=ALU.mult)
                        for k in range(1, 4):
                            v.scalar_tensor_tensor(dst, xinp[:, p0 + k:p0 + k + T],
                                                   convw[:, ct * 4 + k:ct * 4 + k + 1],
                                                   dst, op0=ALU.mult, op1=ALU.add)
                        last = v.tensor_copy(halo[:, ct * 4:ct * 4 + 3],
                                             xinp[:, p0 + T:p0 + T + 3])
                    return last
                vchain([restg, convg])
                def silug(s, g=g):
                    last = None
                    for ct in range(4 * g, 4 * g + 4):
                        sl = slice(ct * T, (ct + 1) * T)
                        last = s.activation(xa_t[:, sl], xa_t[:, sl], ACT.Silu,
                                            bias=convb[:, ct:ct + 1], scale=1.0)
                    return last
                step('scalar', silug, waits=(('v', now('v')),), incs=(('s', 1),))
        dep['xa'] = ('s', now('s'))
        dep['xlT_free'] = ('t', now('t'))
        def szall(s):
            last = None
            for ct in range(CT):
                sl = slice(ct * T, (ct + 1) * T)
                last = s.activation(z_t[:, sl], z_t[:, sl], ACT.Silu,
                                    bias=0.0, scale=1.0)
            return last
        step('scalar', szall, waits=(('v', now('v')),), incs=(('s', 1),))
        dep['szall'] = ('s', now('s'))
        for ct in range(CT):
            def spl(v, ct=ct):
                o = (ct % 2) * T
                sl = slice(ct * T, (ct + 1) * T)
                v.tensor_copy(va_h[:, o:o + T], xa_t[:, sl])
                return v.tensor_tensor(va_l[:, o:o + T], xa_t[:, sl],
                                       va_h[:, o:o + T], ALU.subtract)
            step('vector', spl, waits=wz('xa', f'va{ct % 2}'), incs=(('v', 1),))
            vv_ = now('v')
            def mmp(t, ct=ct):
                o = (ct % 2) * T
                last = None
                for p_, (wl_, xr_) in enumerate(((wx_h, va_h), (wx_h, va_l),
                                                 (wx_l, va_h))):
                    last = t.matmul(bankF[6][0:96, 0:T],
                                    wl_[:, ct * 96:(ct + 1) * 96], xr_[:, o:o + T],
                                    start=(ct == 0 and p_ == 0),
                                    stop=(ct == CT - 1 and p_ == 2))
                return last
            step('tensor', mmp,
                 waits=[('v', vv_)] + (wait_bank(6) if ct == 0 else []),
                 incs=(('t', 1),))
            dep[f'va{ct % 2}'] = ('t', now('t'))
        tv = now('t')
        step('vector', lambda v: v.tensor_copy(ssm_sb[:], bankF[6][0:96, 0:T]),
             waits=(('t', tv),), incs=(('v', 1),))
        bank_free[6] = ('v', now('v'))
        vssm = now('v')
        def spl_dtr(v):
            v.tensor_copy(dtr_h[:], ssm_sb[0:64, :])
            return v.tensor_tensor(dtr_l[:], ssm_sb[0:64, :], dtr_h[:], ALU.subtract)
        step('vector', spl_dtr, incs=(('v', 1),))
        vdtr = now('v')
        for ct in range(CT):
            bi = 6 + ct % 2
            pb = bankF[bi]
            def mmdt(t, ct=ct, pb=pb):
                last = None
                for p_, (wl_, xr_) in enumerate(((wdt_h, dtr_h), (wdt_h, dtr_l),
                                                 (wdt_l, dtr_h))):
                    last = t.matmul(pb[:, 0:T], wl_[:, ct * 128:(ct + 1) * 128],
                                    xr_[:], start=(p_ == 0), stop=(p_ == 2))
                return last
            step('tensor', mmdt, waits=[('v', vdtr)] + wait_bank(bi),
                 incs=(('t', 1),))
            tv = now('t')
            def sp_s(s, ct=ct, pb=pb):
                o = (ct % 2) * 512
                s.activation(sps[:, o:o + T], pb[:, 0:T], ACT.Exp,
                             bias=bdt[:, ct:ct + 1], scale=1.0)
                return s.activation(dt_t[:, ct * T:(ct + 1) * T], sps[:, o:o + T],
                                    ACT.Ln, bias=1.0, scale=1.0)
            step('scalar', sp_s, waits=[('t', tv)] + wz(f'sps{ct % 2}'),
                 incs=(('s', 1),))
            dep[f'sps{ct % 2}'] = ('s', now('s'))
            bank_free[bi] = ('s', now('s'))
        dep['dt'] = ('s', now('s'))
        step('vector', lambda v: v.tensor_tensor(xa_t[:], dt_t[:], xa_t[:], ALU.mult),
             waits=wz('dt'), incs=(('v', 1),))
        dep['u'] = ('v', now('v'))
        wch = []
        def wout_dma(c):
            lh, ll = (c % 2) * 2, (c % 2) * 2 + 1
            wvh = dma('sync', wlane[lh][:], s_woh[:, c * 4096:(c + 1) * 4096], 'dw',
                      waits=wait_lane(lh))
            wvl = dma('sync', wlane[ll][:], s_wol[:, c * 4096:(c + 1) * 4096], 'dw',
                      waits=wait_lane(ll))
            wch.append((lh, ll, wvh, wvl))
        wout_dma(0)
        wout_dma(1)
        vo_ready = {}
        def rowdma(n):
            bv1 = dma('gpsimd', brow[0:1, (n % 2) * T:(n % 2 + 1) * T],
                      ssm_sb[64 + n:65 + n, :], 'dx',
                      waits=[('v', vssm)] + wz(f'row{n % 2}'))
            bv2 = dma('gpsimd', crow[0:1, (n % 2) * T:(n % 2 + 1) * T],
                      ssm_sb[80 + n:81 + n, :], 'dx')
            return [bv1, bv2]
        rowv = [rowdma(0)]
        pend_h = [None]

        def flush_h(s_=None):
            if pend_h[0] is None:
                return
            col_, ho_, vv_ = pend_h[0]
            pend_h[0] = None
            def hcp(s, col_=col_, ho_=ho_):
                return s.activation(h_state[:, col_:col_ + 1],
                                    h_b[:, ho_ + T - 1:ho_ + T],
                                    ACT.Copy, bias=0.0, scale=1.0)
            step('scalar', hcp, waits=(('v', vv_),), incs=(('s', 1),))
        bc_done = {}
        def bc_piece(n, k, waits):
            bb, bc_ = (4, 5) if n % 2 == 0 else (0, 1)
            o = (n % 2) * T
            bank = bb if k < 4 else bc_
            row = brow if k < 4 else crow
            c0 = (k % 4) * (T // 4)
            def pc(t, bank=bank, row=row, o=o, c0=c0):
                return t.matmul(bankF[bank][:, c0:c0 + T // 4], ones_f[:],
                                row[0:1, o + c0:o + c0 + T // 4],
                                start=True, stop=True)
            step('tensor', pc, waits=waits, incs=(('t', 1),))
            bc_done[n] = now('t')
        for k in range(8):
            w = (rowv[0] + wait_bank(4) + wait_bank(5)) if k == 0 else ()
            bc_piece(0, k, w)
        for n in range(NST):
            bmo = (n % 2) * T
            tbc = bc_done[n]
            dep[f'row{n % 2}'] = ('t', tbc)
            if n + 1 < NST:
                rowv.append(rowdma(n + 1))
            bb, bc_ = (4, 5) if n % 2 == 0 else (0, 1)
            def evbc(s, bb=bb, bc_=bc_, bmo=bmo):
                s.activation(bm_sb[:, bmo:bmo + T], bankF[bb][:, 0:T],
                             ACT.Copy, bias=0.0, scale=1.0)
                return s.activation(cm_sb[:, bmo:bmo + T], bankF[bc_][:, 0:T],
                                    ACT.Copy, bias=0.0, scale=1.0)
            step('scalar', evbc, waits=[('t', tbc)] + wz(f'bm{n % 2}'),
                 incs=(('s', 1),))
            bank_free[bb] = ('s', now('s'))
            bank_free[bc_] = ('s', now('s'))
            last_n = (n == NST - 1)

            def emit_fin(ct):
                def fin(v, ct=ct):
                    dst = vv_t[:, ct * T:(ct + 1) * T]
                    v.scalar_tensor_tensor(dst,
                                           xinp[:, ct * XP + 3:ct * XP + 3 + T],
                                           dvec[:, ct:ct + 1], dst,
                                           op0=ALU.mult, op1=ALU.add)
                    v.tensor_tensor(dst, dst, z_t[:, ct * T:(ct + 1) * T],
                                    ALU.mult)
                    ov_ = (ct % 8) * T
                    v.tensor_copy(vo_h[:, ov_:ov_ + T], dst)
                    return v.tensor_tensor(vo_l[:, ov_:ov_ + T], dst,
                                           vo_h[:, ov_:ov_ + T], ALU.subtract)
                step('vector', fin,
                     waits=wz(f'acc{ct}', f'voT{(ct // 4) % 2}', 'szall'),
                     incs=(('v', 1),))
                vo_ready[ct] = now('v')
                if ct % 4 == 3:
                    c = ct // 4
                    lh, ll, wvh, wvl = wch[c]
                    def mmo(t, c=c, lh=lh, ll=ll, wvh=wvh, wvl=wvl,
                            vd=now('v')):
                        t.wait_ge(sems[wvh[0]], wvh[1])
                        t.wait_ge(sems[wvl[0]], wvl[1])
                        t.wait_ge(sems['v'], vd)
                        last = None
                        for cti in range(4 * c, 4 * c + 4):
                            ov_ = (cti % 8) * T
                            co = (cti % 4) * 1024
                            for tt in range(TSUB):
                                for p_, (vb, wl_) in enumerate(
                                        ((vo_h, lh), (vo_l, lh), (vo_h, ll))):
                                    lhs = vb[:, ov_ + tt * 128:ov_ + tt * 128 + 128]
                                    for dc in range(2):
                                        last = t.matmul(
                                            bankF[SETA[2 * tt + dc]], lhs,
                                            wlane[wl_][:, co + dc * 512:co + (dc + 1) * 512],
                                            start=(c == 0 and cti == 0 and p_ == 0),
                                            stop=(c == 3 and cti == 15 and p_ == 2))
                        return last
                    step('tensor', mmo,
                         waits=([w for b in SETA for w in wait_bank(b)]
                                if c == 0 else ()),
                         incs=(('t', 1),))
                    dep[f'voT{c % 2}'] = ('t', now('t'))
                    lane_free[lh] = ('t', now('t'))
                    lane_free[ll] = ('t', now('t'))
                    if c + 2 < 4:
                        wout_dma(c + 2)

            for ct in range(CT):
                o = (ct % 2) * T
                ho = (ct % 4) * T
                col = ct * NST + n
                def mk_dA(s, ct=ct, o=o, col=col):
                    return s.activation(dA_b[:, o:o + T],
                                        dt_t[:, ct * T:(ct + 1) * T],
                                        ACT.Exp, bias=0.0,
                                        scale=a_neg[:, col:col + 1])
                step('scalar', mk_dA, waits=wz(f'dA{ct % 2}'), incs=(('s', 1),))
                sv = now('s')
                flush_h()
                def scn(v, ct=ct, o=o, ho=ho, col=col, n=n, bmo=bmo):
                    v.tensor_tensor(dBx_b[:, o:o + T],
                                    xa_t[:, ct * T:(ct + 1) * T],
                                    bm_sb[:, bmo:bmo + T], ALU.mult)
                    v.tensor_tensor_scan(h_b[:, ho:ho + T], dA_b[:, o:o + T],
                                         dBx_b[:, o:o + T],
                                         h_state[:, col:col + 1],
                                         op0=ALU.mult, op1=ALU.add)
                    if n == 0:
                        return v.tensor_tensor(vv_t[:, ct * T:(ct + 1) * T],
                                               h_b[:, ho:ho + T],
                                               cm_sb[:, bmo:bmo + T], ALU.mult)
                    return v.tensor_tensor(P_b[:, (ct % 2) * T:(ct % 2) * T + T],
                                           h_b[:, ho:ho + T],
                                           cm_sb[:, bmo:bmo + T], ALU.mult)
                step('vector', scn,
                     waits=[('s', sv)] + (wz(f'P{ct % 2}') if n > 0 else []),
                     incs=(('v', 1),))
                dep[f'dA{ct % 2}'] = ('v', now('v'))
                pend_h[0] = (col, ho, now('v'))
                if n > 0:
                    def post(g_, ct=ct):
                        return g_.tensor_tensor(vv_t[:, ct * T:(ct + 1) * T],
                                                vv_t[:, ct * T:(ct + 1) * T],
                                                P_b[:, (ct % 2) * T:(ct % 2) * T + T],
                                                ALU.add)
                    step('gpsimd', post, waits=(('v', now('v')),), incs=(('g', 1),))
                    dep[f'P{ct % 2}'] = ('g', now('g'))
                    dep[f'acc{ct}'] = ('g', now('g'))
                else:
                    dep[f'acc{ct}'] = ('v', now('v'))
                if ct % 2 == 0 and n + 1 < NST:
                    k = ct // 2
                    nb, ncb = (4, 5) if (n + 1) % 2 == 0 else (0, 1)
                    w = [('v', now('v'))]
                    if k == 0:
                        w += rowv[n + 1] + wait_bank(nb) + wait_bank(ncb) + \
                             wz(f'bm{(n + 1) % 2}')
                    bc_piece(n + 1, k, w)
                if last_n and ct >= 2:
                    emit_fin(ct - 2)
            dep[f'bm{n % 2}'] = ('g', now('g'))
            if last_n:
                emit_fin(CT - 2)
                emit_fin(CT - 1)
        flush_h()
        dep['hstate'] = ('g', now('g'))
        def epo(v, r_=None):
            r_ = rbuf if r_ is None else r_
            last = None
            for tt in range(TSUB):
                for dc in range(2):
                    sl = slice(tt * DM + dc * 512, tt * DM + (dc + 1) * 512)
                    last = v.tensor_tensor(r_[:, sl], bankF[SETA[2 * tt + dc]],
                                           r_[:, sl], ALU.add)
            return last
        step('vector', epo, waits=(('t', now('t')),), incs=(('v', 1),))
        for b in SETA:
            bank_free[b] = ('v', now('v'))
        dep['r'] = ('v', now('v'))
        dep['xln'] = ('v', now('v'))

    bufs = (r_t, r_t2)
    prev_ov = [None]

    def prefetch(tile):
        if tile + 1 >= ntiles:
            return None, None
        nb = bufs[(tile + 1) % 2]
        w = [prev_ov[0]] if prev_ov[0] else []
        return nb, dma('gpsimd', nb[:], x_ap(d_x, tile + 1), 'dx', waits=w)

    ix_next = ffn_head(0, rwaits=[dep['rbuf0']], rbuf=bufs[0])
    for tile in range(ntiles):
        rb = bufs[tile % 2]
        ffn_body(ix_next, 0, rb)
        ix_next = None
        if stages >= 2:
            mamba(rb)
        nb, ldv = prefetch(tile)
        if stages >= 3:
            ix2 = ffn_head(24, rbuf=rb)
            if nb is not None:
                hold = {}
                def hook(nb=nb, ldv=ldv, hold=hold):
                    hold['ix'] = ffn_head(0, rwaits=[ldv], rbuf=nb)
                ffn_body(ix2, 24, rb, pre_epi=hook)
                ix_next = hold['ix']
            else:
                ffn_body(ix2, 24, rb)
        prev_ov[0] = dma('gpsimd', x_ap(d_out, tile), rb[:], 'dx', waits=wz('r'))
        if stages < 3 and nb is not None:
            ix_next = ffn_head(0, rwaits=[ldv], rbuf=nb)
    if debug:
        dbg_specs = {"sc_bc": sc_bc, "stat": stat, "xln": xln, "Ix": Ix,
                     "IxT": IxT, "gu": gu, "Igu": Igu, "r_t": r_t,
                     "sps": sps, "u16": u16, "wl0": wlane[0], "IguT": IguT,
                     "xinp": xinp, "z_t": z_t, "xa_t": xa_t, "dt_t": dt_t,
                     "vv_t": vv_t, "ssm_sb": ssm_sb, "h_state": h_state,
                     "a_neg": a_neg, "brow": brow, "crow": crow}
        wall = [(s_, v_) for s_, v_ in cnt.items()]
        first_dbg = True
        for nm, tns in dbg_specs.items():
            shp = list(tns.shape) if hasattr(tns, 'shape') else None
            dt_ = tns.dtype
            dd = nc.dram_tensor("dbg_" + nm, shp, dt_, kind="ExternalOutput").ap()
            dma('gpsimd', dd, tns[:], 'dx', waits=tuple(wall) if first_dbg else ())
            first_dbg = False
    step('gpsimd', lambda g: g.memset(srow[0:1, 6:8], 0.0),
         waits=tuple(qdone('dx')) + tuple(qdone('dw')))

    with ExitStack() as st:
        for n_ in sorted(cnt.keys()):
            sems[n_] = st.enter_context(nc.semaphore(n_))
        block = st.enter_context(nc.Block())

        def section(engname):
            def body(eng):
                for e, waits, fn, incs in ops:
                    if e != engname:
                        continue
                    for sname, val in waits:
                        eng.wait_ge(sems[sname], val)
                    inst = fn(eng)
                    for sname, by in incs:
                        inst.then_inc(sems[sname], by)
            return body

        block.sync(section('sync'))
        block.gpsimd(section('gpsimd'))
        block.vector(section('vector'))
        block.scalar(section('scalar'))
        block.tensor(section('tensor'))
    return nc


_NC_CACHE = {}


def kernel(L0_attn_x, L0_attn_W_in, L0_attn_conv_w, L0_attn_conv_b, L0_attn_W_x,
           L0_attn_W_dt, L0_attn_b_dt, L0_attn_A_log, L0_attn_D, L0_attn_W_out,
           L0_ffn_w_gate, L0_ffn_w_up, L0_ffn_w_down):
    f32 = np.float32

    def pm(w, atiles):
        r, c = w.shape
        assert r == atiles * 128, (r, atiles)
        return np.ascontiguousarray(
            np.asarray(w, f32).reshape(atiles, 128, c).transpose(1, 0, 2)
            .reshape(128, atiles * c))

    wgT = pm(np.asarray(L0_ffn_w_gate, f32).T, AD)
    wuT = pm(np.asarray(L0_ffn_w_up, f32).T, AD)
    wdT = pm(np.asarray(L0_ffn_w_down, f32).T, AE)
    win = pm(L0_attn_W_in, AD)
    wout = pm(L0_attn_W_out, CT)
    wx = pm(L0_attn_W_x, CT)
    alog = pm(L0_attn_A_log, CT)
    convw = pm(L0_attn_conv_w, CT)
    convb = pm(np.asarray(L0_attn_conv_b, f32).reshape(DI, 1), CT)
    dd = pm(np.asarray(L0_attn_D, f32).reshape(DI, 1), CT)
    bdt = pm(np.asarray(L0_attn_b_dt, f32).reshape(DI, 1), CT)
    wdt = np.ascontiguousarray(np.asarray(L0_attn_W_dt, f32))

    key = (_STAGES, _NTILES)
    if key not in _NC_CACHE:
        _NC_CACHE[key] = build_program(_STAGES, _NTILES)
    nc = _NC_CACHE[key]

    x = np.asarray(L0_attn_x, f32)
    in_maps = []
    for b in range(8):
        in_maps.append({
            "x": np.ascontiguousarray(x[b]),
            "wgT": wgT, "wuT": wuT, "wdT": wdT,
            "win": win, "wout": wout, "wx": wx, "wdt": wdt,
            "alog": alog, "convw": convw, "convb": convb,
            "dvec": dd, "bdt": bdt,
        })
    res = run_bass_kernel_spmd(nc, in_maps, core_ids=list(range(8)))
    return np.stack([np.asarray(r["out"], f32) for r in res.results], axis=0)
